# version 05
# baseline (speedup 1.0000x reference)
import numpy as np
import ml_dtypes
import concourse.bass as bass
import concourse.mybir as mybir
from concourse.bass_utils import run_bass_kernel_spmd

F32 = mybir.dt.float32
BF16 = mybir.dt.bfloat16
AF = mybir.ActivationFunctionType
ALU = mybir.AluOpType
AX = mybir.AxisListType
BF16NP = ml_dtypes.bfloat16

B, T, D = 4, 4096, 1024
H, HD = 16, 64
NB = T // 128
TOWN = 2048
E, F = 8, 4096
NE = 9
NITER = 42
KSEL = float(2 * B * T)
EPS = 1e-5
SHUF = [(i + 16) % 32 for i in range(32)]
EORDER = [8, 0, 1, 2, 3, 4, 5, 6, 7]

DEBUG = 0


def _owned_blocks(sub):
    return [j for j in range(NB) if j % 4 in ((0, 3) if sub == 0 else (1, 2))]


def _qs_min(kt):
    c0 = len([j for j in _owned_blocks(0) if j < kt])
    c1 = len([j for j in _owned_blocks(1) if j < kt])
    return 128 * min(c0, c1)


class Plan:
    def __init__(self):
        self.items = []
        self.cnt = {}

    def on(self, eng, fn):
        self.items.append((eng, fn))

    def inc(self, sem, by=1):
        self.cnt[sem] = self.cnt.get(sem, 0) + by
        return self.cnt[sem]

    def val(self, sem):
        return self.cnt.get(sem, 0)


def build_nc(debug=0):
    from contextlib import ExitStack
    nc = bass.Bass("TRN2", target_bir_lowering=False)
    es = ExitStack()

    def SBx(stack, n, s, d=F32, side=None):
        return stack.enter_context(nc.sbuf_tensor(n, list(s), d, side=side))

    def din(name, shape, dt=F32):
        return nc.dram_tensor(name, list(shape), dt, kind="ExternalInput").ap()

    xt = din("xt", [D, T])
    xown = din("xown", [D, TOWN])
    wqT = din("wqT", [D, D]); wkT = din("wkT", [D, D])
    wvT = din("wvT", [D, D]); woT = din("woT", [D, D])
    gwT = din("gwT", [D, E])
    w1d = din("w1", [E, D, F]); w2d = din("w2", [E, F, D])
    ws1d = din("ws1", [D, F]); ws2d = din("ws2", [F, D])
    b1a = din("b1a", [128, NE, F // 128])
    b2a = din("b2a", [128, NE, D // 128])
    cosK = din("cosK", [128, T], BF16)
    sinK = din("sinK", [128, T], BF16)
    cosQ = din("cosQ", [128, TOWN], BF16)
    sinQ = din("sinQ", [128, TOWN], BF16)
    amask = din("amask", [128, NB * 256], BF16)
    onc_b = din("onc_b", [128, 1], BF16)
    on33 = din("on33", [128, 33], BF16)
    onc_f = din("onc_f", [128, 1])
    onr_f = din("onr_f", [1, 128])
    on64 = din("on64", [1, 64])

    out = nc.dram_tensor("out", [D, TOWN], F32, kind="ExternalOutput").ap()
    dbg = {}
    if debug:
        for nm, sh in [("d_psw0", [128, 2048]), ("d_psw1", [128, 2048]),
                       ("d_s0", [128, 1024]),
                       ("d_p0", [128, 1024]), ("d_z1", [128, 1024]),
                       ("d_p1", [128, 1024]), ("d_z4", [128, 64]),
                       ("d_z8", [128, 64]), ("d_z16", [128, 64]),
                       ("d_rz0", [1, 1024]), ("d_rz1", [1, 1024]),
                       ("d_attnE", [128, 4 * TOWN]),
                       ("d_qh", [128, 4 * TOWN]), ("d_kh", [128, 4 * T]),
                       ("d_vh", [128, NB * 520]), ("d_biasq", [128, 8]),
                       ("d_attn", [128, 8 * TOWN]), ("d_r1", [128, 8 * TOWN]),
                       ("d_ln1", [128, 8 * TOWN]), ("d_scores", [8, TOWN]),
                       ("d_gate", [8, TOWN]), ("d_u0", [1, 16])]:
            dbg[nm] = nc.dram_tensor(nm, sh, F32, kind="ExternalOutput").ap()

    ccin = nc.dram_tensor("ccin", [E, TOWN], F32)
    ccout = nc.dram_tensor("ccout", [64, TOWN], F32)

    mrow = SBx(es, "mrow", [1, 1024])
    vrow = SBx(es, "vrow", [1, 1024])
    rrow = SBx(es, "rrow", [1, 1024])
    rowX = SBx(es, "rowX", [1, 1024])
    onesr = SBx(es, "onesr", [1, 128])
    on64r = SBx(es, "on64r", [1, 64])
    onecb = SBx(es, "onecb", [128, 1], BF16)
    zcol = SBx(es, "zcol", [128, 1], BF16)
    on33b = SBx(es, "on33b", [128, 33], BF16)
    mqcol = SBx(es, "mqcol", [128, 4])
    mkcol = SBx(es, "mkcol", [128, 4])
    ubcol = SBx(es, "ubcol", [128, 4])
    ubrow = SBx(es, "ubrow", [1, 8])
    onecf = SBx(es, "onecf", [128, 1])
    biasq = SBx(es, "biasq", [128, 8])
    b1sb = SBx(es, "b1sb", [128, NE, F // 128])
    b2sb = SBx(es, "b2sb", [128, NE, D // 128])
    st16 = SBx(es, "st16", [1, 16])
    cn64 = SBx(es, "cn64", [64, 1])
    crow = SBx(es, "crow", [1, 64])
    midc = SBx(es, "midc", [64, 1])
    midrow = SBx(es, "midrow", [1, 64])
    locol = SBx(es, "locol", [8, 1])
    sqtmp = SBx(es, "sqtmp", [128, TOWN], BF16)
    mb = SBx(es, "mb", [128, 512])
    rb = SBx(es, "rb", [128, 512])
    regAttn = ExitStack()
    attn = SBx(regAttn, "attn", [128, 8, TOWN], BF16, side="right")

    psS = es.enter_context(nc.psum_tensor("psS", [128, 1024], F32))
    psS2 = es.enter_context(nc.psum_tensor("psS2", [128, 1024], F32))
    psw = es.enter_context(nc.psum_tensor("psw", [128, 2048], F32))
    pst4 = [psS[:, 0:512], psS[:, 512:1024], psS2[:, 0:512], psS2[:, 512:1024]]
    pswq = [psw[:, i * 512:(i + 1) * 512] for i in range(4)]

    sems = {}
    for n in ["sX", "sW", "sM", "sPE", "sACT", "sDV", "sGP", "sST",
              "sCH", "sDM", "sDM2", "sCC", "sW0", "sW1", "sW2", "sW3",
              "sWV", "sWO", "sMt", "sMk", "sMx", "sW1a", "sW1b", "sW2s"]:
        sems[n] = es.enter_context(nc.semaphore(n))

    P = Plan()

    def W(eng, sem, val):
        if val:
            P.on(eng, lambda e, s=sems[sem], v=val: e.wait_ge(s, v))

    def I(eng, fn, sem=None, by=1):
        if sem is None:
            P.on(eng, fn)
            return None
        v = P.inc(sem, by)
        P.on(eng, lambda e, f=fn, s=sems[sem], b=by: f(e).then_inc(s, b))
        return v

    def chain(eng, fn, sem="sCH"):
        v = P.inc(sem)
        def go(e, f=fn, s=sems[sem], vv=v):
            f(e).then_inc(s, 1)
            e.wait_ge(s, vv)
        P.on(eng, go)
        return v

    def mmg(out_ap, pairs, waits=(), first=True):
        for (sname, v) in waits:
            W("tensor", sname, v)
        n = len(pairs)
        for i in range(n - 1):
            l, r = pairs[i]
            I("tensor", lambda e, l_=l, r_=r, o=out_ap, s1=(i == 0 and first):
              e.matmul(o, l_, r_, start=s1, stop=False))
        l, r = pairs[n - 1]
        return I("tensor", lambda e, l_=l, r_=r, o=out_ap,
                 s1=(n == 1 and first):
                 e.matmul(o, l_, r_, start=s1, stop=True), "sPE", 1)

    I("gpsimd", lambda e: e.memset(zcol[:, :], 0.0), "sGP", 1)
    sGP_z = P.val("sGP")
    for (dst, src) in [(b1sb[:, :, :], b1a), (b2sb[:, :, :], b2a),
                       (onesr[:, :], onr_f), (on64r[:, :], on64),
                       (onecb[:, :], onc_b), (onecf[:, :], onc_f),
                       (on33b[:, :], on33)]:
        I("sync", lambda e, d=dst, s=src: e.dma_start(d, s), "sM", 16)
    sM_const = P.val("sM")

    vdv_attn_parts = []
    for hg in range(2):
        regH = ExitStack()
        q_h = SBx(regH, f"q_h{hg}", [128, 4, TOWN], BF16, side="right")
        k_h = SBx(regH, f"k_h{hg}", [128, 4, T], BF16, side="right")
        v_h = SBx(regH, f"v_h{hg}", [128, NB * 520], BF16, side="right")

        regP = ExitStack()
        cosKt = SBx(regP, f"cosKt{hg}", [128, T], BF16, side="right")
        sinKt = SBx(regP, f"sinKt{hg}", [128, T], BF16, side="right")
        cosQt = SBx(regP, f"cosQt{hg}", [128, TOWN], BF16, side="right")
        sinQt = SBx(regP, f"sinQt{hg}", [128, TOWN], BF16, side="right")
        xbf = SBx(regP, f"xbf{hg}", [128, 8, 1024], BF16, side="right")
        xqf = SBx(regP, f"xqf{hg}", [128, 8, 512], BF16, side="right")
        wsl = [SBx(regP, f"wsl{hg}_{i}", [128, 8, 128], BF16, side="right") for i in range(4)]
        wvb = SBx(regP, f"wvb{hg}", [128, 8, 512], BF16, side="right")
        rt2 = SBx(regP, f"rt2_{hg}", [128, 512], BF16, side="right")
        rt3 = SBx(regP, f"rt3_{hg}", [128, 512], BF16, side="right")
        rt4 = SBx(regP, f"rt4_{hg}", [128, 512], BF16, side="right")

        W("sync", "sPE", P.val("sPE"))
        W("sync", "sDV", P.val("sDV"))
        W("sync", "sACT", P.val("sACT"))
        for (dst, src) in [(cosKt[:, :], cosK), (sinKt[:, :], sinK),
                           (cosQt[:, :], cosQ), (sinQt[:, :], sinQ)]:
            I("sync", lambda e, d=dst, s=src: e.dma_start(d, s), "sMt", 16)
        W("vector", "sMt", P.val("sMt"))
        if hg > 0:
            W("gpsimd", "sPE", P.val("sPE"))
            W("gpsimd", "sDV", P.val("sDV"))
            W("gpsimd", "sACT", P.val("sACT"))
        vap = bass.AP(v_h, 64, [[520 * NB, 128], [520, NB], [65, 8], [1, 1]])
        vgp_ones = I("gpsimd", lambda e, a=vap: e.memset(a, 1.0), "sGP", 1)

        ps_free = [0, 0, 0, 0]
        ps_rot = [0]

        def next_bank():
            i = ps_rot[0] % 4
            ps_rot[0] += 1
            return i

        wsl_cons = [0, 0, 0, 0]
        wsl_rot = [0]

        def load_wslice(src_dram, gc):
            bi = wsl_rot[0] % 4
            wsl_rot[0] += 1
            if wsl_cons[bi]:
                W("gpsimd", "sPE", wsl_cons[bi])
            sname = f"sW{bi}"
            v = None
            for dc in range(8):
                v = I("gpsimd", lambda e, d=wsl[bi][:, dc, :],
                      s=src_dram[dc * 128:(dc + 1) * 128, gc * 128:(gc + 1) * 128]:
                      e.dma_start(d, s), sname, 16)
            return bi, v, sname

        def rope_evac(bank, dstap, cosap, sinap):
            pst = pst4[bank][:, 0:512]
            I("vector", lambda e, o=rt4[:, :], i=pst: e.tensor_copy(o, i))
            I("vector", lambda e, o=rt2[:, :], i=rt4[:, :], m=SHUF:
              e.stream_shuffle(o, i, m))
            I("vector", lambda e, o=rt3[:, :], i0=pst, i1=cosap:
              e.tensor_tensor(o, i0, i1, ALU.mult))
            I("vector", lambda e, o=rt2[:, :], i0=rt2[:, :], i1=sinap:
              e.tensor_tensor(o, i0, i1, ALU.mult))
            return I("vector", lambda e, o=dstap, i0=rt3[:, :], i1=rt2[:, :]:
                     e.tensor_tensor(o, i0, i1, ALU.add), "sDV", 1)

        first_vdma = [True]
        for sec in range(4):
            if sec > 0 or hg > 0:
                W("gpsimd", "sPE", P.val("sPE"))
            vx = None
            for dc in range(8):
                vx = I("gpsimd", lambda e, d=xbf[:, dc, :],
                       s=xt[dc * 128:(dc + 1) * 128, sec * 1024:(sec + 1) * 1024]:
                       e.dma_start(d, s), "sX", 16)
            for dc in range(8):
                vx = I("gpsimd", lambda e, d=xqf[:, dc, :],
                       s=xown[dc * 128:(dc + 1) * 128, sec * 512:(sec + 1) * 512]:
                       e.dma_start(d, s), "sX", 16)
            sXv = P.val("sX")

            for cl in range(4):
                gc = 4 * hg + cl
                bi, vw, swn = load_wslice(wkT, gc)
                for half in range(2):
                    bank = next_bank()
                    vmm = mmg(pst4[bank][:, 0:512],
                              [(wsl[bi][:, dc, :],
                                xbf[:, dc, half * 512:(half + 1) * 512])
                               for dc in range(8)],
                              [(swn, vw), ("sX", sXv), ("sDV", ps_free[bank])])
                    wsl_cons[bi] = vmm
                    c0 = sec * 1024 + half * 512
                    W("vector", "sPE", vmm)
                    ps_free[bank] = rope_evac(bank, k_h[:, cl, c0:c0 + 512],
                                              cosKt[:, c0:c0 + 512],
                                              sinKt[:, c0:c0 + 512])
            for cl in range(4):
                gc = 4 * hg + cl
                bi, vw, swn = load_wslice(wqT, gc)
                bank = next_bank()
                vmm = mmg(pst4[bank][:, 0:512],
                          [(wsl[bi][:, dc, :], xqf[:, dc, :]) for dc in range(8)],
                          [(swn, vw), ("sX", sXv), ("sDV", ps_free[bank])])
                wsl_cons[bi] = vmm
                c0 = sec * 512
                W("vector", "sPE", vmm)
                ps_free[bank] = rope_evac(bank, q_h[:, cl, c0:c0 + 512],
                                          cosQt[:, c0:c0 + 512],
                                          sinQt[:, c0:c0 + 512])

            if sec == 0:
                W("gpsimd", "sPE", P.val("sPE")) if hg > 0 else None
                vwv = None
                for dc in range(8):
                    vwv = I("gpsimd", lambda e, d=wvb[:, dc, :],
                            s=wvT[dc * 128:(dc + 1) * 128,
                                  hg * 512:(hg + 1) * 512]:
                            e.dma_start(d, s), "sWV", 16)
                vwv_val = vwv
            for tt in range(8):
                bank = next_bank()
                vmm = mmg(pst4[bank][:, 0:512],
                          [(xbf[:, dc, tt * 128:(tt + 1) * 128], wvb[:, dc, :])
                           for dc in range(8)],
                          [("sWV", vwv_val), ("sX", sXv), ("sDV", ps_free[bank])])
                kt = sec * 8 + tt
                dstap = bass.AP(v_h, kt * 520, [[520 * NB, 128], [65, 8], [1, 64]])
                W("vector", "sPE", vmm)
                if first_vdma[0]:
                    W("vector", "sGP", vgp_ones)
                    first_vdma[0] = False
                srcap = pst4[bank][:, 0:512].rearrange("p (h x) -> p h x", h=8)
                ps_free[bank] = I("vector", lambda e, o=dstap, i=srcap:
                                  e.tensor_copy(o, i), "sDV", 1)

        vdv_red = 0
        for (srct, iscol, nch, ncols) in [(q_h, mqcol, 4, TOWN),
                                          (k_h, mkcol, 4, T)]:
            for dc in range(nch):
                nhalf = ncols // TOWN
                vmmu = 0
                for ch in range(nhalf):
                    if vdv_red:
                        W("vector", "sPE", vmmu) if False else None
                    vs = I("vector", lambda e, o=sqtmp[:, :],
                           i=srct[:, dc, ch * TOWN:(ch + 1) * TOWN]:
                           e.tensor_tensor(o, i, i, ALU.mult), "sDV", 1)
                    for qc in range(4):
                        vmmu = mmg(psw[0:33, qc * 512:(qc + 1) * 512],
                                   [(on33b[:, :],
                                     sqtmp[:, qc * 512:(qc + 1) * 512])],
                                   [("sDV", vs), ("sDV", vdv_red)],
                                   first=(ch == 0))
                W("vector", "sPE", vmmu)
                vdv_red = I("vector", lambda e, o=iscol[0:33, dc:dc + 1],
                            i=psw[0:33, 0:TOWN]:
                            e.reduce_max(o, i, AX.X), "sDV", 1)
        W("vector", "sDV", vdv_red)
        vub = I("vector", lambda e: e.tensor_tensor(ubcol[:, :], mqcol[:, :],
                                                    mkcol[:, :], ALU.mult),
                "sDV", 1)
        W("scalar", "sDV", vub)
        vsq_u = I("scalar", lambda e: e.activation(ubcol[:, :], ubcol[:, :],
                                                   AF.Sqrt), "sACT", 1)
        W("vector", "sACT", vsq_u)
        chain("vector", lambda e: e.tensor_scalar(ubcol[:, :], ubcol[:, :],
                                                  -0.125, None, ALU.mult))
        vng = I("vector", lambda e: e.engine_nop(), "sDV", 1)
        W("gpsimd", "sDV", P.val("sDV"))
        W("sync", "sDV", P.val("sDV"))
        W("sync", "sCH", P.val("sCH"))
        if debug >= 2 and hg == 0:
            I("gpsimd", lambda e: e.dma_start(dbg["d_qh"],
              q_h[:, :, :].rearrange("p c t -> p (c t)")), "sST", 16)
            I("gpsimd", lambda e: e.dma_start(dbg["d_kh"],
              k_h[:, :, :].rearrange("p c t -> p (c t)")), "sST", 16)
            I("gpsimd", lambda e: e.dma_start(dbg["d_vh"], v_h[:, :]), "sST", 16)
        W("sync", "sDV", vng)
        vgu = I("sync", lambda e: e.dma_start(
            ubrow[:, :], bass.AP(ubcol, 0, [[32 * 4, 2], [1, 4]])), "sDM", 16)
        vdv_b = 0
        for dcl in range(4):
            for hhx in range(2):
                srcj = hhx * 4 + dcl
                vqb = mmg(psS[0:128, 0:1],
                          [(onesr[0:1, :], ubrow[0:1, srcj:srcj + 1])],
                          [("sDM", vgu), ("sDV", vdv_b),
                           ("sACT", P.val("sACT"))])
                W("vector", "sPE", vqb)
                col = 2 * dcl + hhx
                chain("vector", lambda e, o=biasq[:, col:col + 1],
                      i=psS[:, 0:1]: e.tensor_copy(o, i))
                vdv_b = I("vector", lambda e: e.engine_nop(), "sDV", 1)
        if debug >= 2 and hg == 0:
            W("sync", "sDV", P.val("sDV"))
            I("sync", lambda e: e.dma_start(dbg["d_biasq"], biasq[:, :]),
              "sST", 16)
        regP.close()

        regA = ExitStack()
        trim = SBx(regA, f"trim{hg}", [128, NB * 256], BF16, side="right")
        ptA = SBx(regA, f"ptA{hg}", [128, 1024], BF16, side="right")
        ptB = SBx(regA, f"ptB{hg}", [128, 1024], BF16, side="right")
        W("sync", "sPE", P.val("sPE"))
        W("sync", "sDV", P.val("sDV"))
        W("sync", "sACT", P.val("sACT"))
        vtr = I("sync", lambda e: e.dma_start(trim[:, :], amask), "sMk", 16)
        W("vector", "sMk", vtr)

        def att_tiles(qh):
            res = []
            lo_, hi_ = qh * 1024, (qh + 1) * 1024
            for kt in range(NB):
                a = max(_qs_min(kt), lo_)
                if a < hi_:
                    res.append((kt, a, hi_))
            return res

        W("tensor", "sDV", vdv_b)
        for hpl in range(4):
            hp = 4 * hg + hpl
            for qh in range(2):
                tiles = att_tiles(qh)
                kt0 = tiles[0][0]
                prev = None
                exp_prev = 0
                pv_prev = 0

                def subchunks(l0):
                    ck = []
                    if l0 < 512:
                        ck.append((l0, 512))
                    ck.append((max(l0, 512), 1024))
                    return ck

                def emit_pv(pkt, pl0, vmsk, start_flag):
                    W("tensor", "sDV", vmsk)
                    vv = None
                    for (c0_, c1_) in subchunks(pl0):
                        for hhx in range(2):
                            off = pkt * 520 + (2 * hpl + hhx) * 65
                            vv = I("tensor", lambda e,
                                   l=v_h[:, off:off + 65],
                                   r=(ptA if hhx == 0 else ptB)[:, c0_:c1_],
                                   o=psw[0:65, hhx * 1024 + c0_:
                                         hhx * 1024 + c1_]:
                                   e.matmul(o, l, r, start=start_flag,
                                            stop=True), "sPE", 1)
                    return vv

                for (kt, a, b_) in tiles:
                    l0 = a - qh * 1024
                    if exp_prev:
                        W("tensor", "sACT", exp_prev)
                    vb = None
                    for (c0_, c1_) in subchunks(l0):
                        ga = qh * 1024 + c0_
                        gb_ = qh * 1024 + c1_
                        I("tensor", lambda e,
                          l=k_h[0:64, hpl, kt * 128:(kt + 1) * 128],
                          r=q_h[0:64, hpl, ga:gb_], o=psS[0:128, c0_:c1_]:
                          e.matmul(o, l, r, start=True, stop=True))
                        vb = I("tensor", lambda e,
                               l=k_h[64:128, hpl, kt * 128:(kt + 1) * 128],
                               r=q_h[64:128, hpl, ga:gb_],
                               o=psS2[0:128, c0_:c1_]:
                               e.matmul(o, l, r, start=True, stop=True),
                               "sPE", 1)
                    if prev is not None:
                        pkt, pl0, vmsk = prev
                        pv_prev = emit_pv(pkt, pl0, vmsk, pkt == kt0)
                        if debug and hg == 0 and hpl == -97 and qh == 0 and kt in (2, 8):
                            nm = {2: "d_z4", 8: "d_z8"}[kt]
                            W("vector", "sPE", pv_prev)
                            vz1 = I("vector", lambda e: e.tensor_copy(
                                mb[:, 0:64], psw[0:128, 0:64]), "sDV", 1)
                            W("sync", "sDV", vz1)
                            vz2 = I("sync", lambda e, n=nm: e.dma_start(
                                dbg[n], mb[:, 0:64]), "sST", 16)
                            W("tensor", "sST", vz2)
                        if debug and hg == 0 and hpl == -98 and qh == 0 and kt == 2:
                            W("vector", "sPE", pv_prev)
                            vz1 = I("vector", lambda e: e.tensor_copy(
                                mb[:, :], psw[0:128, 0:512]), "sDV", 1)
                            W("sync", "sDV", vz1)
                            vz2 = I("sync", lambda e: e.dma_start(
                                dbg["d_z1"][:, 0:512], mb[:, :]), "sST", 16)
                            W("gpsimd", "sDV", vz1)
                            vz3 = I("gpsimd", lambda e: e.dma_start(
                                dbg["d_p1"], ptA[:, :]), "sST", 16)
                            W("tensor", "sST", vz2)
                            W("tensor", "sST", vz3)
                    W("scalar", "sPE", vb)
                    if pv_prev:
                        W("scalar", "sPE", pv_prev)
                    I("scalar", lambda e, o=ptA[:, l0:1024],
                      i=psS[0:128, l0:1024], bc=biasq[:, 2 * hpl:2 * hpl + 1]:
                      e.activation(o, i, AF.Exp, bias=bc, scale=0.125))
                    veB = I("scalar", lambda e, o=ptB[:, l0:1024],
                            i=psS2[0:128, l0:1024],
                            bc=biasq[:, 2 * hpl + 1:2 * hpl + 2]:
                            e.activation(o, i, AF.Exp, bias=bc, scale=0.125),
                            "sACT", 1)
                    exp_prev = veB
                    if debug and hg == 0 and hpl == 1 and qh == 0 and kt == 0:
                        W("vector", "sACT", veB)
                        vd1 = I("vector", lambda e: e.tensor_copy(
                            mb[:, :], psS[0:128, 0:512]), "sDV", 1)
                        vd2 = I("vector", lambda e: e.tensor_copy(
                            rb[:, :], psS[0:128, 512:1024]), "sDV", 1)
                        W("sync", "sDV", vd2)
                        I("sync", lambda e: e.dma_start(
                            dbg["d_s0"][:, 0:512], mb[:, :]), "sST", 16)
                        I("sync", lambda e: e.dma_start(
                            dbg["d_s0"][:, 512:1024], rb[:, :]), "sST", 16)
                        W("gpsimd", "sDV", vd2)
                        vd3 = I("gpsimd", lambda e: e.dma_start(
                            dbg["d_p0"], ptA[:, :]), "sST", 16)
                        W("tensor", "sDV", vd2)
                        W("tensor", "sST", vd3)
                    qs = _qs_min(kt)
                    m0, m1 = max(a, qs), min(b_, qs + 256)
                    W("vector", "sACT", veB)
                    if m0 < m1:
                        mw = m1 - m0
                        ml = m0 - qh * 1024
                        I("vector", lambda e, o=ptA[:, ml:ml + mw],
                          i0=ptA[:, ml:ml + mw],
                          i1=trim[:, kt * 256 + m0 - qs:
                                  kt * 256 + m0 - qs + mw]:
                          e.tensor_tensor(o, i0, i1, ALU.mult))
                        chain("vector", lambda e, o=ptB[:, ml:ml + mw],
                              i0=ptB[:, ml:ml + mw],
                              i1=trim[:, kt * 256 + m0 - qs:
                                      kt * 256 + m0 - qs + mw]:
                              e.tensor_tensor(o, i0, i1, ALU.mult))
                        vmsk = I("vector", lambda e: e.engine_nop(), "sDV", 1)
                    else:
                        vmsk = I("vector", lambda e: e.engine_nop(), "sDV", 1)
                    prev = (kt, l0, vmsk)
                pkt, pl0, vmsk = prev
                vpv = emit_pv(pkt, pl0, vmsk, pkt == kt0)
                W("tensor", "sGP", sGP_z)
                for bk in range(4):
                    vpv = I("tensor", lambda e, o=psw[0:1, bk * 512:bk * 512 + 1]:
                            e.matmul(o, zcol[0:1, 0:1], zcol[0:1, 0:1],
                                     start=False, stop=True), "sPE", 1)
                if debug and hg == 0 and hpl == -95:
                    W("vector", "sPE", vpv)
                    vdbg = I("vector", lambda e: e.tensor_copy(
                        mb[:, :] if False else sqtmp[:, :].rearrange("p t -> p t"),
                        psw[0:128, 0:2048]), "sDV", 1)
                    W("gpsimd", "sDV", vdbg)
                    vdbg2 = I("gpsimd", lambda e, n=f"d_psw{qh}": e.dma_start(
                        dbg[n], sqtmp[:, :]), "sST", 16)
                    W("tensor", "sST", vdbg2)
                W("vector", "sPE", vpv)
                vfence = I("vector", lambda e: e.engine_nop(), "sDV", 1)
                W("tensor", "sDV", vfence)
                vev_last = 0
                for hh in range(2):
                    region = hh * 1024
                    W("vector", "sPE", vpv)
                    chain("vector", lambda e, o=rowX[:, 0:1024],
                          i=psw[64:65, region:region + 1024]:
                          e.reciprocal(o, i))
                    vrz = I("vector", lambda e: e.engine_nop(), "sDV", 1)
                    if debug and hg == 0 and hpl == -96 and hh == 0:
                        W("sync", "sDV", vrz)
                        vzz = I("sync", lambda e, n=f"d_rz{qh}": e.dma_start(
                            dbg[n], rowX[:, 0:1024]), "sST", 16)
                        W("vector", "sST", vzz)
                    for qc in range(2):
                        vbc = mmg(psS[0:64, 0:512],
                                  [(onesr[0:1, 0:64],
                                    rowX[0:1, qc * 512:(qc + 1) * 512])],
                                  [("sDV", vrz), ("sACT", P.val("sACT"))])
                        W("scalar", "sPE", vbc)
                        if vev_last:
                            W("scalar", "sDV", vev_last)
                        vcp = I("scalar", lambda e:
                                e.copy(rb[0:64, :], psS[0:64, 0:512]),
                                "sACT", 1)
                        W("vector", "sACT", vcp)
                        dst = attn[64 * hh:64 * (hh + 1), hp,
                                   qh * 1024 + qc * 512:
                                   qh * 1024 + (qc + 1) * 512]
                        vev_last = I("vector", lambda e, o=dst,
                                     i0=psw[0:64, region + qc * 512:
                                            region + (qc + 1) * 512],
                                     i1=rb[0:64, :]:
                                     e.tensor_tensor(o, i0, i1, ALU.mult),
                                     "sDV", 1)
                W("tensor", "sDV", P.val("sDV"))
                W("tensor", "sACT", P.val("sACT"))
        W("gpsimd", "sDV", P.val("sDV"))
        W("sync", "sDV", P.val("sDV"))
        if debug >= 2 and hg == 0:
            vda = I("gpsimd", lambda e: e.dma_start(dbg["d_attnE"],
                    attn[:, 0:4, :].rearrange("p c t -> p (c t)")), "sST", 16)
            W("tensor", "sST", vda)
        regA.close()
        regH.close()

    vdv_attn = P.val("sDV")
    if debug >= 2:
        W("gpsimd", "sDV", vdv_attn)
        I("gpsimd", lambda e: e.dma_start(dbg["d_attn"],
          attn[:, :, :].rearrange("p c t -> p (c t)")), "sST", 16)
        W("sync", "sDV", vdv_attn)
        I("sync", lambda e: e.dma_start(dbg["d_u0"], st16[:, :]), "sST", 16)

    out_acc = SBx(es, "out_acc", [128, 8, TOWN])
    ln1 = SBx(es, "ln1", [128, 8, TOWN], BF16)
    regO = ExitStack()
    wob = SBx(regO, "wob", [128, 8, 1024], BF16, side="right")

    for dc in range(8):
        I("sync", lambda e, d=out_acc[:, dc, :],
          s=xown[dc * 128:(dc + 1) * 128, :]: e.dma_start(d, s), "sMx", 16)
    sM_xown = P.val("sMx")
    W("gpsimd", "sPE", P.val("sPE"))
    W("gpsimd", "sDV", P.val("sDV"))
    W("gpsimd", "sACT", P.val("sACT"))
    vwo = None
    for dc in range(8):
        vwo = I("gpsimd", lambda e, d=wob[:, dc, :],
                s=woT[dc * 128:(dc + 1) * 128, :]: e.dma_start(d, s), "sWO", 16)

    pb_free = [0, 0, 0, 0]
    k = 0
    for dc in range(8):
        for qc in range(4):
            bank = k % 4
            k += 1
            vmm = mmg(pst4[bank][:, 0:512],
                      [(wob[:, ic, dc * 128:(dc + 1) * 128],
                        attn[:, ic, qc * 512:(qc + 1) * 512]) for ic in range(8)],
                      [("sWO", vwo), ("sDV", vdv_attn), ("sMx", sM_xown),
                       ("sDV", pb_free[bank])])
            W("vector", "sPE", vmm)
            pb_free[bank] = I("vector",
                              lambda e, o=out_acc[:, dc, qc * 512:(qc + 1) * 512],
                              i0=out_acc[:, dc, qc * 512:(qc + 1) * 512],
                              i1=pst4[bank][:, 0:512]:
                              e.tensor_tensor(o, i0, i1, ALU.add), "sDV", 1)
    vdv_r1 = P.val("sDV")
    if debug:
        W("sync", "sDV", vdv_r1)
        I("sync", lambda e: e.dma_start(dbg["d_r1"],
          out_acc[:, :, :].rearrange("p c t -> p (c t)")), "sST", 16)

    def layer_norm(src_fp32, dst_bf16, vdv_ready, apply_bf16=True,
                   store_dram=None):
        W("vector", "sDV", vdv_ready)
        vmm_prev = 0
        for dc in range(8):
            if vmm_prev:
                W("vector", "sPE", vmm_prev)
            vcp = I("vector", lambda e, o=dst_bf16[:, dc, :], i=src_fp32[:, dc, :]:
                    e.tensor_copy(o, i), "sDV", 1)
            vsq = I("vector", lambda e, o=sqtmp[:, :], i=dst_bf16[:, dc, :]:
                    e.tensor_tensor(o, i, i, ALU.mult), "sDV", 1)
            for qc in range(4):
                mmg(psw[0:1, qc * 512:(qc + 1) * 512],
                    [(onecb[:, :], dst_bf16[:, dc, qc * 512:(qc + 1) * 512])],
                    [("sDV", vcp)], first=(dc == 0))
                vmm_prev = mmg(psw[32:33, qc * 512:(qc + 1) * 512],
                               [(onecb[:, :], sqtmp[:, qc * 512:(qc + 1) * 512])],
                               [("sDV", vsq)], first=(dc == 0))
        vpe_st = P.val("sPE")
        vst_all = 0
        vc2 = 0
        vap_last = 0
        for th in range(2):
            c0 = th * 1024
            W("vector", "sPE", vpe_st)
            I("vector", lambda e, o=mrow[:, :], i=psw[0:1, c0:c0 + 1024]:
              e.tensor_scalar(o, i, 1.0 / D, None, ALU.mult))
            I("vector", lambda e, o=vrow[:, :], i=psw[32:33, c0:c0 + 1024]:
              e.tensor_scalar(o, i, 1.0 / D, None, ALU.mult))
            I("vector", lambda e: e.tensor_tensor(rrow[:, :], mrow[:, :],
                                                  mrow[:, :], ALU.mult))
            I("vector", lambda e: e.tensor_tensor(vrow[:, :], vrow[:, :],
                                                  rrow[:, :], ALU.subtract))
            I("vector", lambda e: e.tensor_scalar(vrow[:, :], vrow[:, :], EPS,
                                                  None, ALU.add))
            vvar = I("vector", lambda e: e.reciprocal(vrow[:, :], vrow[:, :]),
                     "sDV", 1)
            W("scalar", "sDV", vvar)
            vsr = I("scalar", lambda e: e.activation(rrow[:, :], vrow[:, :],
                                                     AF.Sqrt), "sACT", 1)
            for qc in range(2):
                vb1 = mmg(psS[0:128, 0:512],
                          [(onesr[0:1, :], mrow[0:1, qc * 512:(qc + 1) * 512])],
                          [("sACT", vsr), ("sDV", vc2)])
                vb2 = mmg(psS2[0:128, 0:512],
                          [(onesr[0:1, :], rrow[0:1, qc * 512:(qc + 1) * 512])],
                          [])
                W("vector", "sPE", vb2)
                if vap_last:
                    W("vector", "sDV", vap_last)
                vc1 = I("vector", lambda e: e.tensor_copy(mb[:, :], psS[:, 0:512]),
                        "sDV", 1)
                vc2 = I("vector", lambda e: e.tensor_copy(rb[:, :], psS2[:, 0:512]),
                        "sDV", 1)
                cc0 = c0 + qc * 512
                for dc in range(8):
                    if store_dram is not None and vst_all:
                        W("vector", "sST", vst_all)
                    I("vector", lambda e, o=sqtmp[:, 0:512],
                      i0=src_fp32[:, dc, cc0:cc0 + 512], i1=mb[:, :]:
                      e.tensor_tensor(o, i0, i1, ALU.subtract))
                    if store_dram is None:
                        vap_last = I("vector", lambda e,
                                     o=dst_bf16[:, dc, cc0:cc0 + 512],
                                     i0=sqtmp[:, 0:512], i1=rb[:, :]:
                                     e.tensor_tensor(o, i0, i1, ALU.mult),
                                     "sDV", 1)
                    else:
                        vap_last = I("vector", lambda e, o=ost[:, :],
                                     i0=sqtmp[:, 0:512], i1=rb[:, :]:
                                     e.tensor_tensor(o, i0, i1, ALU.mult),
                                     "sDV", 1)
                        W("sync", "sDV", vap_last)
                        vst_all = I("sync", lambda e,
                                    d=store_dram[dc * 128:(dc + 1) * 128,
                                                 cc0:cc0 + 512],
                                    s=ost[:, :]: e.dma_start(d, s), "sST", 16)
        return P.val("sDV")

    vdv_ln1 = layer_norm(out_acc, ln1, vdv_r1)
    if debug:
        W("gpsimd", "sDV", vdv_ln1)
        I("gpsimd", lambda e: e.dma_start(dbg["d_ln1"],
          ln1[:, :, :].rearrange("p c t -> p (c t)")), "sST", 16)
    regO.close()
    regAttn.close()

    scores = SBx(es, "scores", [8, TOWN])
    gall = SBx(es, "gall", [64, TOWN])
    gmask = SBx(es, "gmask", [64, TOWN], BF16)
    regG = ExitStack()
    gwb = SBx(regG, "gwb", [128, 8, 8], BF16, side="right")
    W("gpsimd", "sPE", P.val("sPE"))
    vgw = None
    for dc in range(8):
        vgw = I("gpsimd", lambda e, d=gwb[:, dc, :],
                s=gwT[dc * 128:(dc + 1) * 128, :]: e.dma_start(d, s), "sWV", 16)
    for qc in range(4):
        mmg(psw[0:8, qc * 512:(qc + 1) * 512],
            [(gwb[:, dc, :], ln1[:, dc, qc * 512:(qc + 1) * 512])
             for dc in range(8)],
            [("sWV", vgw), ("sDV", vdv_ln1)])
    vpe_gl = P.val("sPE")
    W("scalar", "sPE", vpe_gl)
    vex = I("scalar", lambda e: e.activation(scores[:, :], psw[0:8, 0:TOWN],
                                             AF.Exp), "sACT", 1)
    for qc in range(4):
        mmg(psw[32:33, qc * 512:(qc + 1) * 512],
            [(onecf[0:8, 0:1], scores[:, qc * 512:(qc + 1) * 512])],
            [("sACT", vex), ("sDV", vdv_ln1)])
    vpe_gs = P.val("sPE")
    vsc = 0
    for th in range(2):
        c0 = th * 1024
        W("vector", "sPE", vpe_gs)
        vrc = I("vector", lambda e, o=mrow[:, :], i=psw[32:33, c0:c0 + 1024]:
                e.reciprocal(o, i), "sDV", 1)
        for qc in range(2):
            mmg(psS2[0:8, qc * 512:(qc + 1) * 512],
                [(onesr[0:1, 0:8], mrow[0:1, qc * 512:(qc + 1) * 512])],
                [("sDV", vrc), ("sDV", vsc)])
        W("vector", "sPE", P.val("sPE"))
        vsc = I("vector", lambda e, o=scores[:, c0:c0 + 1024],
                i0=scores[:, c0:c0 + 1024], i1=psS2[0:8, 0:1024]:
                e.tensor_tensor(o, i0, i1, ALU.mult), "sDV", 1)
    if debug:
        W("sync", "sDV", vsc)
        I("sync", lambda e: e.dma_start(dbg["d_scores"], scores[:, :]), "sST", 16)

    W("gpsimd", "sDV", vsc)
    vgi = I("gpsimd", lambda e: e.dma_start(ccin[:, :], scores[:, :]), "sGP", 16)
    W("gpsimd", "sGP", P.val("sGP"))
    vcc = I("gpsimd", lambda e: e.collective_compute(
        "AllGather", ALU.bypass, replica_groups=[list(range(8))],
        ins=[ccin[:, :]], outs=[ccout[:, :]]), "sCC", 1)
    W("gpsimd", "sCC", vcc)
    vgo = I("gpsimd", lambda e: e.dma_start(gall[:, :], ccout[:, :]), "sGP", 16)

    chain("vector", lambda e: e.memset(st16[:, 8:9], 0.0))
    chain("vector", lambda e: e.memset(st16[:, 9:10], 2.0))
    chain("vector", lambda e: e.memset(st16[:, 10:11], 1.0))
    chain("vector", lambda e: e.memset(midc[:, :], 1.0))
    vmem_sh = chain("vector", lambda e: e.memset(rowX[:, :], 1.0))
    W("vector", "sGP", vgo)
    for it in range(NITER):
        if it > 0:
            W("vector", "sDM2", 16 * it)
        I("vector", lambda e: e.tensor_scalar(gmask[:, :], gall[:, :],
                                              midc[:, :], None, ALU.is_ge))
        vcn = I("vector", lambda e: e.reduce_sum(cn64[:, :], gmask[:, :], AX.X),
                "sDV", 1)
        W("sync", "sDV", vcn)
        vdm = I("sync", lambda e: e.dma_start(crow[:, :], cn64[:, :]), "sDM", 16)
        W("vector", "sDM", vdm)
        chain("vector", lambda e: e.tensor_reduce(st16[:, 11:12], crow[:, :],
                                                  AX.X, ALU.add))
        chain("vector", lambda e: e.tensor_scalar(st16[:, 12:13], st16[:, 11:12],
                                                  KSEL, None, ALU.is_ge))
        chain("vector", lambda e: e.tensor_tensor(st16[:, 13:14], st16[:, 10:11],
                                                  st16[:, 8:9], ALU.subtract))
        chain("vector", lambda e: e.tensor_tensor(st16[:, 13:14], st16[:, 13:14],
                                                  st16[:, 12:13], ALU.mult))
        chain("vector", lambda e: e.tensor_tensor(st16[:, 8:9], st16[:, 8:9],
                                                  st16[:, 13:14], ALU.add))
        chain("vector", lambda e: e.tensor_tensor(st16[:, 14:15], st16[:, 9:10],
                                                  st16[:, 10:11], ALU.subtract))
        chain("vector", lambda e: e.tensor_tensor(st16[:, 14:15], st16[:, 14:15],
                                                  st16[:, 12:13], ALU.mult))
        chain("vector", lambda e: e.tensor_tensor(st16[:, 9:10], st16[:, 10:11],
                                                  st16[:, 14:15], ALU.add))
        chain("vector", lambda e: e.tensor_tensor(st16[:, 10:11], st16[:, 8:9],
                                                  st16[:, 9:10], ALU.add))
        chain("vector", lambda e: e.tensor_scalar(st16[:, 10:11], st16[:, 10:11],
                                                  0.5, None, ALU.mult))
        last = (it == NITER - 1)
        srcc = st16[:, 8:9] if last else st16[:, 10:11]
        vmr = I("vector", lambda e, s_=srcc: e.tensor_scalar(
            midrow[:, :], on64r[:, :], s_, None, ALU.mult), "sDV", 1)
        if not last:
            W("sync", "sDV", vmr)
            I("sync", lambda e: e.dma_start(midc[:, :], midrow[:, :]), "sDM2", 16)
    W("sync", "sDV", P.val("sDV"))
    vlc = I("sync", lambda e: e.dma_start(locol[:, :], midrow[0:1, 0:8]),
            "sDM", 16)
    W("vector", "sDM", vlc)
    I("vector", lambda e: e.tensor_scalar(gmask[0:8, :], scores[:, :],
                                          locol[:, :], None, ALU.is_ge))
    vgt = I("vector", lambda e: e.tensor_tensor(scores[:, :], gmask[0:8, :],
                                                scores[:, :], ALU.mult),
            "sDV", 1)
    if debug:
        W("sync", "sDV", vgt)
        I("sync", lambda e: e.dma_start(dbg["d_gate"], scores[:, :]), "sST", 16)
    regG.close()

    regF = ExitStack()
    w1b = [SBx(regF, f"w1b{i}", [128, 8, 512], BF16, side="right") for i in range(2)]
    w2b = SBx(regF, "w2b", [128, 8, 512], BF16, side="right")
    hbf = SBx(regF, "hbf", [128, 8, TOWN], BF16, side="right")
    gb = SBx(regF, "gb", [128, TOWN], BF16, side="right")

    def w1_src(e_, fq, fo):
        a = fq * 1024 + fo * 512
        if e_ < 8:
            return lambda dc: w1d[e_, dc * 128:(dc + 1) * 128, a:a + 512]
        return lambda dc: ws1d[dc * 128:(dc + 1) * 128, a:a + 512]

    def w2_src(e_, fq, dh):
        if e_ < 8:
            return lambda fc: w2d[e_, fq * 1024 + fc * 128:
                                  fq * 1024 + (fc + 1) * 128,
                                  dh * 512:(dh + 1) * 512]
        return lambda fc: ws2d[fq * 1024 + fc * 128: fq * 1024 + (fc + 1) * 128,
                               dh * 512:(dh + 1) * 512]

    w1_loaded = {}
    w1_cons = {}

    def load_w1(step):
        e_, fq, fo = EORDER[step // 8], (step % 8) // 2, step % 2
        if step - 2 >= 0:
            W("gpsimd", "sPE", w1_cons.get(step - 2, 0))
        src = w1_src(e_, fq, fo)
        sname = "sW1a" if step % 2 == 0 else "sW1b"
        v = None
        for dc in range(8):
            v = I("gpsimd", lambda e, d=w1b[step % 2][:, dc, :], s=src(dc):
                  e.dma_start(d, s), sname, 16)
        w1_loaded[step] = (sname, v)

    w2_cons = [0]

    def load_w2(e_, fq, dh):
        if w2_cons[0]:
            W("gpsimd", "sPE", w2_cons[0])
        src = w2_src(e_, fq, dh)
        v = None
        for fc in range(8):
            v = I("gpsimd", lambda e, d=w2b[:, fc, :], s=src(fc):
                  e.dma_start(d, s), "sW2s", 16)
        return v

    W("gpsimd", "sPE", vpe_gl)
    load_w1(0)
    hA_free = [0, 0, 0, 0]
    yB_free = [0, 0, 0, 0]

    for step_e, e_ in enumerate(EORDER):
        for th in range(2):
            if e_ < 8:
                W("sync", "sDV", P.val("sDV"))
                stw = I("sync", lambda e, r=e_, c0=th * 1024: e.dma_start(
                    rowX[:, :], scores[r:r + 1, c0:c0 + 1024]), "sDM", 16)
                stg_w = [("sDM", stw)]
            else:
                stg_w = [("sCH", vmem_sh)]
            for qc in range(2):
                tc = th * 2 + qc
                vbc = mmg(pswq[tc][0:128, :],
                          [(onesr[0:1, :], rowX[0:1, qc * 512:(qc + 1) * 512])],
                          stg_w + [("sDV", yB_free[tc])])
                W("vector", "sPE", vbc)
                yB_free[tc] = I("vector",
                                lambda e, o=gb[:, tc * 512:(tc + 1) * 512],
                                i=pswq[tc][:, :]: e.tensor_copy(o, i), "sDV", 1)

        for fq in range(4):
            for fo in range(2):
                step = step_e * 8 + fq * 2 + fo
                bi = step % 2
                sw1n, vw1 = w1_loaded[step]
                if step + 1 < NE * 8:
                    load_w1(step + 1)
                vlast = None
                for ftl in range(4):
                    ft = fo * 4 + ftl
                    for tc in range(4):
                        bank = (ftl * 4 + tc) % 4
                        vmm = mmg(pst4[bank][:, 0:512],
                                  [(w1b[bi][:, dc, ftl * 128:(ftl + 1) * 128],
                                    ln1[:, dc, tc * 512:(tc + 1) * 512])
                                   for dc in range(8)],
                                  [(sw1n, vw1), ("sDV", vdv_ln1),
                                   ("sACT", hA_free[bank])])
                        vlast = vmm
                        W("scalar", "sPE", vmm)
                        bcol = b1sb[:, e_, fq * 8 + ft: fq * 8 + ft + 1]
                        hA_free[bank] = I("scalar", lambda e,
                                          o=hbf[:, ft, tc * 512:(tc + 1) * 512],
                                          i=pst4[bank][:, 0:512], bc=bcol:
                                          e.activation(o, i, AF.Gelu_apprx_tanh,
                                                       bias=bc), "sACT", 1)
                w1_cons[step] = vlast
            vact_h = P.val("sACT")
            for dh in range(2):
                vw2 = load_w2(e_, fq, dh)
                vlb = None
                for dtl in range(4):
                    dt = dh * 4 + dtl
                    for tc in range(4):
                        vmm = mmg(pswq[tc][0:128, :],
                                  [(w2b[:, fc, dtl * 128:(dtl + 1) * 128],
                                    hbf[:, fc, tc * 512:(tc + 1) * 512])
                                   for fc in range(8)],
                                  [("sW2s", vw2), ("sACT", vact_h),
                                   ("sDV", yB_free[tc])])
                        vlb = vmm
                        W("vector", "sPE", vmm)
                        bval = b2sb[:, e_, dt:dt + 1] if fq == 3 else 0.0
                        I("vector", lambda e, o=sqtmp[:, 0:512],
                          i0=pswq[tc][:, :], sc=bval,
                          i1=gb[:, tc * 512:(tc + 1) * 512]:
                          e.scalar_tensor_tensor(o, i0, sc, i1, ALU.add,
                                                 ALU.mult))
                        yB_free[tc] = I("vector", lambda e,
                                        o=out_acc[:, dt, tc * 512:(tc + 1) * 512],
                                        i0=out_acc[:, dt,
                                                   tc * 512:(tc + 1) * 512],
                                        i1=sqtmp[:, 0:512]:
                                        e.tensor_tensor(o, i0, i1, ALU.add),
                                        "sDV", 1)
                w2_cons[0] = vlb

    vdv_ffn = P.val("sDV")
    regF.close()

    ost = SBx(es, "ost", [128, 512])
    layer_norm(out_acc, ln1, vdv_ffn, store_dram=out)
    W("sync", "sST", P.val("sST"))

    with nc.Block() as block:
        def mk(eng_name):
            def body(e):
                for (en, fn) in P.items:
                    if en == eng_name:
                        fn(e)
            return body
        block.tensor(mk("tensor"))
        block.vector(mk("vector"))
        block.scalar(mk("scalar"))
        block.gpsimd(mk("gpsimd"))
        block.sync(mk("sync"))

    es.close()
    return nc


def _rope_tables():
    t = np.arange(T, dtype=np.float64)
    inv = 1.0 / (10000.0 ** (np.arange(0, HD, 2, dtype=np.float64) / HD))
    ang = np.outer(inv, t)
    c = np.cos(ang); s = np.sin(ang)
    cos128 = np.zeros((128, T), np.float64)
    sin128 = np.zeros((128, T), np.float64)
    for r in range(64):
        g, i = divmod(r, 32)
        if i < 16:
            d = 16 * g + i
            cos128[r] = c[d]; cos128[64 + r] = c[d]
            sin128[r] = -s[d]; sin128[64 + r] = -s[d]
        else:
            d = 16 * g + i - 16
            cos128[r] = c[d]; cos128[64 + r] = c[d]
            sin128[r] = s[d]; sin128[64 + r] = s[d]
    return cos128, sin128


def _perm_head():
    p = []
    for h in range(H):
        base = h * HD
        p += [base + i for i in range(16)]
        p += [base + 32 + i for i in range(16)]
        p += [base + 16 + i for i in range(16)]
        p += [base + 48 + i for i in range(16)]
    return np.array(p)


def _amask_for(sub):
    obl = _owned_blocks(sub)
    m = np.zeros((128, NB * 256), np.float32)
    kk = np.arange(128)[:, None]
    qq = np.arange(128)[None, :]
    tri = (kk <= qq).astype(np.float32)
    for kt in range(NB):
        qsm = _qs_min(kt)
        for colblk in range(2):
            a = qsm // 128 + colblk
            if a >= 16:
                continue
            j = obl[a]
            if j < kt:
                blk = np.zeros((128, 128), np.float32)
            elif j == kt:
                blk = tri
            else:
                blk = np.ones((128, 128), np.float32)
            m[:, kt * 256 + colblk * 128: kt * 256 + (colblk + 1) * 128] = blk
    return m


_nc_cache = {}


def kernel(L0_attn_x, L0_attn_wq, L0_attn_wk, L0_attn_wv, L0_attn_wo,
           L0_ffn_gate_weight, L0_ffn_W1, L0_ffn_b1, L0_ffn_W2, L0_ffn_b2,
           L0_ffn_Ws1, L0_ffn_bs1, L0_ffn_Ws2, L0_ffn_bs2):
    x = np.ascontiguousarray(np.asarray(L0_attn_x, dtype=np.float32))
    wq = np.asarray(L0_attn_wq, np.float32); wk = np.asarray(L0_attn_wk, np.float32)
    wv = np.asarray(L0_attn_wv, np.float32); wo = np.asarray(L0_attn_wo, np.float32)
    gw = np.asarray(L0_ffn_gate_weight, np.float32)
    w1 = np.ascontiguousarray(np.asarray(L0_ffn_W1, np.float32))
    w2 = np.ascontiguousarray(np.asarray(L0_ffn_W2, np.float32))
    b1 = np.asarray(L0_ffn_b1, np.float32); b2 = np.asarray(L0_ffn_b2, np.float32)
    ws1 = np.ascontiguousarray(np.asarray(L0_ffn_Ws1, np.float32))
    ws2 = np.ascontiguousarray(np.asarray(L0_ffn_Ws2, np.float32))
    bs1 = np.asarray(L0_ffn_bs1, np.float32); bs2 = np.asarray(L0_ffn_bs2, np.float32)

    perm = _perm_head()
    wqT = np.ascontiguousarray(wq[perm].T)
    wkT = np.ascontiguousarray(wk[perm].T)
    wvT = np.ascontiguousarray(wv.T)
    woT = np.ascontiguousarray(wo.T)
    gwT = np.ascontiguousarray(gw.T)
    b1a = np.zeros((128, NE, F // 128), np.float32)
    b2a = np.zeros((128, NE, D // 128), np.float32)
    b1a[:, :8, :] = b1.reshape(E, F // 128, 128).transpose(2, 0, 1)
    b1a[:, 8, :] = bs1.reshape(F // 128, 128).T
    b2a[:, :8, :] = b2.reshape(E, D // 128, 128).transpose(2, 0, 1)
    b2a[:, 8, :] = bs2.reshape(D // 128, 128).T

    cos128, sin128 = _rope_tables()
    cosKb = cos128.astype(BF16NP)
    sinKb = sin128.astype(BF16NP)
    onc_b = np.ones((128, 1), BF16NP)
    on33 = np.zeros((128, 33), np.float32)
    on33[0:64, 0] = 1.0
    on33[64:128, 32] = 1.0
    on33 = on33.astype(BF16NP)
    onc_f = np.ones((128, 1), np.float32)
    onr_f = np.ones((1, 128), np.float32)
    on64 = np.ones((1, 64), np.float32)

    in_maps = []
    for c in range(8):
        b, sub = divmod(c, 2)
        obl = _owned_blocks(sub)
        own_tok = np.concatenate([np.arange(j * 128, (j + 1) * 128) for j in obl])
        xtb = np.ascontiguousarray(x[b].T)
        xownb = np.ascontiguousarray(xtb[:, own_tok])
        cosQ = np.ascontiguousarray(cos128[:, own_tok]).astype(BF16NP)
        sinQ = np.ascontiguousarray(sin128[:, own_tok]).astype(BF16NP)
        am = _amask_for(sub).astype(BF16NP)
        in_maps.append({
            "xt": xtb, "xown": xownb, "wqT": wqT, "wkT": wkT, "wvT": wvT,
            "woT": woT, "gwT": gwT, "w1": w1, "w2": w2, "ws1": ws1, "ws2": ws2,
            "b1a": b1a, "b2a": b2a, "cosK": cosKb, "sinK": sinKb,
            "cosQ": cosQ, "sinQ": sinQ, "amask": am, "onc_b": onc_b, "on33": on33,
            "onc_f": onc_f, "onr_f": onr_f, "on64": on64,
        })

    key = ("nc", DEBUG)
    if key not in _nc_cache:
        _nc_cache[key] = build_nc(debug=DEBUG)
    nc = _nc_cache[key]
    res = run_bass_kernel_spmd(nc, in_maps, core_ids=list(range(8)))

    outp = np.zeros((B, T, D), np.float32)
    for c in range(8):
        b, sub = divmod(c, 2)
        obl = _owned_blocks(sub)
        own_tok = np.concatenate([np.arange(j * 128, (j + 1) * 128) for j in obl])
        outp[b, own_tok, :] = res.results[c]["out"].T
    if DEBUG:
        kernel._dbg = res.results
    return outp


# version 06
# speedup vs baseline: 1.0075x; 1.0075x over previous
import numpy as np
import ml_dtypes
import concourse.bass as bass
import concourse.mybir as mybir
from concourse.bass_utils import run_bass_kernel_spmd

F32 = mybir.dt.float32
BF16 = mybir.dt.bfloat16
AF = mybir.ActivationFunctionType
ALU = mybir.AluOpType
AX = mybir.AxisListType
BF16NP = ml_dtypes.bfloat16

B, T, D = 4, 4096, 1024
H, HD = 16, 64
NB = T // 128
TOWN = 2048
E, F = 8, 4096
NE = 9
NITER = 42
KSEL = float(2 * B * T)
EPS = 1e-5
SHUF = [(i + 16) % 32 for i in range(32)]
EORDER = [8, 0, 1, 2, 3, 4, 5, 6, 7]

DEBUG = 0


def _owned_blocks(sub):
    return [j for j in range(NB) if j % 4 in ((0, 3) if sub == 0 else (1, 2))]


def _qs_min(kt):
    c0 = len([j for j in _owned_blocks(0) if j < kt])
    c1 = len([j for j in _owned_blocks(1) if j < kt])
    return 128 * min(c0, c1)


class Plan:
    def __init__(self):
        self.items = []
        self.cnt = {}

    def on(self, eng, fn):
        self.items.append((eng, fn))

    def inc(self, sem, by=1):
        self.cnt[sem] = self.cnt.get(sem, 0) + by
        return self.cnt[sem]

    def val(self, sem):
        return self.cnt.get(sem, 0)


def build_nc(debug=0):
    from contextlib import ExitStack
    nc = bass.Bass("TRN2", target_bir_lowering=False)
    es = ExitStack()

    def SBx(stack, n, s, d=F32, side=None):
        return stack.enter_context(nc.sbuf_tensor(n, list(s), d, side=side))

    def din(name, shape, dt=F32):
        return nc.dram_tensor(name, list(shape), dt, kind="ExternalInput").ap()

    xt = din("xt", [D, T])
    xown = din("xown", [D, TOWN])
    wqT = din("wqT", [D, D]); wkT = din("wkT", [D, D])
    wvT = din("wvT", [D, D]); woT = din("woT", [D, D])
    gwT = din("gwT", [D, E])
    w1d = din("w1", [E, D, F]); w2d = din("w2", [E, F, D])
    ws1d = din("ws1", [D, F]); ws2d = din("ws2", [F, D])
    b1a = din("b1a", [128, NE, F // 128])
    b2a = din("b2a", [128, NE, D // 128])
    cosK = din("cosK", [128, T], BF16)
    sinK = din("sinK", [128, T], BF16)
    cosQ = din("cosQ", [128, TOWN], BF16)
    sinQ = din("sinQ", [128, TOWN], BF16)
    amask = din("amask", [128, NB * 256], BF16)
    onc_b = din("onc_b", [128, 1], BF16)
    on33 = din("on33", [128, 33], BF16)
    onc_f = din("onc_f", [128, 1])
    onr_f = din("onr_f", [1, 128])
    on64 = din("on64", [1, 64])

    out = nc.dram_tensor("out", [D, TOWN], F32, kind="ExternalOutput").ap()
    dbg = {}
    if debug:
        for nm, sh in [("d_psw0", [128, 2048]), ("d_psw1", [128, 2048]),
                       ("d_s0", [128, 1024]),
                       ("d_p0", [128, 1024]), ("d_z1", [128, 1024]),
                       ("d_p1", [128, 1024]), ("d_z4", [128, 64]),
                       ("d_z8", [128, 64]), ("d_z16", [128, 64]),
                       ("d_rz0", [1, 1024]), ("d_rz1", [1, 1024]),
                       ("d_attnE", [128, 4 * TOWN]),
                       ("d_qh", [128, 4 * TOWN]), ("d_kh", [128, 4 * T]),
                       ("d_vh", [128, NB * 520]), ("d_biasq", [128, 8]),
                       ("d_attn", [128, 8 * TOWN]), ("d_r1", [128, 8 * TOWN]),
                       ("d_ln1", [128, 8 * TOWN]), ("d_scores", [8, TOWN]),
                       ("d_gate", [8, TOWN]), ("d_u0", [1, 16])]:
            dbg[nm] = nc.dram_tensor(nm, sh, F32, kind="ExternalOutput").ap()

    ccin = nc.dram_tensor("ccin", [E, TOWN], F32)
    ccout = nc.dram_tensor("ccout", [64, TOWN], F32)

    mrow = SBx(es, "mrow", [1, 1024])
    vrow = SBx(es, "vrow", [1, 1024])
    rrow = SBx(es, "rrow", [1, 1024])
    rowX = SBx(es, "rowX", [1, 1024])
    onesr = SBx(es, "onesr", [1, 128])
    on64r = SBx(es, "on64r", [1, 64])
    onecb = SBx(es, "onecb", [128, 1], BF16)
    zcol = SBx(es, "zcol", [128, 1], BF16)
    on33b = SBx(es, "on33b", [128, 33], BF16)
    mqcol = SBx(es, "mqcol", [128, 4])
    mkcol = SBx(es, "mkcol", [128, 4])
    ubcol = SBx(es, "ubcol", [128, 4])
    ubrow = SBx(es, "ubrow", [1, 8])
    onecf = SBx(es, "onecf", [128, 1])
    biasq = SBx(es, "biasq", [128, 8])
    b1sb = SBx(es, "b1sb", [128, NE, F // 128])
    b2sb = SBx(es, "b2sb", [128, NE, D // 128])
    st16 = SBx(es, "st16", [1, 16])
    cn64 = SBx(es, "cn64", [64, 1])
    crow = SBx(es, "crow", [1, 64])
    midc = SBx(es, "midc", [64, 1])
    midrow = SBx(es, "midrow", [1, 64])
    locol = SBx(es, "locol", [8, 1])
    sqtmp = SBx(es, "sqtmp", [128, TOWN], BF16)
    mb = SBx(es, "mb", [128, 512])
    rb = SBx(es, "rb", [128, 512])
    regAttn = ExitStack()
    attn = SBx(regAttn, "attn", [128, 8, TOWN], BF16, side="right")

    psS = es.enter_context(nc.psum_tensor("psS", [128, 1024], F32))
    psS2 = es.enter_context(nc.psum_tensor("psS2", [128, 1024], F32))
    psw = es.enter_context(nc.psum_tensor("psw", [128, 2048], F32))
    pst4 = [psS[:, 0:512], psS[:, 512:1024], psS2[:, 0:512], psS2[:, 512:1024]]
    pswq = [psw[:, i * 512:(i + 1) * 512] for i in range(4)]

    sems = {}
    for n in ["sX", "sW", "sM", "sPE", "sACT", "sDV", "sGP", "sST",
              "sCH", "sDM", "sDM2", "sCC", "sW0", "sW1", "sW2", "sW3",
              "sWV", "sWO", "sMt", "sMk", "sMx", "sW1a", "sW1b", "sW2s"]:
        sems[n] = es.enter_context(nc.semaphore(n))

    P = Plan()

    def W(eng, sem, val):
        if val:
            P.on(eng, lambda e, s=sems[sem], v=val: e.wait_ge(s, v))

    def I(eng, fn, sem=None, by=1):
        if sem is None:
            P.on(eng, fn)
            return None
        v = P.inc(sem, by)
        P.on(eng, lambda e, f=fn, s=sems[sem], b=by: f(e).then_inc(s, b))
        return v

    def chain(eng, fn, sem="sCH"):
        v = P.inc(sem)
        def go(e, f=fn, s=sems[sem], vv=v):
            f(e).then_inc(s, 1)
            e.wait_ge(s, vv)
        P.on(eng, go)
        return v

    def mmg(out_ap, pairs, waits=(), first=True):
        for (sname, v) in waits:
            W("tensor", sname, v)
        n = len(pairs)
        for i in range(n - 1):
            l, r = pairs[i]
            I("tensor", lambda e, l_=l, r_=r, o=out_ap, s1=(i == 0 and first):
              e.matmul(o, l_, r_, start=s1, stop=False))
        l, r = pairs[n - 1]
        return I("tensor", lambda e, l_=l, r_=r, o=out_ap,
                 s1=(n == 1 and first):
                 e.matmul(o, l_, r_, start=s1, stop=True), "sPE", 1)

    I("gpsimd", lambda e: e.memset(zcol[:, :], 0.0), "sGP", 1)
    sGP_z = P.val("sGP")
    for (dst, src) in [(b1sb[:, :, :], b1a), (b2sb[:, :, :], b2a),
                       (onesr[:, :], onr_f), (on64r[:, :], on64),
                       (onecb[:, :], onc_b), (onecf[:, :], onc_f),
                       (on33b[:, :], on33)]:
        I("sync", lambda e, d=dst, s=src: e.dma_start(d, s), "sM", 16)
    sM_const = P.val("sM")

    vdv_attn_parts = []
    for hg in range(2):
        regH = ExitStack()
        q_h = SBx(regH, f"q_h{hg}", [128, 4, TOWN], BF16, side="right")
        k_h = SBx(regH, f"k_h{hg}", [128, 4, T], BF16, side="right")
        v_h = SBx(regH, f"v_h{hg}", [128, NB * 520], BF16, side="right")

        regP = ExitStack()
        cosKt = SBx(regP, f"cosKt{hg}", [128, T], BF16, side="right")
        sinKt = SBx(regP, f"sinKt{hg}", [128, T], BF16, side="right")
        cosQt = SBx(regP, f"cosQt{hg}", [128, TOWN], BF16, side="right")
        sinQt = SBx(regP, f"sinQt{hg}", [128, TOWN], BF16, side="right")
        xbf = SBx(regP, f"xbf{hg}", [128, 8, 1024], BF16, side="right")
        xqf = SBx(regP, f"xqf{hg}", [128, 8, 512], BF16, side="right")
        wsl = [SBx(regP, f"wsl{hg}_{i}", [128, 8, 128], BF16, side="right") for i in range(4)]
        wvb = SBx(regP, f"wvb{hg}", [128, 8, 512], BF16, side="right")
        rt2 = SBx(regP, f"rt2_{hg}", [128, 512], BF16, side="right")
        rt3 = SBx(regP, f"rt3_{hg}", [128, 512], BF16, side="right")
        rt4 = SBx(regP, f"rt4_{hg}", [128, 512], BF16, side="right")

        W("sync", "sPE", P.val("sPE"))
        W("sync", "sDV", P.val("sDV"))
        W("sync", "sACT", P.val("sACT"))
        for (dst, src) in [(cosKt[:, :], cosK), (sinKt[:, :], sinK),
                           (cosQt[:, :], cosQ), (sinQt[:, :], sinQ)]:
            I("sync", lambda e, d=dst, s=src: e.dma_start(d, s), "sMt", 16)
        W("vector", "sMt", P.val("sMt"))
        if hg > 0:
            W("gpsimd", "sPE", P.val("sPE"))
            W("gpsimd", "sDV", P.val("sDV"))
            W("gpsimd", "sACT", P.val("sACT"))
        vap = bass.AP(v_h, 64, [[520 * NB, 128], [520, NB], [65, 8], [1, 1]])
        vgp_ones = I("gpsimd", lambda e, a=vap: e.memset(a, 1.0), "sGP", 1)

        ps_free = [0, 0, 0, 0]
        ps_rot = [0]

        def next_bank():
            i = ps_rot[0] % 4
            ps_rot[0] += 1
            return i

        wsl_cons = [0, 0, 0, 0]
        wsl_rot = [0]

        def load_wslice(src_dram, gc):
            bi = wsl_rot[0] % 4
            wsl_rot[0] += 1
            if wsl_cons[bi]:
                W("gpsimd", "sPE", wsl_cons[bi])
            sname = f"sW{bi}"
            v = None
            for dc in range(8):
                v = I("gpsimd", lambda e, d=wsl[bi][:, dc, :],
                      s=src_dram[dc * 128:(dc + 1) * 128, gc * 128:(gc + 1) * 128]:
                      e.dma_start(d, s), sname, 16)
            return bi, v, sname

        def rope_evac(bank, dstap, cosap, sinap):
            pst = pst4[bank][:, 0:512]
            I("vector", lambda e, o=rt4[:, :], i=pst: e.tensor_copy(o, i))
            I("vector", lambda e, o=rt2[:, :], i=rt4[:, :], m=SHUF:
              e.stream_shuffle(o, i, m))
            I("vector", lambda e, o=rt3[:, :], i0=pst, i1=cosap:
              e.tensor_tensor(o, i0, i1, ALU.mult))
            I("vector", lambda e, o=rt2[:, :], i0=rt2[:, :], i1=sinap:
              e.tensor_tensor(o, i0, i1, ALU.mult))
            return I("vector", lambda e, o=dstap, i0=rt3[:, :], i1=rt2[:, :]:
                     e.tensor_tensor(o, i0, i1, ALU.add), "sDV", 1)

        first_vdma = [True]
        for sec in range(4):
            if sec > 0 or hg > 0:
                W("gpsimd", "sPE", P.val("sPE"))
            vx = None
            for dc in range(8):
                vx = I("gpsimd", lambda e, d=xbf[:, dc, :],
                       s=xt[dc * 128:(dc + 1) * 128, sec * 1024:(sec + 1) * 1024]:
                       e.dma_start(d, s), "sX", 16)
            for dc in range(8):
                vx = I("gpsimd", lambda e, d=xqf[:, dc, :],
                       s=xown[dc * 128:(dc + 1) * 128, sec * 512:(sec + 1) * 512]:
                       e.dma_start(d, s), "sX", 16)
            sXv = P.val("sX")

            for cl in range(4):
                gc = 4 * hg + cl
                bi, vw, swn = load_wslice(wkT, gc)
                for half in range(2):
                    bank = next_bank()
                    vmm = mmg(pst4[bank][:, 0:512],
                              [(wsl[bi][:, dc, :],
                                xbf[:, dc, half * 512:(half + 1) * 512])
                               for dc in range(8)],
                              [(swn, vw), ("sX", sXv), ("sDV", ps_free[bank])])
                    wsl_cons[bi] = vmm
                    c0 = sec * 1024 + half * 512
                    W("vector", "sPE", vmm)
                    ps_free[bank] = rope_evac(bank, k_h[:, cl, c0:c0 + 512],
                                              cosKt[:, c0:c0 + 512],
                                              sinKt[:, c0:c0 + 512])
            for cl in range(4):
                gc = 4 * hg + cl
                bi, vw, swn = load_wslice(wqT, gc)
                bank = next_bank()
                vmm = mmg(pst4[bank][:, 0:512],
                          [(wsl[bi][:, dc, :], xqf[:, dc, :]) for dc in range(8)],
                          [(swn, vw), ("sX", sXv), ("sDV", ps_free[bank])])
                wsl_cons[bi] = vmm
                c0 = sec * 512
                W("vector", "sPE", vmm)
                ps_free[bank] = rope_evac(bank, q_h[:, cl, c0:c0 + 512],
                                          cosQt[:, c0:c0 + 512],
                                          sinQt[:, c0:c0 + 512])

            if sec == 0:
                W("gpsimd", "sPE", P.val("sPE")) if hg > 0 else None
                vwv = None
                for dc in range(8):
                    vwv = I("gpsimd", lambda e, d=wvb[:, dc, :],
                            s=wvT[dc * 128:(dc + 1) * 128,
                                  hg * 512:(hg + 1) * 512]:
                            e.dma_start(d, s), "sWV", 16)
                vwv_val = vwv
            for tt in range(8):
                bank = next_bank()
                vmm = mmg(pst4[bank][:, 0:512],
                          [(xbf[:, dc, tt * 128:(tt + 1) * 128], wvb[:, dc, :])
                           for dc in range(8)],
                          [("sWV", vwv_val), ("sX", sXv), ("sDV", ps_free[bank])])
                kt = sec * 8 + tt
                dstap = bass.AP(v_h, kt * 520, [[520 * NB, 128], [65, 8], [1, 64]])
                W("vector", "sPE", vmm)
                if first_vdma[0]:
                    W("vector", "sGP", vgp_ones)
                    first_vdma[0] = False
                srcap = pst4[bank][:, 0:512].rearrange("p (h x) -> p h x", h=8)
                ps_free[bank] = I("vector", lambda e, o=dstap, i=srcap:
                                  e.tensor_copy(o, i), "sDV", 1)

        vdv_red = 0
        for (srct, iscol, nch, ncols) in [(q_h, mqcol, 4, TOWN),
                                          (k_h, mkcol, 4, T)]:
            for dc in range(nch):
                nhalf = ncols // TOWN
                vmmu = 0
                for ch in range(nhalf):
                    if vdv_red:
                        W("vector", "sPE", vmmu) if False else None
                    vs = I("vector", lambda e, o=sqtmp[:, :],
                           i=srct[:, dc, ch * TOWN:(ch + 1) * TOWN]:
                           e.tensor_tensor(o, i, i, ALU.mult), "sDV", 1)
                    for qc in range(4):
                        vmmu = mmg(psw[0:33, qc * 512:(qc + 1) * 512],
                                   [(on33b[:, :],
                                     sqtmp[:, qc * 512:(qc + 1) * 512])],
                                   [("sDV", vs), ("sDV", vdv_red)],
                                   first=(ch == 0))
                W("vector", "sPE", vmmu)
                vdv_red = I("vector", lambda e, o=iscol[0:33, dc:dc + 1],
                            i=psw[0:33, 0:TOWN]:
                            e.reduce_max(o, i, AX.X), "sDV", 1)
        W("vector", "sDV", vdv_red)
        vub = I("vector", lambda e: e.tensor_tensor(ubcol[:, :], mqcol[:, :],
                                                    mkcol[:, :], ALU.mult),
                "sDV", 1)
        W("scalar", "sDV", vub)
        vsq_u = I("scalar", lambda e: e.activation(ubcol[:, :], ubcol[:, :],
                                                   AF.Sqrt), "sACT", 1)
        W("vector", "sACT", vsq_u)
        chain("vector", lambda e: e.tensor_scalar(ubcol[:, :], ubcol[:, :],
                                                  -0.125, None, ALU.mult))
        vng = I("vector", lambda e: e.engine_nop(), "sDV", 1)
        W("gpsimd", "sDV", P.val("sDV"))
        W("sync", "sDV", P.val("sDV"))
        W("sync", "sCH", P.val("sCH"))
        if debug >= 2 and hg == 0:
            I("gpsimd", lambda e: e.dma_start(dbg["d_qh"],
              q_h[:, :, :].rearrange("p c t -> p (c t)")), "sST", 16)
            I("gpsimd", lambda e: e.dma_start(dbg["d_kh"],
              k_h[:, :, :].rearrange("p c t -> p (c t)")), "sST", 16)
            I("gpsimd", lambda e: e.dma_start(dbg["d_vh"], v_h[:, :]), "sST", 16)
        W("sync", "sDV", vng)
        vgu = I("sync", lambda e: e.dma_start(
            ubrow[:, :], bass.AP(ubcol, 0, [[32 * 4, 2], [1, 4]])), "sDM", 16)
        vdv_b = 0
        for dcl in range(4):
            for hhx in range(2):
                srcj = hhx * 4 + dcl
                vqb = mmg(psS[0:128, 0:1],
                          [(onesr[0:1, :], ubrow[0:1, srcj:srcj + 1])],
                          [("sDM", vgu), ("sDV", vdv_b),
                           ("sACT", P.val("sACT"))])
                W("vector", "sPE", vqb)
                col = 2 * dcl + hhx
                chain("vector", lambda e, o=biasq[:, col:col + 1],
                      i=psS[:, 0:1]: e.tensor_copy(o, i))
                vdv_b = I("vector", lambda e: e.engine_nop(), "sDV", 1)
        if debug >= 2 and hg == 0:
            W("sync", "sDV", P.val("sDV"))
            I("sync", lambda e: e.dma_start(dbg["d_biasq"], biasq[:, :]),
              "sST", 16)
        regP.close()

        regA = ExitStack()
        trim = SBx(regA, f"trim{hg}", [128, NB * 256], BF16, side="right")
        ptA = SBx(regA, f"ptA{hg}", [128, 1024], BF16, side="right")
        ptB = SBx(regA, f"ptB{hg}", [128, 1024], BF16, side="right")
        W("sync", "sPE", P.val("sPE"))
        W("sync", "sDV", P.val("sDV"))
        W("sync", "sACT", P.val("sACT"))
        vtr = I("sync", lambda e: e.dma_start(trim[:, :], amask), "sMk", 16)
        W("vector", "sMk", vtr)

        def att_tiles(qh):
            res = []
            lo_, hi_ = qh * 1024, (qh + 1) * 1024
            for kt in range(NB):
                a = max(_qs_min(kt), lo_)
                if a < hi_:
                    res.append((kt, a, hi_))
            return res

        W("tensor", "sDV", vdv_b)
        for hpl in range(4):
            hp = 4 * hg + hpl
            for qh in range(2):
                tiles = att_tiles(qh)
                kt0 = tiles[0][0]
                prev = None
                exp_prev = 0
                pv_prev = 0

                def subchunks(l0):
                    ck = []
                    if l0 < 512:
                        ck.append((l0, 512))
                    ck.append((max(l0, 512), 1024))
                    return ck

                def emit_pv(pkt, pl0, vmsk, start_flag):
                    W("tensor", "sDV", vmsk)
                    vv = None
                    for (c0_, c1_) in subchunks(pl0):
                        for hhx in range(2):
                            off = pkt * 520 + (2 * hpl + hhx) * 65
                            vv = I("tensor", lambda e,
                                   l=v_h[:, off:off + 65],
                                   r=(ptA if hhx == 0 else ptB)[:, c0_:c1_],
                                   o=psw[0:65, hhx * 1024 + c0_:
                                         hhx * 1024 + c1_]:
                                   e.matmul(o, l, r, start=start_flag,
                                            stop=True), "sPE", 1)
                    return vv

                for (kt, a, b_) in tiles:
                    l0 = a - qh * 1024
                    if exp_prev:
                        W("tensor", "sACT", exp_prev)
                    vb = None
                    for (c0_, c1_) in subchunks(l0):
                        ga = qh * 1024 + c0_
                        gb_ = qh * 1024 + c1_
                        I("tensor", lambda e,
                          l=k_h[0:64, hpl, kt * 128:(kt + 1) * 128],
                          r=q_h[0:64, hpl, ga:gb_], o=psS[0:128, c0_:c1_]:
                          e.matmul(o, l, r, start=True, stop=True))
                        vb = I("tensor", lambda e,
                               l=k_h[64:128, hpl, kt * 128:(kt + 1) * 128],
                               r=q_h[64:128, hpl, ga:gb_],
                               o=psS2[0:128, c0_:c1_]:
                               e.matmul(o, l, r, start=True, stop=True),
                               "sPE", 1)
                    if prev is not None:
                        pkt, pl0, vmsk = prev
                        pv_prev = emit_pv(pkt, pl0, vmsk, pkt == kt0)
                        if debug and hg == 0 and hpl == -97 and qh == 0 and kt in (2, 8):
                            nm = {2: "d_z4", 8: "d_z8"}[kt]
                            W("vector", "sPE", pv_prev)
                            vz1 = I("vector", lambda e: e.tensor_copy(
                                mb[:, 0:64], psw[0:128, 0:64]), "sDV", 1)
                            W("sync", "sDV", vz1)
                            vz2 = I("sync", lambda e, n=nm: e.dma_start(
                                dbg[n], mb[:, 0:64]), "sST", 16)
                            W("tensor", "sST", vz2)
                        if debug and hg == 0 and hpl == -98 and qh == 0 and kt == 2:
                            W("vector", "sPE", pv_prev)
                            vz1 = I("vector", lambda e: e.tensor_copy(
                                mb[:, :], psw[0:128, 0:512]), "sDV", 1)
                            W("sync", "sDV", vz1)
                            vz2 = I("sync", lambda e: e.dma_start(
                                dbg["d_z1"][:, 0:512], mb[:, :]), "sST", 16)
                            W("gpsimd", "sDV", vz1)
                            vz3 = I("gpsimd", lambda e: e.dma_start(
                                dbg["d_p1"], ptA[:, :]), "sST", 16)
                            W("tensor", "sST", vz2)
                            W("tensor", "sST", vz3)
                    W("scalar", "sPE", vb)
                    if pv_prev:
                        W("scalar", "sPE", pv_prev)
                    I("scalar", lambda e, o=ptA[:, l0:1024],
                      i=psS[0:128, l0:1024], bc=biasq[:, 2 * hpl:2 * hpl + 1]:
                      e.activation(o, i, AF.Exp, bias=bc, scale=0.125))
                    veB = I("scalar", lambda e, o=ptB[:, l0:1024],
                            i=psS2[0:128, l0:1024],
                            bc=biasq[:, 2 * hpl + 1:2 * hpl + 2]:
                            e.activation(o, i, AF.Exp, bias=bc, scale=0.125),
                            "sACT", 1)
                    exp_prev = veB
                    if debug and hg == 0 and hpl == 1 and qh == 0 and kt == 0:
                        W("vector", "sACT", veB)
                        vd1 = I("vector", lambda e: e.tensor_copy(
                            mb[:, :], psS[0:128, 0:512]), "sDV", 1)
                        vd2 = I("vector", lambda e: e.tensor_copy(
                            rb[:, :], psS[0:128, 512:1024]), "sDV", 1)
                        W("sync", "sDV", vd2)
                        I("sync", lambda e: e.dma_start(
                            dbg["d_s0"][:, 0:512], mb[:, :]), "sST", 16)
                        I("sync", lambda e: e.dma_start(
                            dbg["d_s0"][:, 512:1024], rb[:, :]), "sST", 16)
                        W("gpsimd", "sDV", vd2)
                        vd3 = I("gpsimd", lambda e: e.dma_start(
                            dbg["d_p0"], ptA[:, :]), "sST", 16)
                        W("tensor", "sDV", vd2)
                        W("tensor", "sST", vd3)
                    qs = _qs_min(kt)
                    m0, m1 = max(a, qs), min(b_, qs + 256)
                    W("vector", "sACT", veB)
                    if m0 < m1:
                        mw = m1 - m0
                        ml = m0 - qh * 1024
                        I("vector", lambda e, o=ptA[:, ml:ml + mw],
                          i0=ptA[:, ml:ml + mw],
                          i1=trim[:, kt * 256 + m0 - qs:
                                  kt * 256 + m0 - qs + mw]:
                          e.tensor_tensor(o, i0, i1, ALU.mult))
                        chain("vector", lambda e, o=ptB[:, ml:ml + mw],
                              i0=ptB[:, ml:ml + mw],
                              i1=trim[:, kt * 256 + m0 - qs:
                                      kt * 256 + m0 - qs + mw]:
                              e.tensor_tensor(o, i0, i1, ALU.mult))
                        vmsk = I("vector", lambda e: e.engine_nop(), "sDV", 1)
                    else:
                        vmsk = I("vector", lambda e: e.engine_nop(), "sDV", 1)
                    prev = (kt, l0, vmsk)
                pkt, pl0, vmsk = prev
                vpv = emit_pv(pkt, pl0, vmsk, pkt == kt0)
                W("tensor", "sGP", sGP_z)
                for bk in range(4):
                    vpv = I("tensor", lambda e, o=psw[0:1, bk * 512:bk * 512 + 1]:
                            e.matmul(o, zcol[0:1, 0:1], zcol[0:1, 0:1],
                                     start=False, stop=True), "sPE", 1)
                if debug and hg == 0 and hpl == -95:
                    W("vector", "sPE", vpv)
                    vdbg = I("vector", lambda e: e.tensor_copy(
                        mb[:, :] if False else sqtmp[:, :].rearrange("p t -> p t"),
                        psw[0:128, 0:2048]), "sDV", 1)
                    W("gpsimd", "sDV", vdbg)
                    vdbg2 = I("gpsimd", lambda e, n=f"d_psw{qh}": e.dma_start(
                        dbg[n], sqtmp[:, :]), "sST", 16)
                    W("tensor", "sST", vdbg2)
                W("vector", "sPE", vpv)
                vfence = I("vector", lambda e: e.engine_nop(), "sDV", 1)
                W("tensor", "sDV", vfence)
                vev_last = 0
                for hh in range(2):
                    region = hh * 1024
                    W("vector", "sPE", vpv)
                    chain("vector", lambda e, o=rowX[:, 0:1024],
                          i=psw[64:65, region:region + 1024]:
                          e.reciprocal(o, i))
                    vrz = I("vector", lambda e: e.engine_nop(), "sDV", 1)
                    if debug and hg == 0 and hpl == -96 and hh == 0:
                        W("sync", "sDV", vrz)
                        vzz = I("sync", lambda e, n=f"d_rz{qh}": e.dma_start(
                            dbg[n], rowX[:, 0:1024]), "sST", 16)
                        W("vector", "sST", vzz)
                    for qc in range(2):
                        vbc = mmg(psS[0:64, 0:512],
                                  [(onesr[0:1, 0:64],
                                    rowX[0:1, qc * 512:(qc + 1) * 512])],
                                  [("sDV", vrz), ("sACT", P.val("sACT"))])
                        W("scalar", "sPE", vbc)
                        if vev_last:
                            W("scalar", "sDV", vev_last)
                        vcp = I("scalar", lambda e:
                                e.copy(rb[0:64, :], psS[0:64, 0:512]),
                                "sACT", 1)
                        W("vector", "sACT", vcp)
                        dst = attn[64 * hh:64 * (hh + 1), hp,
                                   qh * 1024 + qc * 512:
                                   qh * 1024 + (qc + 1) * 512]
                        vev_last = I("vector", lambda e, o=dst,
                                     i0=psw[0:64, region + qc * 512:
                                            region + (qc + 1) * 512],
                                     i1=rb[0:64, :]:
                                     e.tensor_tensor(o, i0, i1, ALU.mult),
                                     "sDV", 1)
                W("tensor", "sDV", P.val("sDV"))
                W("tensor", "sACT", P.val("sACT"))
        W("gpsimd", "sDV", P.val("sDV"))
        W("sync", "sDV", P.val("sDV"))
        if debug >= 2 and hg == 0:
            vda = I("gpsimd", lambda e: e.dma_start(dbg["d_attnE"],
                    attn[:, 0:4, :].rearrange("p c t -> p (c t)")), "sST", 16)
            W("tensor", "sST", vda)
        regA.close()
        regH.close()

    vdv_attn = P.val("sDV")
    if debug >= 2:
        W("gpsimd", "sDV", vdv_attn)
        I("gpsimd", lambda e: e.dma_start(dbg["d_attn"],
          attn[:, :, :].rearrange("p c t -> p (c t)")), "sST", 16)
        W("sync", "sDV", vdv_attn)
        I("sync", lambda e: e.dma_start(dbg["d_u0"], st16[:, :]), "sST", 16)

    out_acc = SBx(es, "out_acc", [128, 8, TOWN])
    ln1 = SBx(es, "ln1", [128, 8, TOWN], BF16)
    regO = ExitStack()
    wob = SBx(regO, "wob", [128, 8, 1024], BF16, side="right")

    for dc in range(8):
        I("sync", lambda e, d=out_acc[:, dc, :],
          s=xown[dc * 128:(dc + 1) * 128, :]: e.dma_start(d, s), "sMx", 16)
    sM_xown = P.val("sMx")
    W("gpsimd", "sPE", P.val("sPE"))
    W("gpsimd", "sDV", P.val("sDV"))
    W("gpsimd", "sACT", P.val("sACT"))
    vwo = None
    for dc in range(8):
        vwo = I("gpsimd", lambda e, d=wob[:, dc, :],
                s=woT[dc * 128:(dc + 1) * 128, :]: e.dma_start(d, s), "sWO", 16)

    pb_free = [0, 0, 0, 0]
    k = 0
    for dc in range(8):
        for qc in range(4):
            bank = k % 4
            k += 1
            vmm = mmg(pst4[bank][:, 0:512],
                      [(wob[:, ic, dc * 128:(dc + 1) * 128],
                        attn[:, ic, qc * 512:(qc + 1) * 512]) for ic in range(8)],
                      [("sWO", vwo), ("sDV", vdv_attn), ("sMx", sM_xown),
                       ("sDV", pb_free[bank])])
            W("vector", "sPE", vmm)
            pb_free[bank] = I("vector",
                              lambda e, o=out_acc[:, dc, qc * 512:(qc + 1) * 512],
                              i0=out_acc[:, dc, qc * 512:(qc + 1) * 512],
                              i1=pst4[bank][:, 0:512]:
                              e.tensor_tensor(o, i0, i1, ALU.add), "sDV", 1)
    vdv_r1 = P.val("sDV")
    if debug:
        W("sync", "sDV", vdv_r1)
        I("sync", lambda e: e.dma_start(dbg["d_r1"],
          out_acc[:, :, :].rearrange("p c t -> p (c t)")), "sST", 16)

    def layer_norm(src_fp32, dst_bf16, vdv_ready, apply_bf16=True,
                   store_dram=None):
        W("vector", "sDV", vdv_ready)
        vmm_prev = 0
        for dc in range(8):
            if vmm_prev:
                W("vector", "sPE", vmm_prev)
            vcp = I("vector", lambda e, o=dst_bf16[:, dc, :], i=src_fp32[:, dc, :]:
                    e.tensor_copy(o, i), "sDV", 1)
            vsq = I("vector", lambda e, o=sqtmp[:, :], i=dst_bf16[:, dc, :]:
                    e.tensor_tensor(o, i, i, ALU.mult), "sDV", 1)
            for qc in range(4):
                mmg(psw[0:1, qc * 512:(qc + 1) * 512],
                    [(onecb[:, :], dst_bf16[:, dc, qc * 512:(qc + 1) * 512])],
                    [("sDV", vcp)], first=(dc == 0))
                vmm_prev = mmg(psw[32:33, qc * 512:(qc + 1) * 512],
                               [(onecb[:, :], sqtmp[:, qc * 512:(qc + 1) * 512])],
                               [("sDV", vsq)], first=(dc == 0))
        vpe_st = P.val("sPE")
        vst_all = 0
        vc2 = 0
        vap_last = 0
        for th in range(2):
            c0 = th * 1024
            W("vector", "sPE", vpe_st)
            I("vector", lambda e, o=mrow[:, :], i=psw[0:1, c0:c0 + 1024]:
              e.tensor_scalar(o, i, 1.0 / D, None, ALU.mult))
            I("vector", lambda e, o=vrow[:, :], i=psw[32:33, c0:c0 + 1024]:
              e.tensor_scalar(o, i, 1.0 / D, None, ALU.mult))
            I("vector", lambda e: e.tensor_tensor(rrow[:, :], mrow[:, :],
                                                  mrow[:, :], ALU.mult))
            I("vector", lambda e: e.tensor_tensor(vrow[:, :], vrow[:, :],
                                                  rrow[:, :], ALU.subtract))
            I("vector", lambda e: e.tensor_scalar(vrow[:, :], vrow[:, :], EPS,
                                                  None, ALU.add))
            vvar = I("vector", lambda e: e.reciprocal(vrow[:, :], vrow[:, :]),
                     "sDV", 1)
            W("scalar", "sDV", vvar)
            vsr = I("scalar", lambda e: e.activation(rrow[:, :], vrow[:, :],
                                                     AF.Sqrt), "sACT", 1)
            for qc in range(2):
                vb1 = mmg(psS[0:128, 0:512],
                          [(onesr[0:1, :], mrow[0:1, qc * 512:(qc + 1) * 512])],
                          [("sACT", vsr), ("sDV", vc2)])
                vb2 = mmg(psS2[0:128, 0:512],
                          [(onesr[0:1, :], rrow[0:1, qc * 512:(qc + 1) * 512])],
                          [])
                W("vector", "sPE", vb2)
                if vap_last:
                    W("vector", "sDV", vap_last)
                vc1 = I("vector", lambda e: e.tensor_copy(mb[:, :], psS[:, 0:512]),
                        "sDV", 1)
                vc2 = I("vector", lambda e: e.tensor_copy(rb[:, :], psS2[:, 0:512]),
                        "sDV", 1)
                cc0 = c0 + qc * 512
                for dc in range(8):
                    if store_dram is not None and vst_all:
                        W("vector", "sST", vst_all)
                    I("vector", lambda e, o=sqtmp[:, 0:512],
                      i0=src_fp32[:, dc, cc0:cc0 + 512], i1=mb[:, :]:
                      e.tensor_tensor(o, i0, i1, ALU.subtract))
                    if store_dram is None:
                        vap_last = I("vector", lambda e,
                                     o=dst_bf16[:, dc, cc0:cc0 + 512],
                                     i0=sqtmp[:, 0:512], i1=rb[:, :]:
                                     e.tensor_tensor(o, i0, i1, ALU.mult),
                                     "sDV", 1)
                    else:
                        vap_last = I("vector", lambda e, o=ost[:, :],
                                     i0=sqtmp[:, 0:512], i1=rb[:, :]:
                                     e.tensor_tensor(o, i0, i1, ALU.mult),
                                     "sDV", 1)
                        W("sync", "sDV", vap_last)
                        vst_all = I("sync", lambda e,
                                    d=store_dram[dc * 128:(dc + 1) * 128,
                                                 cc0:cc0 + 512],
                                    s=ost[:, :]: e.dma_start(d, s), "sST", 16)
        return P.val("sDV")

    vdv_ln1 = layer_norm(out_acc, ln1, vdv_r1)
    if debug:
        W("gpsimd", "sDV", vdv_ln1)
        I("gpsimd", lambda e: e.dma_start(dbg["d_ln1"],
          ln1[:, :, :].rearrange("p c t -> p (c t)")), "sST", 16)
    regO.close()
    regAttn.close()

    scores = SBx(es, "scores", [8, TOWN])
    gall = SBx(es, "gall", [64, TOWN])
    gmask = SBx(es, "gmask", [64, TOWN], BF16)
    regG = ExitStack()
    gwb = SBx(regG, "gwb", [128, 8, 8], BF16, side="right")
    W("gpsimd", "sPE", P.val("sPE"))
    vgw = None
    for dc in range(8):
        vgw = I("gpsimd", lambda e, d=gwb[:, dc, :],
                s=gwT[dc * 128:(dc + 1) * 128, :]: e.dma_start(d, s), "sWV", 16)
    for qc in range(4):
        mmg(psw[0:8, qc * 512:(qc + 1) * 512],
            [(gwb[:, dc, :], ln1[:, dc, qc * 512:(qc + 1) * 512])
             for dc in range(8)],
            [("sWV", vgw), ("sDV", vdv_ln1)])
    vpe_gl = P.val("sPE")
    W("scalar", "sPE", vpe_gl)
    vex = I("scalar", lambda e: e.activation(scores[:, :], psw[0:8, 0:TOWN],
                                             AF.Exp), "sACT", 1)
    for qc in range(4):
        mmg(psw[32:33, qc * 512:(qc + 1) * 512],
            [(onecf[0:8, 0:1], scores[:, qc * 512:(qc + 1) * 512])],
            [("sACT", vex), ("sDV", vdv_ln1)])
    vpe_gs = P.val("sPE")
    vsc = 0
    for th in range(2):
        c0 = th * 1024
        W("vector", "sPE", vpe_gs)
        vrc = I("vector", lambda e, o=mrow[:, :], i=psw[32:33, c0:c0 + 1024]:
                e.reciprocal(o, i), "sDV", 1)
        for qc in range(2):
            mmg(psS2[0:8, qc * 512:(qc + 1) * 512],
                [(onesr[0:1, 0:8], mrow[0:1, qc * 512:(qc + 1) * 512])],
                [("sDV", vrc), ("sDV", vsc)])
        W("vector", "sPE", P.val("sPE"))
        vsc = I("vector", lambda e, o=scores[:, c0:c0 + 1024],
                i0=scores[:, c0:c0 + 1024], i1=psS2[0:8, 0:1024]:
                e.tensor_tensor(o, i0, i1, ALU.mult), "sDV", 1)
    if debug:
        W("sync", "sDV", vsc)
        I("sync", lambda e: e.dma_start(dbg["d_scores"], scores[:, :]), "sST", 16)

    W("gpsimd", "sDV", vsc)
    vgi = I("gpsimd", lambda e: e.dma_start(ccin[:, :], scores[:, :]), "sGP", 16)
    W("gpsimd", "sGP", P.val("sGP"))
    vcc = I("gpsimd", lambda e: e.collective_compute(
        "AllGather", ALU.bypass, replica_groups=[list(range(8))],
        ins=[ccin[:, :]], outs=[ccout[:, :]]), "sCC", 1)
    W("gpsimd", "sCC", vcc)
    vgo = I("gpsimd", lambda e: e.dma_start(gall[:, :], ccout[:, :]), "sGP", 16)

    chain("vector", lambda e: e.memset(st16[:, 8:9], 0.0))
    chain("vector", lambda e: e.memset(st16[:, 9:10], 2.0))
    chain("vector", lambda e: e.memset(st16[:, 10:11], 1.0))
    chain("vector", lambda e: e.memset(midc[:, :], 1.0))
    vmem_sh = chain("vector", lambda e: e.memset(rowX[:, :], 1.0))
    W("vector", "sGP", vgo)
    for it in range(NITER):
        if it > 0:
            W("vector", "sDM2", 16 * it)
        I("vector", lambda e: e.tensor_scalar(gmask[:, :], gall[:, :],
                                              midc[:, :], None, ALU.is_ge))
        vcn = I("vector", lambda e: e.reduce_sum(cn64[:, :], gmask[:, :], AX.X),
                "sDV", 1)
        W("sync", "sDV", vcn)
        vdm = I("sync", lambda e: e.dma_start(crow[:, :], cn64[:, :]), "sDM", 16)
        W("vector", "sDM", vdm)
        chain("vector", lambda e: e.tensor_reduce(st16[:, 11:12], crow[:, :],
                                                  AX.X, ALU.add))
        chain("vector", lambda e: e.tensor_scalar(st16[:, 12:13], st16[:, 11:12],
                                                  KSEL, None, ALU.is_ge))
        chain("vector", lambda e: e.tensor_tensor(st16[:, 13:14], st16[:, 10:11],
                                                  st16[:, 8:9], ALU.subtract))
        chain("vector", lambda e: e.tensor_tensor(st16[:, 13:14], st16[:, 13:14],
                                                  st16[:, 12:13], ALU.mult))
        chain("vector", lambda e: e.tensor_tensor(st16[:, 8:9], st16[:, 8:9],
                                                  st16[:, 13:14], ALU.add))
        chain("vector", lambda e: e.tensor_tensor(st16[:, 14:15], st16[:, 9:10],
                                                  st16[:, 10:11], ALU.subtract))
        chain("vector", lambda e: e.tensor_tensor(st16[:, 14:15], st16[:, 14:15],
                                                  st16[:, 12:13], ALU.mult))
        chain("vector", lambda e: e.tensor_tensor(st16[:, 9:10], st16[:, 10:11],
                                                  st16[:, 14:15], ALU.add))
        chain("vector", lambda e: e.tensor_tensor(st16[:, 10:11], st16[:, 8:9],
                                                  st16[:, 9:10], ALU.add))
        chain("vector", lambda e: e.tensor_scalar(st16[:, 10:11], st16[:, 10:11],
                                                  0.5, None, ALU.mult))
        last = (it == NITER - 1)
        srcc = st16[:, 8:9] if last else st16[:, 10:11]
        vmr = I("vector", lambda e, s_=srcc: e.tensor_scalar(
            midrow[:, :], on64r[:, :], s_, None, ALU.mult), "sDV", 1)
        if not last:
            W("sync", "sDV", vmr)
            I("sync", lambda e: e.dma_start(midc[:, :], midrow[:, :]), "sDM2", 16)
    W("sync", "sDV", P.val("sDV"))
    vlc = I("sync", lambda e: e.dma_start(locol[:, :], midrow[0:1, 0:8]),
            "sDM", 16)
    W("vector", "sDM", vlc)
    I("vector", lambda e: e.tensor_scalar(gmask[0:8, :], scores[:, :],
                                          locol[:, :], None, ALU.is_ge))
    vgt = I("vector", lambda e: e.tensor_tensor(scores[:, :], gmask[0:8, :],
                                                scores[:, :], ALU.mult),
            "sDV", 1)
    if debug:
        W("sync", "sDV", vgt)
        I("sync", lambda e: e.dma_start(dbg["d_gate"], scores[:, :]), "sST", 16)
    regG.close()

    regF = ExitStack()
    w1b = [SBx(regF, f"w1b{i}", [128, 8, 512], BF16, side="right") for i in range(2)]
    w2b = SBx(regF, "w2b", [128, 8, 512], BF16, side="right")
    hbf = SBx(regF, "hbf", [128, 8, TOWN], BF16, side="right")
    gb = SBx(regF, "gb", [128, TOWN], BF16, side="right")

    def w1_src(e_, fq, fo):
        a = fq * 1024 + fo * 512
        if e_ < 8:
            return lambda dc: w1d[e_, dc * 128:(dc + 1) * 128, a:a + 512]
        return lambda dc: ws1d[dc * 128:(dc + 1) * 128, a:a + 512]

    def w2_src(e_, fq, dh):
        if e_ < 8:
            return lambda fc: w2d[e_, fq * 1024 + fc * 128:
                                  fq * 1024 + (fc + 1) * 128,
                                  dh * 512:(dh + 1) * 512]
        return lambda fc: ws2d[fq * 1024 + fc * 128: fq * 1024 + (fc + 1) * 128,
                               dh * 512:(dh + 1) * 512]

    w1_loaded = {}
    w1_cons = {}

    def load_w1(step):
        e_, fq, fo = EORDER[step // 8], (step % 8) // 2, step % 2
        if step - 2 >= 0:
            W("gpsimd", "sPE", w1_cons.get(step - 2, 0))
        src = w1_src(e_, fq, fo)
        sname = "sW1a" if step % 2 == 0 else "sW1b"
        v = None
        for dc in range(8):
            v = I("gpsimd", lambda e, d=w1b[step % 2][:, dc, :], s=src(dc):
                  e.dma_start(d, s), sname, 16)
        w1_loaded[step] = (sname, v)

    w2_cons = [0]

    def load_w2(e_, fq, dh):
        if w2_cons[0]:
            W("gpsimd", "sPE", w2_cons[0])
        src = w2_src(e_, fq, dh)
        v = None
        for fc in range(8):
            v = I("gpsimd", lambda e, d=w2b[:, fc, :], s=src(fc):
                  e.dma_start(d, s), "sW2s", 16)
        return v

    W("gpsimd", "sPE", vpe_gl)
    load_w1(0)
    hA_free = [0, 0, 0, 0]
    yB_free = [0, 0, 0, 0]
    rowx_free = 0

    for step_e, e_ in enumerate(EORDER):
        for th in range(2):
            if e_ < 8:
                W("sync", "sDV", vgt)
                if rowx_free:
                    W("sync", "sPE", rowx_free)
                stw = I("sync", lambda e, r=e_, c0=th * 1024: e.dma_start(
                    rowX[:, :], scores[r:r + 1, c0:c0 + 1024]), "sDM", 16)
                stg_w = [("sDM", stw)]
            else:
                stg_w = [("sCH", vmem_sh)]
            for qc in range(2):
                tc = th * 2 + qc
                vbc = mmg(pswq[tc][0:128, :],
                          [(onesr[0:1, :], rowX[0:1, qc * 512:(qc + 1) * 512])],
                          stg_w + [("sDV", yB_free[tc])])
                rowx_free = vbc
                W("vector", "sPE", vbc)
                yB_free[tc] = I("vector",
                                lambda e, o=gb[:, tc * 512:(tc + 1) * 512],
                                i=pswq[tc][:, :]: e.tensor_copy(o, i), "sDV", 1)

        for fq in range(4):
            for fo in range(2):
                step = step_e * 8 + fq * 2 + fo
                bi = step % 2
                sw1n, vw1 = w1_loaded[step]
                if step + 1 < NE * 8:
                    load_w1(step + 1)
                vlast = None
                for ftl in range(4):
                    ft = fo * 4 + ftl
                    for tc in range(4):
                        bank = (ftl * 4 + tc) % 4
                        vmm = mmg(pst4[bank][:, 0:512],
                                  [(w1b[bi][:, dc, ftl * 128:(ftl + 1) * 128],
                                    ln1[:, dc, tc * 512:(tc + 1) * 512])
                                   for dc in range(8)],
                                  [(sw1n, vw1), ("sDV", vdv_ln1),
                                   ("sACT", hA_free[bank])])
                        vlast = vmm
                        W("scalar", "sPE", vmm)
                        bcol = b1sb[:, e_, fq * 8 + ft: fq * 8 + ft + 1]
                        hA_free[bank] = I("scalar", lambda e,
                                          o=hbf[:, ft, tc * 512:(tc + 1) * 512],
                                          i=pst4[bank][:, 0:512], bc=bcol:
                                          e.activation(o, i, AF.Gelu_apprx_tanh,
                                                       bias=bc), "sACT", 1)
                w1_cons[step] = vlast
            vact_h = P.val("sACT")
            for dh in range(2):
                vw2 = load_w2(e_, fq, dh)
                vlb = None
                for dtl in range(4):
                    dt = dh * 4 + dtl
                    for tc in range(4):
                        vmm = mmg(pswq[tc][0:128, :],
                                  [(w2b[:, fc, dtl * 128:(dtl + 1) * 128],
                                    hbf[:, fc, tc * 512:(tc + 1) * 512])
                                   for fc in range(8)],
                                  [("sW2s", vw2), ("sACT", vact_h),
                                   ("sDV", yB_free[tc])])
                        vlb = vmm
                        W("vector", "sPE", vmm)
                        bval = b2sb[:, e_, dt:dt + 1] if fq == 3 else 0.0
                        I("vector", lambda e, o=sqtmp[:, 0:512],
                          i0=pswq[tc][:, :], sc=bval,
                          i1=gb[:, tc * 512:(tc + 1) * 512]:
                          e.scalar_tensor_tensor(o, i0, sc, i1, ALU.add,
                                                 ALU.mult))
                        yB_free[tc] = I("vector", lambda e,
                                        o=out_acc[:, dt, tc * 512:(tc + 1) * 512],
                                        i0=out_acc[:, dt,
                                                   tc * 512:(tc + 1) * 512],
                                        i1=sqtmp[:, 0:512]:
                                        e.tensor_tensor(o, i0, i1, ALU.add),
                                        "sDV", 1)
                w2_cons[0] = vlb

    vdv_ffn = P.val("sDV")
    regF.close()

    ost = SBx(es, "ost", [128, 512])
    layer_norm(out_acc, ln1, vdv_ffn, store_dram=out)
    W("sync", "sST", P.val("sST"))

    with nc.Block() as block:
        def mk(eng_name):
            def body(e):
                for (en, fn) in P.items:
                    if en == eng_name:
                        fn(e)
            return body
        block.tensor(mk("tensor"))
        block.vector(mk("vector"))
        block.scalar(mk("scalar"))
        block.gpsimd(mk("gpsimd"))
        block.sync(mk("sync"))

    es.close()
    return nc


def _rope_tables():
    t = np.arange(T, dtype=np.float64)
    inv = 1.0 / (10000.0 ** (np.arange(0, HD, 2, dtype=np.float64) / HD))
    ang = np.outer(inv, t)
    c = np.cos(ang); s = np.sin(ang)
    cos128 = np.zeros((128, T), np.float64)
    sin128 = np.zeros((128, T), np.float64)
    for r in range(64):
        g, i = divmod(r, 32)
        if i < 16:
            d = 16 * g + i
            cos128[r] = c[d]; cos128[64 + r] = c[d]
            sin128[r] = -s[d]; sin128[64 + r] = -s[d]
        else:
            d = 16 * g + i - 16
            cos128[r] = c[d]; cos128[64 + r] = c[d]
            sin128[r] = s[d]; sin128[64 + r] = s[d]
    return cos128, sin128


def _perm_head():
    p = []
    for h in range(H):
        base = h * HD
        p += [base + i for i in range(16)]
        p += [base + 32 + i for i in range(16)]
        p += [base + 16 + i for i in range(16)]
        p += [base + 48 + i for i in range(16)]
    return np.array(p)


def _amask_for(sub):
    obl = _owned_blocks(sub)
    m = np.zeros((128, NB * 256), np.float32)
    kk = np.arange(128)[:, None]
    qq = np.arange(128)[None, :]
    tri = (kk <= qq).astype(np.float32)
    for kt in range(NB):
        qsm = _qs_min(kt)
        for colblk in range(2):
            a = qsm // 128 + colblk
            if a >= 16:
                continue
            j = obl[a]
            if j < kt:
                blk = np.zeros((128, 128), np.float32)
            elif j == kt:
                blk = tri
            else:
                blk = np.ones((128, 128), np.float32)
            m[:, kt * 256 + colblk * 128: kt * 256 + (colblk + 1) * 128] = blk
    return m


_nc_cache = {}


def kernel(L0_attn_x, L0_attn_wq, L0_attn_wk, L0_attn_wv, L0_attn_wo,
           L0_ffn_gate_weight, L0_ffn_W1, L0_ffn_b1, L0_ffn_W2, L0_ffn_b2,
           L0_ffn_Ws1, L0_ffn_bs1, L0_ffn_Ws2, L0_ffn_bs2):
    x = np.ascontiguousarray(np.asarray(L0_attn_x, dtype=np.float32))
    wq = np.asarray(L0_attn_wq, np.float32); wk = np.asarray(L0_attn_wk, np.float32)
    wv = np.asarray(L0_attn_wv, np.float32); wo = np.asarray(L0_attn_wo, np.float32)
    gw = np.asarray(L0_ffn_gate_weight, np.float32)
    w1 = np.ascontiguousarray(np.asarray(L0_ffn_W1, np.float32))
    w2 = np.ascontiguousarray(np.asarray(L0_ffn_W2, np.float32))
    b1 = np.asarray(L0_ffn_b1, np.float32); b2 = np.asarray(L0_ffn_b2, np.float32)
    ws1 = np.ascontiguousarray(np.asarray(L0_ffn_Ws1, np.float32))
    ws2 = np.ascontiguousarray(np.asarray(L0_ffn_Ws2, np.float32))
    bs1 = np.asarray(L0_ffn_bs1, np.float32); bs2 = np.asarray(L0_ffn_bs2, np.float32)

    perm = _perm_head()
    wqT = np.ascontiguousarray(wq[perm].T)
    wkT = np.ascontiguousarray(wk[perm].T)
    wvT = np.ascontiguousarray(wv.T)
    woT = np.ascontiguousarray(wo.T)
    gwT = np.ascontiguousarray(gw.T)
    b1a = np.zeros((128, NE, F // 128), np.float32)
    b2a = np.zeros((128, NE, D // 128), np.float32)
    b1a[:, :8, :] = b1.reshape(E, F // 128, 128).transpose(2, 0, 1)
    b1a[:, 8, :] = bs1.reshape(F // 128, 128).T
    b2a[:, :8, :] = b2.reshape(E, D // 128, 128).transpose(2, 0, 1)
    b2a[:, 8, :] = bs2.reshape(D // 128, 128).T

    cos128, sin128 = _rope_tables()
    cosKb = cos128.astype(BF16NP)
    sinKb = sin128.astype(BF16NP)
    onc_b = np.ones((128, 1), BF16NP)
    on33 = np.zeros((128, 33), np.float32)
    on33[0:64, 0] = 1.0
    on33[64:128, 32] = 1.0
    on33 = on33.astype(BF16NP)
    onc_f = np.ones((128, 1), np.float32)
    onr_f = np.ones((1, 128), np.float32)
    on64 = np.ones((1, 64), np.float32)

    in_maps = []
    for c in range(8):
        b, sub = divmod(c, 2)
        obl = _owned_blocks(sub)
        own_tok = np.concatenate([np.arange(j * 128, (j + 1) * 128) for j in obl])
        xtb = np.ascontiguousarray(x[b].T)
        xownb = np.ascontiguousarray(xtb[:, own_tok])
        cosQ = np.ascontiguousarray(cos128[:, own_tok]).astype(BF16NP)
        sinQ = np.ascontiguousarray(sin128[:, own_tok]).astype(BF16NP)
        am = _amask_for(sub).astype(BF16NP)
        in_maps.append({
            "xt": xtb, "xown": xownb, "wqT": wqT, "wkT": wkT, "wvT": wvT,
            "woT": woT, "gwT": gwT, "w1": w1, "w2": w2, "ws1": ws1, "ws2": ws2,
            "b1a": b1a, "b2a": b2a, "cosK": cosKb, "sinK": sinKb,
            "cosQ": cosQ, "sinQ": sinQ, "amask": am, "onc_b": onc_b, "on33": on33,
            "onc_f": onc_f, "onr_f": onr_f, "on64": on64,
        })

    key = ("nc", DEBUG)
    if key not in _nc_cache:
        _nc_cache[key] = build_nc(debug=DEBUG)
    nc = _nc_cache[key]
    res = run_bass_kernel_spmd(nc, in_maps, core_ids=list(range(8)))

    outp = np.zeros((B, T, D), np.float32)
    for c in range(8):
        b, sub = divmod(c, 2)
        obl = _owned_blocks(sub)
        own_tok = np.concatenate([np.arange(j * 128, (j + 1) * 128) for j in obl])
        outp[b, own_tok, :] = res.results[c]["out"].T
    if DEBUG:
        kernel._dbg = res.results
    return outp


# version 07
# speedup vs baseline: 1.0171x; 1.0095x over previous
import numpy as np
import ml_dtypes
import concourse.bass as bass
import concourse.mybir as mybir
from concourse.bass_utils import run_bass_kernel_spmd

F32 = mybir.dt.float32
BF16 = mybir.dt.bfloat16
AF = mybir.ActivationFunctionType
ALU = mybir.AluOpType
AX = mybir.AxisListType
BF16NP = ml_dtypes.bfloat16

B, T, D = 4, 4096, 1024
H, HD = 16, 64
NB = T // 128
TOWN = 2048
E, F = 8, 4096
NE = 9
NITER = 42
KSEL = float(2 * B * T)
EPS = 1e-5
SHUF = [(i + 16) % 32 for i in range(32)]
EORDER = [8, 0, 1, 2, 3, 4, 5, 6, 7]

DEBUG = 0


def _owned_blocks(sub):
    return [j for j in range(NB) if j % 4 in ((0, 3) if sub == 0 else (1, 2))]


def _qs_min(kt):
    c0 = len([j for j in _owned_blocks(0) if j < kt])
    c1 = len([j for j in _owned_blocks(1) if j < kt])
    return 128 * min(c0, c1)


class Plan:
    def __init__(self):
        self.items = []
        self.cnt = {}

    def on(self, eng, fn):
        self.items.append((eng, fn))

    def inc(self, sem, by=1):
        self.cnt[sem] = self.cnt.get(sem, 0) + by
        return self.cnt[sem]

    def val(self, sem):
        return self.cnt.get(sem, 0)


def build_nc(debug=0):
    from contextlib import ExitStack
    nc = bass.Bass("TRN2", target_bir_lowering=False)
    es = ExitStack()

    def SBx(stack, n, s, d=F32, side=None):
        return stack.enter_context(nc.sbuf_tensor(n, list(s), d, side=side))

    def din(name, shape, dt=F32):
        return nc.dram_tensor(name, list(shape), dt, kind="ExternalInput").ap()

    xt = din("xt", [D, T])
    xown = din("xown", [D, TOWN])
    wqT = din("wqT", [D, D]); wkT = din("wkT", [D, D])
    wvT = din("wvT", [D, D]); woT = din("woT", [D, D])
    gwT = din("gwT", [D, E])
    w1d = din("w1", [E, D, F]); w2d = din("w2", [E, F, D])
    ws1d = din("ws1", [D, F]); ws2d = din("ws2", [F, D])
    b1a = din("b1a", [128, NE, F // 128])
    b2a = din("b2a", [128, NE, D // 128])
    cosK = din("cosK", [128, T], BF16)
    sinK = din("sinK", [128, T], BF16)
    cosQ = din("cosQ", [128, TOWN], BF16)
    sinQ = din("sinQ", [128, TOWN], BF16)
    amask = din("amask", [128, NB * 256], BF16)
    onc_b = din("onc_b", [128, 1], BF16)
    on33 = din("on33", [128, 33], BF16)
    onc_f = din("onc_f", [128, 1])
    onr_f = din("onr_f", [1, 128])
    on64 = din("on64", [1, 64])

    out = nc.dram_tensor("out", [D, TOWN], F32, kind="ExternalOutput").ap()
    dbg = {}
    if debug:
        for nm, sh in [("d_psw0", [128, 2048]), ("d_psw1", [128, 2048]),
                       ("d_s0", [128, 1024]),
                       ("d_p0", [128, 1024]), ("d_z1", [128, 1024]),
                       ("d_p1", [128, 1024]), ("d_z4", [128, 64]),
                       ("d_z8", [128, 64]), ("d_z16", [128, 64]),
                       ("d_rz0", [1, 1024]), ("d_rz1", [1, 1024]),
                       ("d_attnE", [128, 4 * TOWN]),
                       ("d_qh", [128, 4 * TOWN]), ("d_kh", [128, 4 * T]),
                       ("d_vh", [128, NB * 520]), ("d_biasq", [128, 8]),
                       ("d_attn", [128, 8 * TOWN]), ("d_r1", [128, 8 * TOWN]),
                       ("d_ln1", [128, 8 * TOWN]), ("d_scores", [8, TOWN]),
                       ("d_gate", [8, TOWN]), ("d_u0", [1, 16])]:
            dbg[nm] = nc.dram_tensor(nm, sh, F32, kind="ExternalOutput").ap()

    ccin = nc.dram_tensor("ccin", [E, TOWN], F32)
    ccout = nc.dram_tensor("ccout", [64, TOWN], F32)

    mrow = SBx(es, "mrow", [1, 1024])
    vrow = SBx(es, "vrow", [1, 1024])
    rrow = SBx(es, "rrow", [1, 1024])
    rowX = SBx(es, "rowX", [1, 1024])
    onesr = SBx(es, "onesr", [1, 128])
    on64r = SBx(es, "on64r", [1, 64])
    onecb = SBx(es, "onecb", [128, 1], BF16)
    zcol = SBx(es, "zcol", [128, 1], BF16)
    on33b = SBx(es, "on33b", [128, 33], BF16)
    mqcol = SBx(es, "mqcol", [128, 4])
    mkcol = SBx(es, "mkcol", [128, 4])
    ubcol = SBx(es, "ubcol", [128, 4])
    ubrow = SBx(es, "ubrow", [1, 8])
    onecf = SBx(es, "onecf", [128, 1])
    biasq = SBx(es, "biasq", [128, 8])
    b1sb = SBx(es, "b1sb", [128, NE, F // 128])
    b2sb = SBx(es, "b2sb", [128, NE, D // 128])
    st16 = SBx(es, "st16", [1, 16])
    cn64 = SBx(es, "cn64", [64, 1])
    crow = SBx(es, "crow", [1, 64])
    midc = SBx(es, "midc", [64, 1])
    midrow = SBx(es, "midrow", [1, 64])
    locol = SBx(es, "locol", [8, 1])
    sqtmp = SBx(es, "sqtmp", [128, TOWN], BF16)
    mb = SBx(es, "mb", [128, 512])
    rb = SBx(es, "rb", [128, 512])
    regAttn = ExitStack()
    attn = SBx(regAttn, "attn", [128, 8, TOWN], BF16, side="right")

    psS = es.enter_context(nc.psum_tensor("psS", [128, 1024], F32))
    psS2 = es.enter_context(nc.psum_tensor("psS2", [128, 1024], F32))
    psw = es.enter_context(nc.psum_tensor("psw", [128, 2048], F32))
    pst4 = [psS[:, 0:512], psS[:, 512:1024], psS2[:, 0:512], psS2[:, 512:1024]]
    pswq = [psw[:, i * 512:(i + 1) * 512] for i in range(4)]

    sems = {}
    for n in ["sX", "sW", "sM", "sPE", "sACT", "sDV", "sGP", "sST",
              "sCH", "sDM", "sDM2", "sCC", "sW0", "sW1", "sW2", "sW3",
              "sWV", "sWO", "sMt", "sMk", "sMx", "sW1a", "sW1b", "sW2s"]:
        sems[n] = es.enter_context(nc.semaphore(n))

    P = Plan()

    def W(eng, sem, val):
        if val:
            P.on(eng, lambda e, s=sems[sem], v=val: e.wait_ge(s, v))

    def I(eng, fn, sem=None, by=1):
        if sem is None:
            P.on(eng, fn)
            return None
        v = P.inc(sem, by)
        P.on(eng, lambda e, f=fn, s=sems[sem], b=by: f(e).then_inc(s, b))
        return v

    def chain(eng, fn, sem="sCH"):
        v = P.inc(sem)
        def go(e, f=fn, s=sems[sem], vv=v):
            f(e).then_inc(s, 1)
            e.wait_ge(s, vv)
        P.on(eng, go)
        return v

    def mmg(out_ap, pairs, waits=(), first=True):
        for (sname, v) in waits:
            W("tensor", sname, v)
        n = len(pairs)
        for i in range(n - 1):
            l, r = pairs[i]
            I("tensor", lambda e, l_=l, r_=r, o=out_ap, s1=(i == 0 and first):
              e.matmul(o, l_, r_, start=s1, stop=False))
        l, r = pairs[n - 1]
        return I("tensor", lambda e, l_=l, r_=r, o=out_ap,
                 s1=(n == 1 and first):
                 e.matmul(o, l_, r_, start=s1, stop=True), "sPE", 1)

    I("gpsimd", lambda e: e.memset(zcol[:, :], 0.0), "sGP", 1)
    sGP_z = P.val("sGP")
    for (dst, src) in [(b1sb[:, :, :], b1a), (b2sb[:, :, :], b2a),
                       (onesr[:, :], onr_f), (on64r[:, :], on64),
                       (onecb[:, :], onc_b), (onecf[:, :], onc_f),
                       (on33b[:, :], on33)]:
        I("sync", lambda e, d=dst, s=src: e.dma_start(d, s), "sM", 16)
    sM_const = P.val("sM")

    vdv_attn_parts = []
    for hg in range(2):
        regH = ExitStack()
        q_h = SBx(regH, f"q_h{hg}", [128, 4, TOWN], BF16, side="right")
        k_h = SBx(regH, f"k_h{hg}", [128, 4, T], BF16, side="right")
        v_h = SBx(regH, f"v_h{hg}", [128, NB * 520], BF16, side="right")

        regP = ExitStack()
        cosKt = SBx(regP, f"cosKt{hg}", [128, T], BF16, side="right")
        sinKt = SBx(regP, f"sinKt{hg}", [128, T], BF16, side="right")
        cosQt = SBx(regP, f"cosQt{hg}", [128, TOWN], BF16, side="right")
        sinQt = SBx(regP, f"sinQt{hg}", [128, TOWN], BF16, side="right")
        xbf = SBx(regP, f"xbf{hg}", [128, 8, 1024], BF16, side="right")
        xqf = SBx(regP, f"xqf{hg}", [128, 8, 512], BF16, side="right")
        wsl = [SBx(regP, f"wsl{hg}_{i}", [128, 8, 128], BF16, side="right") for i in range(4)]
        wvb = SBx(regP, f"wvb{hg}", [128, 8, 512], BF16, side="right")
        rt2 = SBx(regP, f"rt2_{hg}", [128, 512], BF16, side="right")
        rt3 = SBx(regP, f"rt3_{hg}", [128, 512], BF16, side="right")
        rt4 = SBx(regP, f"rt4_{hg}", [128, 512], BF16, side="right")

        W("sync", "sPE", P.val("sPE"))
        W("sync", "sDV", P.val("sDV"))
        W("sync", "sACT", P.val("sACT"))
        for (dst, src) in [(cosKt[:, :], cosK), (sinKt[:, :], sinK),
                           (cosQt[:, :], cosQ), (sinQt[:, :], sinQ)]:
            I("sync", lambda e, d=dst, s=src: e.dma_start(d, s), "sMt", 16)
        W("vector", "sMt", P.val("sMt"))
        if hg > 0:
            W("gpsimd", "sPE", P.val("sPE"))
            W("gpsimd", "sDV", P.val("sDV"))
            W("gpsimd", "sACT", P.val("sACT"))
        vap = bass.AP(v_h, 64, [[520 * NB, 128], [520, NB], [65, 8], [1, 1]])
        vgp_ones = I("gpsimd", lambda e, a=vap: e.memset(a, 1.0), "sGP", 1)

        ps_free = [0, 0, 0, 0]
        ps_rot = [0]

        def next_bank():
            i = ps_rot[0] % 4
            ps_rot[0] += 1
            return i

        wsl_cons = [0, 0, 0, 0]
        wsl_rot = [0]

        def load_wslice(src_dram, gc):
            bi = wsl_rot[0] % 4
            wsl_rot[0] += 1
            if wsl_cons[bi]:
                W("gpsimd", "sPE", wsl_cons[bi])
            sname = f"sW{bi}"
            v = None
            for dc in range(8):
                v = I("gpsimd", lambda e, d=wsl[bi][:, dc, :],
                      s=src_dram[dc * 128:(dc + 1) * 128, gc * 128:(gc + 1) * 128]:
                      e.dma_start(d, s), sname, 16)
            return bi, v, sname

        def rope_evac(bank, dstap, cosap, sinap):
            pst = pst4[bank][:, 0:512]
            I("vector", lambda e, o=rt4[:, :], i=pst: e.tensor_copy(o, i))
            I("vector", lambda e, o=rt2[:, :], i=rt4[:, :], m=SHUF:
              e.stream_shuffle(o, i, m))
            I("vector", lambda e, o=rt3[:, :], i0=pst, i1=cosap:
              e.tensor_tensor(o, i0, i1, ALU.mult))
            I("vector", lambda e, o=rt2[:, :], i0=rt2[:, :], i1=sinap:
              e.tensor_tensor(o, i0, i1, ALU.mult))
            return I("vector", lambda e, o=dstap, i0=rt3[:, :], i1=rt2[:, :]:
                     e.tensor_tensor(o, i0, i1, ALU.add), "sDV", 1)

        first_vdma = [True]
        for sec in range(4):
            if sec > 0 or hg > 0:
                W("gpsimd", "sPE", P.val("sPE"))
            vx = None
            for dc in range(8):
                vx = I("gpsimd", lambda e, d=xbf[:, dc, :],
                       s=xt[dc * 128:(dc + 1) * 128, sec * 1024:(sec + 1) * 1024]:
                       e.dma_start(d, s), "sX", 16)
            for dc in range(8):
                vx = I("gpsimd", lambda e, d=xqf[:, dc, :],
                       s=xown[dc * 128:(dc + 1) * 128, sec * 512:(sec + 1) * 512]:
                       e.dma_start(d, s), "sX", 16)
            sXv = P.val("sX")

            for cl in range(4):
                gc = 4 * hg + cl
                bi, vw, swn = load_wslice(wkT, gc)
                for half in range(2):
                    bank = next_bank()
                    vmm = mmg(pst4[bank][:, 0:512],
                              [(wsl[bi][:, dc, :],
                                xbf[:, dc, half * 512:(half + 1) * 512])
                               for dc in range(8)],
                              [(swn, vw), ("sX", sXv), ("sDV", ps_free[bank])])
                    wsl_cons[bi] = vmm
                    c0 = sec * 1024 + half * 512
                    W("vector", "sPE", vmm)
                    ps_free[bank] = rope_evac(bank, k_h[:, cl, c0:c0 + 512],
                                              cosKt[:, c0:c0 + 512],
                                              sinKt[:, c0:c0 + 512])
            for cl in range(4):
                gc = 4 * hg + cl
                bi, vw, swn = load_wslice(wqT, gc)
                bank = next_bank()
                vmm = mmg(pst4[bank][:, 0:512],
                          [(wsl[bi][:, dc, :], xqf[:, dc, :]) for dc in range(8)],
                          [(swn, vw), ("sX", sXv), ("sDV", ps_free[bank])])
                wsl_cons[bi] = vmm
                c0 = sec * 512
                W("vector", "sPE", vmm)
                ps_free[bank] = rope_evac(bank, q_h[:, cl, c0:c0 + 512],
                                          cosQt[:, c0:c0 + 512],
                                          sinQt[:, c0:c0 + 512])

            if sec == 0:
                W("gpsimd", "sPE", P.val("sPE")) if hg > 0 else None
                vwv = None
                for dc in range(8):
                    vwv = I("gpsimd", lambda e, d=wvb[:, dc, :],
                            s=wvT[dc * 128:(dc + 1) * 128,
                                  hg * 512:(hg + 1) * 512]:
                            e.dma_start(d, s), "sWV", 16)
                vwv_val = vwv
            for tt in range(8):
                bank = next_bank()
                vmm = mmg(pst4[bank][:, 0:512],
                          [(xbf[:, dc, tt * 128:(tt + 1) * 128], wvb[:, dc, :])
                           for dc in range(8)],
                          [("sWV", vwv_val), ("sX", sXv), ("sDV", ps_free[bank])])
                kt = sec * 8 + tt
                dstap = bass.AP(v_h, kt * 520, [[520 * NB, 128], [65, 8], [1, 64]])
                W("vector", "sPE", vmm)
                if first_vdma[0]:
                    W("vector", "sGP", vgp_ones)
                    first_vdma[0] = False
                srcap = pst4[bank][:, 0:512].rearrange("p (h x) -> p h x", h=8)
                ps_free[bank] = I("vector", lambda e, o=dstap, i=srcap:
                                  e.tensor_copy(o, i), "sDV", 1)

        vdv_red = 0
        for (srct, iscol, nch, ncols) in [(q_h, mqcol, 4, TOWN),
                                          (k_h, mkcol, 4, T)]:
            for dc in range(nch):
                nhalf = ncols // TOWN
                vmmu = 0
                for ch in range(nhalf):
                    if vdv_red:
                        W("vector", "sPE", vmmu) if False else None
                    vs = I("vector", lambda e, o=sqtmp[:, :],
                           i=srct[:, dc, ch * TOWN:(ch + 1) * TOWN]:
                           e.tensor_tensor(o, i, i, ALU.mult), "sDV", 1)
                    for qc in range(4):
                        vmmu = mmg(psw[0:33, qc * 512:(qc + 1) * 512],
                                   [(on33b[:, :],
                                     sqtmp[:, qc * 512:(qc + 1) * 512])],
                                   [("sDV", vs), ("sDV", vdv_red)],
                                   first=(ch == 0))
                W("vector", "sPE", vmmu)
                vdv_red = I("vector", lambda e, o=iscol[0:33, dc:dc + 1],
                            i=psw[0:33, 0:TOWN]:
                            e.reduce_max(o, i, AX.X), "sDV", 1)
        W("vector", "sDV", vdv_red)
        vub = I("vector", lambda e: e.tensor_tensor(ubcol[:, :], mqcol[:, :],
                                                    mkcol[:, :], ALU.mult),
                "sDV", 1)
        W("scalar", "sDV", vub)
        vsq_u = I("scalar", lambda e: e.activation(ubcol[:, :], ubcol[:, :],
                                                   AF.Sqrt), "sACT", 1)
        W("vector", "sACT", vsq_u)
        chain("vector", lambda e: e.tensor_scalar(ubcol[:, :], ubcol[:, :],
                                                  -0.125, None, ALU.mult))
        vng = I("vector", lambda e: e.engine_nop(), "sDV", 1)
        W("gpsimd", "sDV", P.val("sDV"))
        W("sync", "sDV", P.val("sDV"))
        W("sync", "sCH", P.val("sCH"))
        if debug >= 2 and hg == 0:
            I("gpsimd", lambda e: e.dma_start(dbg["d_qh"],
              q_h[:, :, :].rearrange("p c t -> p (c t)")), "sST", 16)
            I("gpsimd", lambda e: e.dma_start(dbg["d_kh"],
              k_h[:, :, :].rearrange("p c t -> p (c t)")), "sST", 16)
            I("gpsimd", lambda e: e.dma_start(dbg["d_vh"], v_h[:, :]), "sST", 16)
        W("sync", "sDV", vng)
        vgu = I("sync", lambda e: e.dma_start(
            ubrow[:, :], bass.AP(ubcol, 0, [[32 * 4, 2], [1, 4]])), "sDM", 16)
        vdv_b = 0
        for dcl in range(4):
            for hhx in range(2):
                srcj = hhx * 4 + dcl
                vqb = mmg(psS[0:128, 0:1],
                          [(onesr[0:1, :], ubrow[0:1, srcj:srcj + 1])],
                          [("sDM", vgu), ("sDV", vdv_b),
                           ("sACT", P.val("sACT"))])
                W("vector", "sPE", vqb)
                col = 2 * dcl + hhx
                chain("vector", lambda e, o=biasq[:, col:col + 1],
                      i=psS[:, 0:1]: e.tensor_copy(o, i))
                vdv_b = I("vector", lambda e: e.engine_nop(), "sDV", 1)
        if debug >= 2 and hg == 0:
            W("sync", "sDV", P.val("sDV"))
            I("sync", lambda e: e.dma_start(dbg["d_biasq"], biasq[:, :]),
              "sST", 16)
        regP.close()

        regA = ExitStack()
        trim = SBx(regA, f"trim{hg}", [128, NB * 256], BF16, side="right")
        ptA = SBx(regA, f"ptA{hg}", [128, 1024], BF16, side="right")
        ptB = SBx(regA, f"ptB{hg}", [128, 1024], BF16, side="right")
        W("sync", "sPE", P.val("sPE"))
        W("sync", "sDV", P.val("sDV"))
        W("sync", "sACT", P.val("sACT"))
        vtr = I("sync", lambda e: e.dma_start(trim[:, :], amask), "sMk", 16)
        W("vector", "sMk", vtr)

        def att_tiles(qh):
            res = []
            lo_, hi_ = qh * 1024, (qh + 1) * 1024
            for kt in range(NB):
                a = max(_qs_min(kt), lo_)
                if a < hi_:
                    res.append((kt, a, hi_))
            return res

        W("tensor", "sDV", vdv_b)
        for hpl in range(4):
            hp = 4 * hg + hpl
            for qh in range(2):
                tiles = att_tiles(qh)
                kt0 = tiles[0][0]
                prev = None
                exp_prev = 0
                pv_prev = 0

                def subchunks(l0):
                    ck = []
                    if l0 < 512:
                        ck.append((l0, 512))
                    ck.append((max(l0, 512), 1024))
                    return ck

                def emit_pv(pkt, pl0, vmsk, start_flag):
                    W("tensor", "sDV", vmsk)
                    vv = None
                    for (c0_, c1_) in subchunks(pl0):
                        for hhx in range(2):
                            off = pkt * 520 + (2 * hpl + hhx) * 65
                            vv = I("tensor", lambda e,
                                   l=v_h[:, off:off + 65],
                                   r=(ptA if hhx == 0 else ptB)[:, c0_:c1_],
                                   o=psw[0:65, hhx * 1024 + c0_:
                                         hhx * 1024 + c1_]:
                                   e.matmul(o, l, r, start=start_flag,
                                            stop=True), "sPE", 1)
                    return vv

                for (kt, a, b_) in tiles:
                    l0 = a - qh * 1024
                    if exp_prev:
                        W("tensor", "sACT", exp_prev)
                    vb = None
                    for (c0_, c1_) in subchunks(l0):
                        ga = qh * 1024 + c0_
                        gb_ = qh * 1024 + c1_
                        I("tensor", lambda e,
                          l=k_h[0:64, hpl, kt * 128:(kt + 1) * 128],
                          r=q_h[0:64, hpl, ga:gb_], o=psS[0:128, c0_:c1_]:
                          e.matmul(o, l, r, start=True, stop=True))
                        vb = I("tensor", lambda e,
                               l=k_h[64:128, hpl, kt * 128:(kt + 1) * 128],
                               r=q_h[64:128, hpl, ga:gb_],
                               o=psS2[0:128, c0_:c1_]:
                               e.matmul(o, l, r, start=True, stop=True),
                               "sPE", 1)
                    if prev is not None:
                        pkt, pl0, vmsk = prev
                        pv_prev = emit_pv(pkt, pl0, vmsk, pkt == kt0)
                        if debug and hg == 0 and hpl == -97 and qh == 0 and kt in (2, 8):
                            nm = {2: "d_z4", 8: "d_z8"}[kt]
                            W("vector", "sPE", pv_prev)
                            vz1 = I("vector", lambda e: e.tensor_copy(
                                mb[:, 0:64], psw[0:128, 0:64]), "sDV", 1)
                            W("sync", "sDV", vz1)
                            vz2 = I("sync", lambda e, n=nm: e.dma_start(
                                dbg[n], mb[:, 0:64]), "sST", 16)
                            W("tensor", "sST", vz2)
                        if debug and hg == 0 and hpl == -98 and qh == 0 and kt == 2:
                            W("vector", "sPE", pv_prev)
                            vz1 = I("vector", lambda e: e.tensor_copy(
                                mb[:, :], psw[0:128, 0:512]), "sDV", 1)
                            W("sync", "sDV", vz1)
                            vz2 = I("sync", lambda e: e.dma_start(
                                dbg["d_z1"][:, 0:512], mb[:, :]), "sST", 16)
                            W("gpsimd", "sDV", vz1)
                            vz3 = I("gpsimd", lambda e: e.dma_start(
                                dbg["d_p1"], ptA[:, :]), "sST", 16)
                            W("tensor", "sST", vz2)
                            W("tensor", "sST", vz3)
                    W("scalar", "sPE", vb)
                    if pv_prev:
                        W("scalar", "sPE", pv_prev)
                    I("scalar", lambda e, o=ptA[:, l0:1024],
                      i=psS[0:128, l0:1024], bc=biasq[:, 2 * hpl:2 * hpl + 1]:
                      e.activation(o, i, AF.Exp, bias=bc, scale=0.125))
                    veB = I("scalar", lambda e, o=ptB[:, l0:1024],
                            i=psS2[0:128, l0:1024],
                            bc=biasq[:, 2 * hpl + 1:2 * hpl + 2]:
                            e.activation(o, i, AF.Exp, bias=bc, scale=0.125),
                            "sACT", 1)
                    exp_prev = veB
                    if debug and hg == 0 and hpl == 1 and qh == 0 and kt == 0:
                        W("vector", "sACT", veB)
                        vd1 = I("vector", lambda e: e.tensor_copy(
                            mb[:, :], psS[0:128, 0:512]), "sDV", 1)
                        vd2 = I("vector", lambda e: e.tensor_copy(
                            rb[:, :], psS[0:128, 512:1024]), "sDV", 1)
                        W("sync", "sDV", vd2)
                        I("sync", lambda e: e.dma_start(
                            dbg["d_s0"][:, 0:512], mb[:, :]), "sST", 16)
                        I("sync", lambda e: e.dma_start(
                            dbg["d_s0"][:, 512:1024], rb[:, :]), "sST", 16)
                        W("gpsimd", "sDV", vd2)
                        vd3 = I("gpsimd", lambda e: e.dma_start(
                            dbg["d_p0"], ptA[:, :]), "sST", 16)
                        W("tensor", "sDV", vd2)
                        W("tensor", "sST", vd3)
                    qs = _qs_min(kt)
                    m0, m1 = max(a, qs), min(b_, qs + 256)
                    W("vector", "sACT", veB)
                    if m0 < m1:
                        mw = m1 - m0
                        ml = m0 - qh * 1024
                        I("vector", lambda e, o=ptA[:, ml:ml + mw],
                          i0=ptA[:, ml:ml + mw],
                          i1=trim[:, kt * 256 + m0 - qs:
                                  kt * 256 + m0 - qs + mw]:
                          e.tensor_tensor(o, i0, i1, ALU.mult))
                        chain("vector", lambda e, o=ptB[:, ml:ml + mw],
                              i0=ptB[:, ml:ml + mw],
                              i1=trim[:, kt * 256 + m0 - qs:
                                      kt * 256 + m0 - qs + mw]:
                              e.tensor_tensor(o, i0, i1, ALU.mult))
                        vmsk = I("vector", lambda e: e.engine_nop(), "sDV", 1)
                    else:
                        vmsk = I("vector", lambda e: e.engine_nop(), "sDV", 1)
                    prev = (kt, l0, vmsk)
                pkt, pl0, vmsk = prev
                vpv = emit_pv(pkt, pl0, vmsk, pkt == kt0)
                W("tensor", "sGP", sGP_z)
                for bk in range(4):
                    vpv = I("tensor", lambda e, o=psw[0:1, bk * 512:bk * 512 + 1]:
                            e.matmul(o, zcol[0:1, 0:1], zcol[0:1, 0:1],
                                     start=False, stop=True), "sPE", 1)
                if debug and hg == 0 and hpl == -95:
                    W("vector", "sPE", vpv)
                    vdbg = I("vector", lambda e: e.tensor_copy(
                        mb[:, :] if False else sqtmp[:, :].rearrange("p t -> p t"),
                        psw[0:128, 0:2048]), "sDV", 1)
                    W("gpsimd", "sDV", vdbg)
                    vdbg2 = I("gpsimd", lambda e, n=f"d_psw{qh}": e.dma_start(
                        dbg[n], sqtmp[:, :]), "sST", 16)
                    W("tensor", "sST", vdbg2)
                W("vector", "sPE", vpv)
                vfence = I("vector", lambda e: e.engine_nop(), "sDV", 1)
                W("tensor", "sDV", vfence)
                vev_last = 0
                for hh in range(2):
                    region = hh * 1024
                    W("vector", "sPE", vpv)
                    chain("vector", lambda e, o=rowX[:, 0:1024],
                          i=psw[64:65, region:region + 1024]:
                          e.reciprocal(o, i))
                    vrz = I("vector", lambda e: e.engine_nop(), "sDV", 1)
                    if debug and hg == 0 and hpl == -96 and hh == 0:
                        W("sync", "sDV", vrz)
                        vzz = I("sync", lambda e, n=f"d_rz{qh}": e.dma_start(
                            dbg[n], rowX[:, 0:1024]), "sST", 16)
                        W("vector", "sST", vzz)
                    for qc in range(2):
                        vbc = mmg(psS[0:64, 0:512],
                                  [(onesr[0:1, 0:64],
                                    rowX[0:1, qc * 512:(qc + 1) * 512])],
                                  [("sDV", vrz), ("sACT", P.val("sACT"))])
                        W("scalar", "sPE", vbc)
                        if vev_last:
                            W("scalar", "sDV", vev_last)
                        vcp = I("scalar", lambda e:
                                e.copy(rb[0:64, :], psS[0:64, 0:512]),
                                "sACT", 1)
                        W("vector", "sACT", vcp)
                        dst = attn[64 * hh:64 * (hh + 1), hp,
                                   qh * 1024 + qc * 512:
                                   qh * 1024 + (qc + 1) * 512]
                        vev_last = I("vector", lambda e, o=dst,
                                     i0=psw[0:64, region + qc * 512:
                                            region + (qc + 1) * 512],
                                     i1=rb[0:64, :]:
                                     e.tensor_tensor(o, i0, i1, ALU.mult),
                                     "sDV", 1)
                W("tensor", "sDV", P.val("sDV"))
                W("tensor", "sACT", P.val("sACT"))
        W("gpsimd", "sDV", P.val("sDV"))
        W("sync", "sDV", P.val("sDV"))
        if debug >= 2 and hg == 0:
            vda = I("gpsimd", lambda e: e.dma_start(dbg["d_attnE"],
                    attn[:, 0:4, :].rearrange("p c t -> p (c t)")), "sST", 16)
            W("tensor", "sST", vda)
        regA.close()
        regH.close()

    vdv_attn = P.val("sDV")
    if debug >= 2:
        W("gpsimd", "sDV", vdv_attn)
        I("gpsimd", lambda e: e.dma_start(dbg["d_attn"],
          attn[:, :, :].rearrange("p c t -> p (c t)")), "sST", 16)
        W("sync", "sDV", vdv_attn)
        I("sync", lambda e: e.dma_start(dbg["d_u0"], st16[:, :]), "sST", 16)

    out_acc = SBx(es, "out_acc", [128, 8, TOWN])
    ln1 = SBx(es, "ln1", [128, 8, TOWN], BF16)
    regO = ExitStack()
    wob = SBx(regO, "wob", [128, 8, 1024], BF16, side="right")

    for dc in range(8):
        I("sync", lambda e, d=out_acc[:, dc, :],
          s=xown[dc * 128:(dc + 1) * 128, :]: e.dma_start(d, s), "sMx", 16)
    sM_xown = P.val("sMx")
    W("gpsimd", "sPE", P.val("sPE"))
    W("gpsimd", "sDV", P.val("sDV"))
    W("gpsimd", "sACT", P.val("sACT"))
    vwo = None
    for dc in range(8):
        vwo = I("gpsimd", lambda e, d=wob[:, dc, :],
                s=woT[dc * 128:(dc + 1) * 128, :]: e.dma_start(d, s), "sWO", 16)

    pb_free = [0, 0, 0, 0]
    k = 0
    for dc in range(8):
        for qc in range(4):
            bank = k % 4
            k += 1
            vmm = mmg(pst4[bank][:, 0:512],
                      [(wob[:, ic, dc * 128:(dc + 1) * 128],
                        attn[:, ic, qc * 512:(qc + 1) * 512]) for ic in range(8)],
                      [("sWO", vwo), ("sDV", vdv_attn), ("sMx", sM_xown),
                       ("sDV", pb_free[bank])])
            W("vector", "sPE", vmm)
            pb_free[bank] = I("vector",
                              lambda e, o=out_acc[:, dc, qc * 512:(qc + 1) * 512],
                              i0=out_acc[:, dc, qc * 512:(qc + 1) * 512],
                              i1=pst4[bank][:, 0:512]:
                              e.tensor_tensor(o, i0, i1, ALU.add), "sDV", 1)
    vdv_r1 = P.val("sDV")
    if debug:
        W("sync", "sDV", vdv_r1)
        I("sync", lambda e: e.dma_start(dbg["d_r1"],
          out_acc[:, :, :].rearrange("p c t -> p (c t)")), "sST", 16)

    def layer_norm(src_fp32, dst_bf16, vdv_ready, apply_bf16=True,
                   store_dram=None):
        W("vector", "sDV", vdv_ready)
        vmm_prev = 0
        for dc in range(8):
            if vmm_prev:
                W("vector", "sPE", vmm_prev)
            vcp = I("vector", lambda e, o=dst_bf16[:, dc, :], i=src_fp32[:, dc, :]:
                    e.tensor_copy(o, i), "sDV", 1)
            vsq = I("vector", lambda e, o=sqtmp[:, :], i=dst_bf16[:, dc, :]:
                    e.tensor_tensor(o, i, i, ALU.mult), "sDV", 1)
            for qc in range(4):
                mmg(psw[0:1, qc * 512:(qc + 1) * 512],
                    [(onecb[:, :], dst_bf16[:, dc, qc * 512:(qc + 1) * 512])],
                    [("sDV", vcp)], first=(dc == 0))
                vmm_prev = mmg(psw[32:33, qc * 512:(qc + 1) * 512],
                               [(onecb[:, :], sqtmp[:, qc * 512:(qc + 1) * 512])],
                               [("sDV", vsq)], first=(dc == 0))
        vpe_st = P.val("sPE")
        vst_all = 0
        vc2 = 0
        vap_last = 0
        for th in range(2):
            c0 = th * 1024
            W("vector", "sPE", vpe_st)
            I("vector", lambda e, o=mrow[:, :], i=psw[0:1, c0:c0 + 1024]:
              e.tensor_scalar(o, i, 1.0 / D, None, ALU.mult))
            I("vector", lambda e, o=vrow[:, :], i=psw[32:33, c0:c0 + 1024]:
              e.tensor_scalar(o, i, 1.0 / D, None, ALU.mult))
            I("vector", lambda e: e.tensor_tensor(rrow[:, :], mrow[:, :],
                                                  mrow[:, :], ALU.mult))
            I("vector", lambda e: e.tensor_tensor(vrow[:, :], vrow[:, :],
                                                  rrow[:, :], ALU.subtract))
            I("vector", lambda e: e.tensor_scalar(vrow[:, :], vrow[:, :], EPS,
                                                  None, ALU.add))
            vvar = I("vector", lambda e: e.reciprocal(vrow[:, :], vrow[:, :]),
                     "sDV", 1)
            W("scalar", "sDV", vvar)
            vsr = I("scalar", lambda e: e.activation(rrow[:, :], vrow[:, :],
                                                     AF.Sqrt), "sACT", 1)
            for qc in range(2):
                vb1 = mmg(psS[0:128, 0:512],
                          [(onesr[0:1, :], mrow[0:1, qc * 512:(qc + 1) * 512])],
                          [("sACT", vsr), ("sDV", vc2)])
                vb2 = mmg(psS2[0:128, 0:512],
                          [(onesr[0:1, :], rrow[0:1, qc * 512:(qc + 1) * 512])],
                          [])
                W("vector", "sPE", vb2)
                if vap_last:
                    W("vector", "sDV", vap_last)
                vc1 = I("vector", lambda e: e.tensor_copy(mb[:, :], psS[:, 0:512]),
                        "sDV", 1)
                vc2 = I("vector", lambda e: e.tensor_copy(rb[:, :], psS2[:, 0:512]),
                        "sDV", 1)
                cc0 = c0 + qc * 512
                for dc in range(8):
                    if store_dram is not None and vst_all:
                        W("vector", "sST", vst_all)
                    I("vector", lambda e, o=sqtmp[:, 0:512],
                      i0=src_fp32[:, dc, cc0:cc0 + 512], i1=mb[:, :]:
                      e.tensor_tensor(o, i0, i1, ALU.subtract))
                    if store_dram is None:
                        vap_last = I("vector", lambda e,
                                     o=dst_bf16[:, dc, cc0:cc0 + 512],
                                     i0=sqtmp[:, 0:512], i1=rb[:, :]:
                                     e.tensor_tensor(o, i0, i1, ALU.mult),
                                     "sDV", 1)
                    else:
                        vap_last = I("vector", lambda e, o=ost[:, :],
                                     i0=sqtmp[:, 0:512], i1=rb[:, :]:
                                     e.tensor_tensor(o, i0, i1, ALU.mult),
                                     "sDV", 1)
                        W("sync", "sDV", vap_last)
                        vst_all = I("sync", lambda e,
                                    d=store_dram[dc * 128:(dc + 1) * 128,
                                                 cc0:cc0 + 512],
                                    s=ost[:, :]: e.dma_start(d, s), "sST", 16)
        return P.val("sDV")

    vdv_ln1 = layer_norm(out_acc, ln1, vdv_r1)
    if debug:
        W("gpsimd", "sDV", vdv_ln1)
        I("gpsimd", lambda e: e.dma_start(dbg["d_ln1"],
          ln1[:, :, :].rearrange("p c t -> p (c t)")), "sST", 16)
    regO.close()
    regAttn.close()

    scores = SBx(es, "scores", [8, TOWN])
    gall = SBx(es, "gall", [64, TOWN])
    gmask = SBx(es, "gmask", [64, TOWN], BF16)
    regG = ExitStack()
    gwb = SBx(regG, "gwb", [128, 8, 8], BF16, side="right")
    W("gpsimd", "sPE", P.val("sPE"))
    vgw = None
    for dc in range(8):
        vgw = I("gpsimd", lambda e, d=gwb[:, dc, :],
                s=gwT[dc * 128:(dc + 1) * 128, :]: e.dma_start(d, s), "sWV", 16)
    for qc in range(4):
        mmg(psw[0:8, qc * 512:(qc + 1) * 512],
            [(gwb[:, dc, :], ln1[:, dc, qc * 512:(qc + 1) * 512])
             for dc in range(8)],
            [("sWV", vgw), ("sDV", vdv_ln1)])
    vpe_gl = P.val("sPE")
    regG.close()
    W("scalar", "sPE", vpe_gl)
    vex = I("scalar", lambda e: e.activation(scores[:, :], psw[0:8, 0:TOWN],
                                             AF.Exp), "sACT", 1)
    for qc in range(4):
        mmg(psw[32:33, qc * 512:(qc + 1) * 512],
            [(onecf[0:8, 0:1], scores[:, qc * 512:(qc + 1) * 512])],
            [("sACT", vex), ("sDV", vdv_ln1)])
    vpe_gs = P.val("sPE")
    vsc = 0
    for th in range(2):
        c0 = th * 1024
        W("vector", "sPE", vpe_gs)
        vrc = I("vector", lambda e, o=mrow[:, :], i=psw[32:33, c0:c0 + 1024]:
                e.reciprocal(o, i), "sDV", 1)
        for qc in range(2):
            mmg(psS2[0:8, qc * 512:(qc + 1) * 512],
                [(onesr[0:1, 0:8], mrow[0:1, qc * 512:(qc + 1) * 512])],
                [("sDV", vrc), ("sDV", vsc)])
        W("vector", "sPE", P.val("sPE"))
        vsc = I("vector", lambda e, o=scores[:, c0:c0 + 1024],
                i0=scores[:, c0:c0 + 1024], i1=psS2[0:8, 0:1024]:
                e.tensor_tensor(o, i0, i1, ALU.mult), "sDV", 1)
    if debug:
        W("sync", "sDV", vsc)
        I("sync", lambda e: e.dma_start(dbg["d_scores"], scores[:, :]), "sST", 16)

    regF = ExitStack()
    w1b = [SBx(regF, f"w1b{i}", [128, 8, 512], BF16, side="right") for i in range(2)]
    w2b = SBx(regF, "w2b", [128, 8, 512], BF16, side="right")
    hbf = SBx(regF, "hbf", [128, 8, TOWN], BF16, side="right")
    gb = SBx(regF, "gb", [128, TOWN], BF16, side="right")

    def w1_src(e_, fq, fo):
        a = fq * 1024 + fo * 512
        if e_ < 8:
            return lambda dc: w1d[e_, dc * 128:(dc + 1) * 128, a:a + 512]
        return lambda dc: ws1d[dc * 128:(dc + 1) * 128, a:a + 512]

    def w2_src(e_, fq, dh):
        if e_ < 8:
            return lambda fc: w2d[e_, fq * 1024 + fc * 128:
                                  fq * 1024 + (fc + 1) * 128,
                                  dh * 512:(dh + 1) * 512]
        return lambda fc: ws2d[fq * 1024 + fc * 128: fq * 1024 + (fc + 1) * 128,
                               dh * 512:(dh + 1) * 512]

    w1_loaded = {}
    w1_cons = {}

    def load_w1(step):
        e_, fq, fo = EORDER[step // 8], (step % 8) // 2, step % 2
        if step - 2 >= 0:
            W("gpsimd", "sPE", w1_cons.get(step - 2, 0))
        src = w1_src(e_, fq, fo)
        sname = "sW1a" if step % 2 == 0 else "sW1b"
        v = None
        for dc in range(8):
            v = I("gpsimd", lambda e, d=w1b[step % 2][:, dc, :], s=src(dc):
                  e.dma_start(d, s), sname, 16)
        w1_loaded[step] = (sname, v)

    w2_cons = [0]

    def load_w2(e_, fq, dh):
        if w2_cons[0]:
            W("gpsimd", "sPE", w2_cons[0])
        src = w2_src(e_, fq, dh)
        v = None
        for fc in range(8):
            v = I("gpsimd", lambda e, d=w2b[:, fc, :], s=src(fc):
                  e.dma_start(d, s), "sW2s", 16)
        return v

    W("gpsimd", "sPE", vpe_gl)
    W("gpsimd", "sDV", vdv_ln1)
    load_w1(0)
    load_w1(1)
    W("sync", "sDV", vsc)
    vgi = I("sync", lambda e: e.dma_start(ccin[:, :], scores[:, :]), "sGP", 16)
    W("gpsimd", "sGP", vgi)
    vcc = I("gpsimd", lambda e: e.collective_compute(
        "AllGather", ALU.bypass, replica_groups=[list(range(8))],
        ins=[ccin[:, :]], outs=[ccout[:, :]]), "sCC", 1)
    W("sync", "sCC", vcc)
    vgo = I("sync", lambda e: e.dma_start(gall[:, :], ccout[:, :]), "sGP", 16)

    chain("vector", lambda e: e.memset(st16[:, 8:9], 0.0))
    chain("vector", lambda e: e.memset(st16[:, 9:10], 1.0))
    chain("vector", lambda e: e.memset(st16[:, 10:11], 1.0))
    chain("vector", lambda e: e.memset(midc[:, :], 1.0))
    vmem_sh = chain("vector", lambda e: e.memset(rowX[:, :], 1.0))
    W("vector", "sGP", vgo)
    for it in range(NITER):
        if it > 0:
            W("vector", "sDM2", 16 * it)
        I("vector", lambda e: e.tensor_scalar(gmask[:, :], gall[:, :],
                                              midc[:, :], None, ALU.is_ge))
        vcn = I("vector", lambda e: e.reduce_sum(cn64[:, :], gmask[:, :], AX.X),
                "sDV", 1)
        W("sync", "sDV", vcn)
        vdm = I("sync", lambda e: e.dma_start(crow[:, :], cn64[:, :]), "sDM", 16)
        W("vector", "sDM", vdm)
        chain("vector", lambda e: e.tensor_reduce(st16[:, 11:12], crow[:, :],
                                                  AX.X, ALU.add))
        chain("vector", lambda e: e.tensor_scalar(st16[:, 12:13], st16[:, 11:12],
                                                  KSEL, None, ALU.is_ge))
        chain("vector", lambda e: e.tensor_tensor(st16[:, 13:14], st16[:, 12:13],
                                                  st16[:, 9:10], ALU.mult))
        chain("vector", lambda e: e.tensor_tensor(st16[:, 8:9], st16[:, 8:9],
                                                  st16[:, 13:14], ALU.add))
        last = (it == NITER - 1)
        if not last:
            chain("vector", lambda e: e.tensor_scalar(st16[:, 9:10],
                                                      st16[:, 9:10],
                                                      0.5, None, ALU.mult))
            chain("vector", lambda e: e.tensor_tensor(st16[:, 10:11],
                                                      st16[:, 8:9],
                                                      st16[:, 9:10], ALU.add))
        srcc = st16[:, 8:9] if last else st16[:, 10:11]
        vmr = I("vector", lambda e, s_=srcc: e.tensor_scalar(
            midrow[:, :], on64r[:, :], s_, None, ALU.mult), "sDV", 1)
        if not last:
            W("sync", "sDV", vmr)
            I("sync", lambda e: e.dma_start(midc[:, :], midrow[:, :]), "sDM2", 16)
    W("sync", "sDV", P.val("sDV"))
    vlc = I("sync", lambda e: e.dma_start(locol[:, :], midrow[0:1, 0:8]),
            "sDM", 16)
    W("vector", "sDM", vlc)
    I("vector", lambda e: e.tensor_scalar(gmask[0:8, :], scores[:, :],
                                          locol[:, :], None, ALU.is_ge))
    vgt = I("vector", lambda e: e.tensor_tensor(scores[:, :], gmask[0:8, :],
                                                scores[:, :], ALU.mult),
            "sDV", 1)
    if debug:
        W("sync", "sDV", vgt)
        I("sync", lambda e: e.dma_start(dbg["d_gate"], scores[:, :]), "sST", 16)


    hA_free = [0, 0, 0, 0]
    yB_free = [0, 0, 0, 0]
    rowx_free = 0

    for step_e, e_ in enumerate(EORDER):
        for th in range(2):
            if e_ < 8:
                W("sync", "sDV", vgt)
                if rowx_free:
                    W("sync", "sPE", rowx_free)
                stw = I("sync", lambda e, r=e_, c0=th * 1024: e.dma_start(
                    rowX[:, :], scores[r:r + 1, c0:c0 + 1024]), "sDM", 16)
                stg_w = [("sDM", stw)]
            else:
                stg_w = [("sCH", vmem_sh)]
            for qc in range(2):
                tc = th * 2 + qc
                vbc = mmg(pswq[tc][0:128, :],
                          [(onesr[0:1, :], rowX[0:1, qc * 512:(qc + 1) * 512])],
                          stg_w + [("sDV", yB_free[tc])])
                rowx_free = vbc
                W("vector", "sPE", vbc)
                yB_free[tc] = I("vector",
                                lambda e, o=gb[:, tc * 512:(tc + 1) * 512],
                                i=pswq[tc][:, :]: e.tensor_copy(o, i), "sDV", 1)

        for fq in range(4):
            for fo in range(2):
                step = step_e * 8 + fq * 2 + fo
                bi = step % 2
                sw1n, vw1 = w1_loaded[step]
                if step + 1 < NE * 8 and (step + 1) not in w1_loaded:
                    load_w1(step + 1)
                vlast = None
                for ftl in range(4):
                    ft = fo * 4 + ftl
                    for tc in range(4):
                        bank = (ftl * 4 + tc) % 4
                        vmm = mmg(pst4[bank][:, 0:512],
                                  [(w1b[bi][:, dc, ftl * 128:(ftl + 1) * 128],
                                    ln1[:, dc, tc * 512:(tc + 1) * 512])
                                   for dc in range(8)],
                                  [(sw1n, vw1), ("sDV", vdv_ln1),
                                   ("sACT", hA_free[bank])])
                        vlast = vmm
                        W("scalar", "sPE", vmm)
                        bcol = b1sb[:, e_, fq * 8 + ft: fq * 8 + ft + 1]
                        hA_free[bank] = I("scalar", lambda e,
                                          o=hbf[:, ft, tc * 512:(tc + 1) * 512],
                                          i=pst4[bank][:, 0:512], bc=bcol:
                                          e.activation(o, i, AF.Gelu_apprx_tanh,
                                                       bias=bc), "sACT", 1)
                w1_cons[step] = vlast
            vact_h = P.val("sACT")
            for dh in range(2):
                vw2 = load_w2(e_, fq, dh)
                vlb = None
                for dtl in range(4):
                    dt = dh * 4 + dtl
                    for tc in range(4):
                        vmm = mmg(pswq[tc][0:128, :],
                                  [(w2b[:, fc, dtl * 128:(dtl + 1) * 128],
                                    hbf[:, fc, tc * 512:(tc + 1) * 512])
                                   for fc in range(8)],
                                  [("sW2s", vw2), ("sACT", vact_h),
                                   ("sDV", yB_free[tc])])
                        vlb = vmm
                        W("vector", "sPE", vmm)
                        bval = b2sb[:, e_, dt:dt + 1] if fq == 3 else 0.0
                        I("vector", lambda e, o=sqtmp[:, 0:512],
                          i0=pswq[tc][:, :], sc=bval,
                          i1=gb[:, tc * 512:(tc + 1) * 512]:
                          e.scalar_tensor_tensor(o, i0, sc, i1, ALU.add,
                                                 ALU.mult))
                        yB_free[tc] = I("vector", lambda e,
                                        o=out_acc[:, dt, tc * 512:(tc + 1) * 512],
                                        i0=out_acc[:, dt,
                                                   tc * 512:(tc + 1) * 512],
                                        i1=sqtmp[:, 0:512]:
                                        e.tensor_tensor(o, i0, i1, ALU.add),
                                        "sDV", 1)
                w2_cons[0] = vlb

    vdv_ffn = P.val("sDV")
    regF.close()

    ost = SBx(es, "ost", [128, 512])
    layer_norm(out_acc, ln1, vdv_ffn, store_dram=out)
    W("sync", "sST", P.val("sST"))

    with nc.Block() as block:
        def mk(eng_name):
            def body(e):
                for (en, fn) in P.items:
                    if en == eng_name:
                        fn(e)
            return body
        block.tensor(mk("tensor"))
        block.vector(mk("vector"))
        block.scalar(mk("scalar"))
        block.gpsimd(mk("gpsimd"))
        block.sync(mk("sync"))

    es.close()
    return nc


def _rope_tables():
    t = np.arange(T, dtype=np.float64)
    inv = 1.0 / (10000.0 ** (np.arange(0, HD, 2, dtype=np.float64) / HD))
    ang = np.outer(inv, t)
    c = np.cos(ang); s = np.sin(ang)
    cos128 = np.zeros((128, T), np.float64)
    sin128 = np.zeros((128, T), np.float64)
    for r in range(64):
        g, i = divmod(r, 32)
        if i < 16:
            d = 16 * g + i
            cos128[r] = c[d]; cos128[64 + r] = c[d]
            sin128[r] = -s[d]; sin128[64 + r] = -s[d]
        else:
            d = 16 * g + i - 16
            cos128[r] = c[d]; cos128[64 + r] = c[d]
            sin128[r] = s[d]; sin128[64 + r] = s[d]
    return cos128, sin128


def _perm_head():
    p = []
    for h in range(H):
        base = h * HD
        p += [base + i for i in range(16)]
        p += [base + 32 + i for i in range(16)]
        p += [base + 16 + i for i in range(16)]
        p += [base + 48 + i for i in range(16)]
    return np.array(p)


def _amask_for(sub):
    obl = _owned_blocks(sub)
    m = np.zeros((128, NB * 256), np.float32)
    kk = np.arange(128)[:, None]
    qq = np.arange(128)[None, :]
    tri = (kk <= qq).astype(np.float32)
    for kt in range(NB):
        qsm = _qs_min(kt)
        for colblk in range(2):
            a = qsm // 128 + colblk
            if a >= 16:
                continue
            j = obl[a]
            if j < kt:
                blk = np.zeros((128, 128), np.float32)
            elif j == kt:
                blk = tri
            else:
                blk = np.ones((128, 128), np.float32)
            m[:, kt * 256 + colblk * 128: kt * 256 + (colblk + 1) * 128] = blk
    return m


_nc_cache = {}


def kernel(L0_attn_x, L0_attn_wq, L0_attn_wk, L0_attn_wv, L0_attn_wo,
           L0_ffn_gate_weight, L0_ffn_W1, L0_ffn_b1, L0_ffn_W2, L0_ffn_b2,
           L0_ffn_Ws1, L0_ffn_bs1, L0_ffn_Ws2, L0_ffn_bs2):
    x = np.ascontiguousarray(np.asarray(L0_attn_x, dtype=np.float32))
    wq = np.asarray(L0_attn_wq, np.float32); wk = np.asarray(L0_attn_wk, np.float32)
    wv = np.asarray(L0_attn_wv, np.float32); wo = np.asarray(L0_attn_wo, np.float32)
    gw = np.asarray(L0_ffn_gate_weight, np.float32)
    w1 = np.ascontiguousarray(np.asarray(L0_ffn_W1, np.float32))
    w2 = np.ascontiguousarray(np.asarray(L0_ffn_W2, np.float32))
    b1 = np.asarray(L0_ffn_b1, np.float32); b2 = np.asarray(L0_ffn_b2, np.float32)
    ws1 = np.ascontiguousarray(np.asarray(L0_ffn_Ws1, np.float32))
    ws2 = np.ascontiguousarray(np.asarray(L0_ffn_Ws2, np.float32))
    bs1 = np.asarray(L0_ffn_bs1, np.float32); bs2 = np.asarray(L0_ffn_bs2, np.float32)

    perm = _perm_head()
    wqT = np.ascontiguousarray(wq[perm].T)
    wkT = np.ascontiguousarray(wk[perm].T)
    wvT = np.ascontiguousarray(wv.T)
    woT = np.ascontiguousarray(wo.T)
    gwT = np.ascontiguousarray(gw.T)
    b1a = np.zeros((128, NE, F // 128), np.float32)
    b2a = np.zeros((128, NE, D // 128), np.float32)
    b1a[:, :8, :] = b1.reshape(E, F // 128, 128).transpose(2, 0, 1)
    b1a[:, 8, :] = bs1.reshape(F // 128, 128).T
    b2a[:, :8, :] = b2.reshape(E, D // 128, 128).transpose(2, 0, 1)
    b2a[:, 8, :] = bs2.reshape(D // 128, 128).T

    cos128, sin128 = _rope_tables()
    cosKb = cos128.astype(BF16NP)
    sinKb = sin128.astype(BF16NP)
    onc_b = np.ones((128, 1), BF16NP)
    on33 = np.zeros((128, 33), np.float32)
    on33[0:64, 0] = 1.0
    on33[64:128, 32] = 1.0
    on33 = on33.astype(BF16NP)
    onc_f = np.ones((128, 1), np.float32)
    onr_f = np.ones((1, 128), np.float32)
    on64 = np.ones((1, 64), np.float32)

    in_maps = []
    for c in range(8):
        b, sub = divmod(c, 2)
        obl = _owned_blocks(sub)
        own_tok = np.concatenate([np.arange(j * 128, (j + 1) * 128) for j in obl])
        xtb = np.ascontiguousarray(x[b].T)
        xownb = np.ascontiguousarray(xtb[:, own_tok])
        cosQ = np.ascontiguousarray(cos128[:, own_tok]).astype(BF16NP)
        sinQ = np.ascontiguousarray(sin128[:, own_tok]).astype(BF16NP)
        am = _amask_for(sub).astype(BF16NP)
        in_maps.append({
            "xt": xtb, "xown": xownb, "wqT": wqT, "wkT": wkT, "wvT": wvT,
            "woT": woT, "gwT": gwT, "w1": w1, "w2": w2, "ws1": ws1, "ws2": ws2,
            "b1a": b1a, "b2a": b2a, "cosK": cosKb, "sinK": sinKb,
            "cosQ": cosQ, "sinQ": sinQ, "amask": am, "onc_b": onc_b, "on33": on33,
            "onc_f": onc_f, "onr_f": onr_f, "on64": on64,
        })

    key = ("nc", DEBUG)
    if key not in _nc_cache:
        _nc_cache[key] = build_nc(debug=DEBUG)
    nc = _nc_cache[key]
    res = run_bass_kernel_spmd(nc, in_maps, core_ids=list(range(8)))

    outp = np.zeros((B, T, D), np.float32)
    for c in range(8):
        b, sub = divmod(c, 2)
        obl = _owned_blocks(sub)
        own_tok = np.concatenate([np.arange(j * 128, (j + 1) * 128) for j in obl])
        outp[b, own_tok, :] = res.results[c]["out"].T
    if DEBUG:
        kernel._dbg = res.results
    return outp


# version 09
# speedup vs baseline: 1.0728x; 1.0548x over previous
import numpy as np
import ml_dtypes
import concourse.bass as bass
import concourse.mybir as mybir
from concourse.bass_utils import run_bass_kernel_spmd

F32 = mybir.dt.float32
BF16 = mybir.dt.bfloat16
AF = mybir.ActivationFunctionType
ALU = mybir.AluOpType
AX = mybir.AxisListType
BF16NP = ml_dtypes.bfloat16

B, T, D = 4, 4096, 1024
H, HD = 16, 64
NB = T // 128
TOWN = 2048
E, F = 8, 4096
NE = 9
NITER = 42
KSEL = float(2 * B * T)
EPS = 1e-5
SHUF = [(i + 16) % 32 for i in range(32)]
EORDER = [8, 0, 1, 2, 3, 4, 5, 6, 7]

DEBUG = 0


def _owned_blocks(sub):
    return [j for j in range(NB) if j % 4 in ((0, 3) if sub == 0 else (1, 2))]


def _qs_min(kt):
    c0 = len([j for j in _owned_blocks(0) if j < kt])
    c1 = len([j for j in _owned_blocks(1) if j < kt])
    return 128 * min(c0, c1)


class Plan:
    def __init__(self):
        self.items = []
        self.cnt = {}

    def on(self, eng, fn):
        self.items.append((eng, fn))

    def inc(self, sem, by=1):
        self.cnt[sem] = self.cnt.get(sem, 0) + by
        return self.cnt[sem]

    def val(self, sem):
        return self.cnt.get(sem, 0)


def build_nc(debug=0):
    from contextlib import ExitStack
    nc = bass.Bass("TRN2", target_bir_lowering=False)
    es = ExitStack()

    def SBx(stack, n, s, d=F32, side=None):
        return stack.enter_context(nc.sbuf_tensor(n, list(s), d, side=side))

    def din(name, shape, dt=F32):
        return nc.dram_tensor(name, list(shape), dt, kind="ExternalInput").ap()

    xt = din("xt", [D, T])
    xown = din("xown", [D, TOWN])
    wqT = din("wqT", [D, D]); wkT = din("wkT", [D, D])
    wvT = din("wvT", [D, D]); woT = din("woT", [D, D])
    gwT = din("gwT", [D, E])
    w1d = din("w1", [E, D, F]); w2d = din("w2", [E, F, D])
    ws1d = din("ws1", [D, F]); ws2d = din("ws2", [F, D])
    b1a = din("b1a", [128, NE, F // 128])
    b2a = din("b2a", [128, NE, D // 128])
    cosK = din("cosK", [128, T], BF16)
    sinK = din("sinK", [128, T], BF16)
    cosQ = din("cosQ", [128, TOWN], BF16)
    sinQ = din("sinQ", [128, TOWN], BF16)
    amask = din("amask", [128, NB * 256], BF16)
    onc_b = din("onc_b", [128, 1], BF16)
    on33 = din("on33", [128, 33], BF16)
    onc_f = din("onc_f", [128, 1])
    onr_f = din("onr_f", [1, 128])
    on64 = din("on64", [1, 64])

    out = nc.dram_tensor("out", [D, TOWN], F32, kind="ExternalOutput").ap()
    dbg = {}
    if debug:
        for nm, sh in [("d_psw0", [128, 2048]), ("d_psw1", [128, 2048]),
                       ("d_s0", [128, 1024]),
                       ("d_p0", [128, 1024]), ("d_z1", [128, 1024]),
                       ("d_p1", [128, 1024]), ("d_z4", [128, 64]),
                       ("d_z8", [128, 64]), ("d_z16", [128, 64]),
                       ("d_rz0", [1, 1024]), ("d_rz1", [1, 1024]),
                       ("d_attnE", [128, 4 * TOWN]),
                       ("d_qh", [128, 4 * TOWN]), ("d_kh", [128, 4 * T]),
                       ("d_vh", [128, NB * 520]), ("d_biasq", [128, 8]),
                       ("d_attn", [128, 8 * TOWN]), ("d_r1", [128, 8 * TOWN]),
                       ("d_ln1", [128, 8 * TOWN]), ("d_scores", [8, TOWN]),
                       ("d_gate", [8, TOWN]), ("d_u0", [1, 16])]:
            dbg[nm] = nc.dram_tensor(nm, sh, F32, kind="ExternalOutput").ap()

    ccin = nc.dram_tensor("ccin", [E, TOWN], F32)
    ccout = nc.dram_tensor("ccout", [64, TOWN], F32)

    mrow = SBx(es, "mrow", [1, 1024])
    vrow = SBx(es, "vrow", [1, 1024])
    rrow = SBx(es, "rrow", [1, 1024])
    rowX = SBx(es, "rowX", [1, 1024])
    onesr = SBx(es, "onesr", [1, 128])
    on64r = SBx(es, "on64r", [1, 64])
    onecb = SBx(es, "onecb", [128, 1], BF16)
    zcol = SBx(es, "zcol", [128, 1], BF16)
    on33b = SBx(es, "on33b", [128, 33], BF16)
    mqcol = SBx(es, "mqcol", [128, 4])
    mkcol = SBx(es, "mkcol", [128, 4])
    ubcol = SBx(es, "ubcol", [128, 4])
    ubrow = SBx(es, "ubrow", [1, 8])
    onecf = SBx(es, "onecf", [128, 1])
    biasq = SBx(es, "biasq", [128, 8])
    b1sb = SBx(es, "b1sb", [128, NE, F // 128])
    b2sb = SBx(es, "b2sb", [128, NE, D // 128])
    st16 = SBx(es, "st16", [1, 16])
    cn64 = SBx(es, "cn64", [64, 1])
    crow = SBx(es, "crow", [1, 64])
    midc = SBx(es, "midc", [64, 1])
    midrow = SBx(es, "midrow", [1, 64])
    locol = SBx(es, "locol", [8, 1])
    sqtmp = SBx(es, "sqtmp", [128, TOWN], BF16)
    mb = SBx(es, "mb", [128, 512])
    rb = SBx(es, "rb", [128, 512])
    regAttn = ExitStack()
    attn = SBx(regAttn, "attn", [128, 8, TOWN], BF16, side="right")

    psS = es.enter_context(nc.psum_tensor("psS", [128, 1024], F32))
    psS2 = es.enter_context(nc.psum_tensor("psS2", [128, 1024], F32))
    psw = es.enter_context(nc.psum_tensor("psw", [128, 2048], F32))
    pst4 = [psS[:, 0:512], psS[:, 512:1024], psS2[:, 0:512], psS2[:, 512:1024]]
    pswq = [psw[:, i * 512:(i + 1) * 512] for i in range(4)]

    sems = {}
    for n in ["sX", "sW", "sM", "sPE", "sACT", "sDV", "sGP", "sST",
              "sCH", "sDM", "sDM2", "sCC", "sW0", "sW1", "sW2", "sW3",
              "sWV", "sWO", "sMt", "sMk", "sMx", "sW1a", "sW1b", "sW2s", "sW2t"]:
        sems[n] = es.enter_context(nc.semaphore(n))

    P = Plan()

    def W(eng, sem, val):
        if val:
            P.on(eng, lambda e, s=sems[sem], v=val: e.wait_ge(s, v))

    def I(eng, fn, sem=None, by=1):
        if sem is None:
            P.on(eng, fn)
            return None
        v = P.inc(sem, by)
        P.on(eng, lambda e, f=fn, s=sems[sem], b=by: f(e).then_inc(s, b))
        return v

    def chain(eng, fn, sem="sCH"):
        v = P.inc(sem)
        def go(e, f=fn, s=sems[sem], vv=v):
            f(e).then_inc(s, 1)
            e.wait_ge(s, vv)
        P.on(eng, go)
        return v

    def mmg(out_ap, pairs, waits=(), first=True):
        for (sname, v) in waits:
            W("tensor", sname, v)
        n = len(pairs)
        for i in range(n - 1):
            l, r = pairs[i]
            I("tensor", lambda e, l_=l, r_=r, o=out_ap, s1=(i == 0 and first):
              e.matmul(o, l_, r_, start=s1, stop=False))
        l, r = pairs[n - 1]
        return I("tensor", lambda e, l_=l, r_=r, o=out_ap,
                 s1=(n == 1 and first):
                 e.matmul(o, l_, r_, start=s1, stop=True), "sPE", 1)

    I("gpsimd", lambda e: e.memset(zcol[:, :], 0.0), "sGP", 1)
    sGP_z = P.val("sGP")
    for (dst, src) in [(b1sb[:, :, :], b1a), (b2sb[:, :, :], b2a),
                       (onesr[:, :], onr_f), (on64r[:, :], on64),
                       (onecb[:, :], onc_b), (onecf[:, :], onc_f),
                       (on33b[:, :], on33)]:
        I("sync", lambda e, d=dst, s=src: e.dma_start(d, s), "sM", 16)
    sM_const = P.val("sM")

    vdv_attn_parts = []
    for hg in range(2):
        regH = ExitStack()
        q_h = SBx(regH, f"q_h{hg}", [128, 4, TOWN], BF16, side="right")
        k_h = SBx(regH, f"k_h{hg}", [128, 4, T], BF16, side="right")
        v_h = SBx(regH, f"v_h{hg}", [128, NB * 520], BF16, side="right")

        regP = ExitStack()
        cosKt = SBx(regP, f"cosKt{hg}", [128, T], BF16, side="right")
        sinKt = SBx(regP, f"sinKt{hg}", [128, T], BF16, side="right")
        cosQt = SBx(regP, f"cosQt{hg}", [128, TOWN], BF16, side="right")
        sinQt = SBx(regP, f"sinQt{hg}", [128, TOWN], BF16, side="right")
        xbf = SBx(regP, f"xbf{hg}", [128, 8, 1024], BF16, side="right")
        xqf = SBx(regP, f"xqf{hg}", [128, 8, 512], BF16, side="right")
        wsl = [SBx(regP, f"wsl{hg}_{i}", [128, 8, 128], BF16, side="right") for i in range(4)]
        wvb = SBx(regP, f"wvb{hg}", [128, 8, 512], BF16, side="right")
        rt2 = SBx(regP, f"rt2_{hg}", [128, 512], BF16, side="right")
        rt3 = SBx(regP, f"rt3_{hg}", [128, 512], BF16, side="right")
        rt4 = SBx(regP, f"rt4_{hg}", [128, 512], BF16, side="right")

        W("sync", "sPE", P.val("sPE"))
        W("sync", "sDV", P.val("sDV"))
        W("sync", "sACT", P.val("sACT"))
        for (dst, src) in [(cosKt[:, :], cosK), (sinKt[:, :], sinK),
                           (cosQt[:, :], cosQ), (sinQt[:, :], sinQ)]:
            I("sync", lambda e, d=dst, s=src: e.dma_start(d, s), "sMt", 16)
        W("vector", "sMt", P.val("sMt"))
        if hg > 0:
            W("gpsimd", "sPE", P.val("sPE"))
            W("gpsimd", "sDV", P.val("sDV"))
            W("gpsimd", "sACT", P.val("sACT"))
        vap = bass.AP(v_h, 64, [[520 * NB, 128], [520, NB], [65, 8], [1, 1]])
        vgp_ones = I("gpsimd", lambda e, a=vap: e.memset(a, 1.0), "sGP", 1)

        ps_free = [0, 0, 0, 0]
        ps_rot = [0]

        def next_bank():
            i = ps_rot[0] % 4
            ps_rot[0] += 1
            return i

        wsl_cons = [0, 0, 0, 0]
        wsl_rot = [0]

        def load_wslice(src_dram, gc):
            bi = wsl_rot[0] % 4
            wsl_rot[0] += 1
            if wsl_cons[bi]:
                W("gpsimd", "sPE", wsl_cons[bi])
            sname = f"sW{bi}"
            v = None
            for dc in range(8):
                v = I("gpsimd", lambda e, d=wsl[bi][:, dc, :],
                      s=src_dram[dc * 128:(dc + 1) * 128, gc * 128:(gc + 1) * 128]:
                      e.dma_start(d, s), sname, 16)
            return bi, v, sname

        def rope_evac(bank, dstap, cosap, sinap):
            pst = pst4[bank][:, 0:512]
            I("vector", lambda e, o=rt4[:, :], i=pst: e.tensor_copy(o, i))
            I("vector", lambda e, o=rt2[:, :], i=rt4[:, :], m=SHUF:
              e.stream_shuffle(o, i, m))
            I("vector", lambda e, o=rt3[:, :], i0=pst, i1=cosap:
              e.tensor_tensor(o, i0, i1, ALU.mult))
            I("vector", lambda e, o=rt2[:, :], i0=rt2[:, :], i1=sinap:
              e.tensor_tensor(o, i0, i1, ALU.mult))
            return I("vector", lambda e, o=dstap, i0=rt3[:, :], i1=rt2[:, :]:
                     e.tensor_tensor(o, i0, i1, ALU.add), "sDV", 1)

        first_vdma = [True]
        for sec in range(4):
            if sec > 0 or hg > 0:
                W("gpsimd", "sPE", P.val("sPE"))
            vx = None
            for dc in range(8):
                vx = I("gpsimd", lambda e, d=xbf[:, dc, :],
                       s=xt[dc * 128:(dc + 1) * 128, sec * 1024:(sec + 1) * 1024]:
                       e.dma_start(d, s), "sX", 16)
            for dc in range(8):
                vx = I("gpsimd", lambda e, d=xqf[:, dc, :],
                       s=xown[dc * 128:(dc + 1) * 128, sec * 512:(sec + 1) * 512]:
                       e.dma_start(d, s), "sX", 16)
            sXv = P.val("sX")

            for cl in range(4):
                gc = 4 * hg + cl
                bi, vw, swn = load_wslice(wkT, gc)
                for half in range(2):
                    bank = next_bank()
                    vmm = mmg(pst4[bank][:, 0:512],
                              [(wsl[bi][:, dc, :],
                                xbf[:, dc, half * 512:(half + 1) * 512])
                               for dc in range(8)],
                              [(swn, vw), ("sX", sXv), ("sDV", ps_free[bank])])
                    wsl_cons[bi] = vmm
                    c0 = sec * 1024 + half * 512
                    W("vector", "sPE", vmm)
                    ps_free[bank] = rope_evac(bank, k_h[:, cl, c0:c0 + 512],
                                              cosKt[:, c0:c0 + 512],
                                              sinKt[:, c0:c0 + 512])
            for cl in range(4):
                gc = 4 * hg + cl
                bi, vw, swn = load_wslice(wqT, gc)
                bank = next_bank()
                vmm = mmg(pst4[bank][:, 0:512],
                          [(wsl[bi][:, dc, :], xqf[:, dc, :]) for dc in range(8)],
                          [(swn, vw), ("sX", sXv), ("sDV", ps_free[bank])])
                wsl_cons[bi] = vmm
                c0 = sec * 512
                W("vector", "sPE", vmm)
                ps_free[bank] = rope_evac(bank, q_h[:, cl, c0:c0 + 512],
                                          cosQt[:, c0:c0 + 512],
                                          sinQt[:, c0:c0 + 512])

            if sec == 0:
                W("gpsimd", "sPE", P.val("sPE")) if hg > 0 else None
                vwv = None
                for dc in range(8):
                    vwv = I("gpsimd", lambda e, d=wvb[:, dc, :],
                            s=wvT[dc * 128:(dc + 1) * 128,
                                  hg * 512:(hg + 1) * 512]:
                            e.dma_start(d, s), "sWV", 16)
                vwv_val = vwv
            for tt in range(8):
                bank = next_bank()
                vmm = mmg(pst4[bank][:, 0:512],
                          [(xbf[:, dc, tt * 128:(tt + 1) * 128], wvb[:, dc, :])
                           for dc in range(8)],
                          [("sWV", vwv_val), ("sX", sXv), ("sDV", ps_free[bank])])
                kt = sec * 8 + tt
                dstap = bass.AP(v_h, kt * 520, [[520 * NB, 128], [65, 8], [1, 64]])
                W("vector", "sPE", vmm)
                if first_vdma[0]:
                    W("vector", "sGP", vgp_ones)
                    first_vdma[0] = False
                srcap = pst4[bank][:, 0:512].rearrange("p (h x) -> p h x", h=8)
                ps_free[bank] = I("vector", lambda e, o=dstap, i=srcap:
                                  e.tensor_copy(o, i), "sDV", 1)

        vdv_red = 0
        for (srct, iscol, nch, ncols) in [(q_h, mqcol, 4, TOWN),
                                          (k_h, mkcol, 4, T)]:
            for dc in range(nch):
                nhalf = ncols // TOWN
                vmmu = 0
                for ch in range(nhalf):
                    if vdv_red:
                        W("vector", "sPE", vmmu) if False else None
                    vs = I("vector", lambda e, o=sqtmp[:, :],
                           i=srct[:, dc, ch * TOWN:(ch + 1) * TOWN]:
                           e.tensor_tensor(o, i, i, ALU.mult), "sDV", 1)
                    for qc in range(4):
                        vmmu = mmg(psw[0:33, qc * 512:(qc + 1) * 512],
                                   [(on33b[:, :],
                                     sqtmp[:, qc * 512:(qc + 1) * 512])],
                                   [("sDV", vs), ("sDV", vdv_red)],
                                   first=(ch == 0))
                W("vector", "sPE", vmmu)
                vdv_red = I("vector", lambda e, o=iscol[0:33, dc:dc + 1],
                            i=psw[0:33, 0:TOWN]:
                            e.reduce_max(o, i, AX.X), "sDV", 1)
        W("vector", "sDV", vdv_red)
        vub = I("vector", lambda e: e.tensor_tensor(ubcol[:, :], mqcol[:, :],
                                                    mkcol[:, :], ALU.mult),
                "sDV", 1)
        W("scalar", "sDV", vub)
        vsq_u = I("scalar", lambda e: e.activation(ubcol[:, :], ubcol[:, :],
                                                   AF.Sqrt), "sACT", 1)
        W("vector", "sACT", vsq_u)
        chain("vector", lambda e: e.tensor_scalar(ubcol[:, :], ubcol[:, :],
                                                  -0.125, None, ALU.mult))
        vng = I("vector", lambda e: e.engine_nop(), "sDV", 1)
        W("gpsimd", "sDV", P.val("sDV"))
        W("sync", "sDV", P.val("sDV"))
        W("sync", "sCH", P.val("sCH"))
        if debug >= 2 and hg == 0:
            I("gpsimd", lambda e: e.dma_start(dbg["d_qh"],
              q_h[:, :, :].rearrange("p c t -> p (c t)")), "sST", 16)
            I("gpsimd", lambda e: e.dma_start(dbg["d_kh"],
              k_h[:, :, :].rearrange("p c t -> p (c t)")), "sST", 16)
            I("gpsimd", lambda e: e.dma_start(dbg["d_vh"], v_h[:, :]), "sST", 16)
        W("sync", "sDV", vng)
        vgu = I("sync", lambda e: e.dma_start(
            ubrow[:, :], bass.AP(ubcol, 0, [[32 * 4, 2], [1, 4]])), "sDM", 16)
        vdv_b = 0
        for dcl in range(4):
            for hhx in range(2):
                srcj = hhx * 4 + dcl
                vqb = mmg(psS[0:128, 0:1],
                          [(onesr[0:1, :], ubrow[0:1, srcj:srcj + 1])],
                          [("sDM", vgu), ("sDV", vdv_b),
                           ("sACT", P.val("sACT"))])
                W("vector", "sPE", vqb)
                col = 2 * dcl + hhx
                chain("vector", lambda e, o=biasq[:, col:col + 1],
                      i=psS[:, 0:1]: e.tensor_copy(o, i))
                vdv_b = I("vector", lambda e: e.engine_nop(), "sDV", 1)
        if debug >= 2 and hg == 0:
            W("sync", "sDV", P.val("sDV"))
            I("sync", lambda e: e.dma_start(dbg["d_biasq"], biasq[:, :]),
              "sST", 16)
        regP.close()

        regA = ExitStack()
        trim = SBx(regA, f"trim{hg}", [128, NB * 256], BF16, side="right")
        ptA = SBx(regA, f"ptA{hg}", [128, 1024], BF16, side="right")
        ptB = SBx(regA, f"ptB{hg}", [128, 1024], BF16, side="right")
        W("sync", "sPE", P.val("sPE"))
        W("sync", "sDV", P.val("sDV"))
        W("sync", "sACT", P.val("sACT"))
        vtr = I("sync", lambda e: e.dma_start(trim[:, :], amask), "sMk", 16)
        W("vector", "sMk", vtr)

        def att_tiles(qh):
            res = []
            lo_, hi_ = qh * 1024, (qh + 1) * 1024
            for kt in range(NB):
                a = max(_qs_min(kt), lo_)
                if a < hi_:
                    res.append((kt, a, hi_))
            return res

        W("tensor", "sDV", vdv_b)
        for hpl in range(4):
            hp = 4 * hg + hpl
            for qh in range(2):
                tiles = att_tiles(qh)
                kt0 = tiles[0][0]
                prev = None
                exp_prev = 0
                pv_prev = 0

                def subchunks(l0):
                    ck = []
                    if l0 < 512:
                        ck.append((l0, 512))
                    ck.append((max(l0, 512), 1024))
                    return ck

                def emit_pv(pkt, pl0, vmsk, start_flag):
                    W("tensor", "sDV", vmsk)
                    vv = None
                    for (c0_, c1_) in subchunks(pl0):
                        for hhx in range(2):
                            off = pkt * 520 + (2 * hpl + hhx) * 65
                            vv = I("tensor", lambda e,
                                   l=v_h[:, off:off + 65],
                                   r=(ptA if hhx == 0 else ptB)[:, c0_:c1_],
                                   o=psw[0:65, hhx * 1024 + c0_:
                                         hhx * 1024 + c1_]:
                                   e.matmul(o, l, r, start=start_flag,
                                            stop=True), "sPE", 1)
                    return vv

                for (kt, a, b_) in tiles:
                    l0 = a - qh * 1024
                    if exp_prev:
                        W("tensor", "sACT", exp_prev)
                    vb = None
                    for (c0_, c1_) in subchunks(l0):
                        ga = qh * 1024 + c0_
                        gb_ = qh * 1024 + c1_
                        I("tensor", lambda e,
                          l=k_h[0:64, hpl, kt * 128:(kt + 1) * 128],
                          r=q_h[0:64, hpl, ga:gb_], o=psS[0:128, c0_:c1_]:
                          e.matmul(o, l, r, start=True, stop=True))
                        vb = I("tensor", lambda e,
                               l=k_h[64:128, hpl, kt * 128:(kt + 1) * 128],
                               r=q_h[64:128, hpl, ga:gb_],
                               o=psS2[0:128, c0_:c1_]:
                               e.matmul(o, l, r, start=True, stop=True),
                               "sPE", 1)
                    if prev is not None:
                        pkt, pl0, vmsk = prev
                        pv_prev = emit_pv(pkt, pl0, vmsk, pkt == kt0)
                        if debug and hg == 0 and hpl == -97 and qh == 0 and kt in (2, 8):
                            nm = {2: "d_z4", 8: "d_z8"}[kt]
                            W("vector", "sPE", pv_prev)
                            vz1 = I("vector", lambda e: e.tensor_copy(
                                mb[:, 0:64], psw[0:128, 0:64]), "sDV", 1)
                            W("sync", "sDV", vz1)
                            vz2 = I("sync", lambda e, n=nm: e.dma_start(
                                dbg[n], mb[:, 0:64]), "sST", 16)
                            W("tensor", "sST", vz2)
                        if debug and hg == 0 and hpl == -98 and qh == 0 and kt == 2:
                            W("vector", "sPE", pv_prev)
                            vz1 = I("vector", lambda e: e.tensor_copy(
                                mb[:, :], psw[0:128, 0:512]), "sDV", 1)
                            W("sync", "sDV", vz1)
                            vz2 = I("sync", lambda e: e.dma_start(
                                dbg["d_z1"][:, 0:512], mb[:, :]), "sST", 16)
                            W("gpsimd", "sDV", vz1)
                            vz3 = I("gpsimd", lambda e: e.dma_start(
                                dbg["d_p1"], ptA[:, :]), "sST", 16)
                            W("tensor", "sST", vz2)
                            W("tensor", "sST", vz3)
                    W("scalar", "sPE", vb)
                    if pv_prev:
                        W("scalar", "sPE", pv_prev)
                    I("scalar", lambda e, o=ptA[:, l0:1024],
                      i=psS[0:128, l0:1024], bc=biasq[:, 2 * hpl:2 * hpl + 1]:
                      e.activation(o, i, AF.Exp, bias=bc, scale=0.125))
                    veB = I("scalar", lambda e, o=ptB[:, l0:1024],
                            i=psS2[0:128, l0:1024],
                            bc=biasq[:, 2 * hpl + 1:2 * hpl + 2]:
                            e.activation(o, i, AF.Exp, bias=bc, scale=0.125),
                            "sACT", 1)
                    exp_prev = veB
                    if debug and hg == 0 and hpl == 1 and qh == 0 and kt == 0:
                        W("vector", "sACT", veB)
                        vd1 = I("vector", lambda e: e.tensor_copy(
                            mb[:, :], psS[0:128, 0:512]), "sDV", 1)
                        vd2 = I("vector", lambda e: e.tensor_copy(
                            rb[:, :], psS[0:128, 512:1024]), "sDV", 1)
                        W("sync", "sDV", vd2)
                        I("sync", lambda e: e.dma_start(
                            dbg["d_s0"][:, 0:512], mb[:, :]), "sST", 16)
                        I("sync", lambda e: e.dma_start(
                            dbg["d_s0"][:, 512:1024], rb[:, :]), "sST", 16)
                        W("gpsimd", "sDV", vd2)
                        vd3 = I("gpsimd", lambda e: e.dma_start(
                            dbg["d_p0"], ptA[:, :]), "sST", 16)
                        W("tensor", "sDV", vd2)
                        W("tensor", "sST", vd3)
                    qs = _qs_min(kt)
                    m0, m1 = max(a, qs), min(b_, qs + 256)
                    W("vector", "sACT", veB)
                    if m0 < m1:
                        mw = m1 - m0
                        ml = m0 - qh * 1024
                        I("vector", lambda e, o=ptA[:, ml:ml + mw],
                          i0=ptA[:, ml:ml + mw],
                          i1=trim[:, kt * 256 + m0 - qs:
                                  kt * 256 + m0 - qs + mw]:
                          e.tensor_tensor(o, i0, i1, ALU.mult))
                        chain("vector", lambda e, o=ptB[:, ml:ml + mw],
                              i0=ptB[:, ml:ml + mw],
                              i1=trim[:, kt * 256 + m0 - qs:
                                      kt * 256 + m0 - qs + mw]:
                              e.tensor_tensor(o, i0, i1, ALU.mult))
                        vmsk = I("vector", lambda e: e.engine_nop(), "sDV", 1)
                    else:
                        vmsk = I("vector", lambda e: e.engine_nop(), "sDV", 1)
                    prev = (kt, l0, vmsk)
                pkt, pl0, vmsk = prev
                vpv = emit_pv(pkt, pl0, vmsk, pkt == kt0)
                W("tensor", "sGP", sGP_z)
                for bk in range(4):
                    vpv = I("tensor", lambda e, o=psw[0:1, bk * 512:bk * 512 + 1]:
                            e.matmul(o, zcol[0:1, 0:1], zcol[0:1, 0:1],
                                     start=False, stop=True), "sPE", 1)
                if debug and hg == 0 and hpl == -95:
                    W("vector", "sPE", vpv)
                    vdbg = I("vector", lambda e: e.tensor_copy(
                        mb[:, :] if False else sqtmp[:, :].rearrange("p t -> p t"),
                        psw[0:128, 0:2048]), "sDV", 1)
                    W("gpsimd", "sDV", vdbg)
                    vdbg2 = I("gpsimd", lambda e, n=f"d_psw{qh}": e.dma_start(
                        dbg[n], sqtmp[:, :]), "sST", 16)
                    W("tensor", "sST", vdbg2)
                W("vector", "sPE", vpv)
                vfence = I("vector", lambda e: e.engine_nop(), "sDV", 1)
                W("tensor", "sDV", vfence)
                vev_last = 0
                for hh in range(2):
                    region = hh * 1024
                    W("vector", "sPE", vpv)
                    chain("vector", lambda e, o=rowX[:, 0:1024],
                          i=psw[64:65, region:region + 1024]:
                          e.reciprocal(o, i))
                    vrz = I("vector", lambda e: e.engine_nop(), "sDV", 1)
                    if debug and hg == 0 and hpl == -96 and hh == 0:
                        W("sync", "sDV", vrz)
                        vzz = I("sync", lambda e, n=f"d_rz{qh}": e.dma_start(
                            dbg[n], rowX[:, 0:1024]), "sST", 16)
                        W("vector", "sST", vzz)
                    for qc in range(2):
                        vbc = mmg(psS[0:64, 0:512],
                                  [(onesr[0:1, 0:64],
                                    rowX[0:1, qc * 512:(qc + 1) * 512])],
                                  [("sDV", vrz), ("sACT", P.val("sACT"))])
                        W("scalar", "sPE", vbc)
                        if vev_last:
                            W("scalar", "sDV", vev_last)
                        vcp = I("scalar", lambda e:
                                e.copy(rb[0:64, :], psS[0:64, 0:512]),
                                "sACT", 1)
                        W("vector", "sACT", vcp)
                        dst = attn[64 * hh:64 * (hh + 1), hp,
                                   qh * 1024 + qc * 512:
                                   qh * 1024 + (qc + 1) * 512]
                        vev_last = I("vector", lambda e, o=dst,
                                     i0=psw[0:64, region + qc * 512:
                                            region + (qc + 1) * 512],
                                     i1=rb[0:64, :]:
                                     e.tensor_tensor(o, i0, i1, ALU.mult),
                                     "sDV", 1)
                W("tensor", "sDV", P.val("sDV"))
                W("tensor", "sACT", P.val("sACT"))
        W("gpsimd", "sDV", P.val("sDV"))
        W("sync", "sDV", P.val("sDV"))
        if debug >= 2 and hg == 0:
            vda = I("gpsimd", lambda e: e.dma_start(dbg["d_attnE"],
                    attn[:, 0:4, :].rearrange("p c t -> p (c t)")), "sST", 16)
            W("tensor", "sST", vda)
        regA.close()
        regH.close()

    vdv_attn = P.val("sDV")
    if debug >= 2:
        W("gpsimd", "sDV", vdv_attn)
        I("gpsimd", lambda e: e.dma_start(dbg["d_attn"],
          attn[:, :, :].rearrange("p c t -> p (c t)")), "sST", 16)
        W("sync", "sDV", vdv_attn)
        I("sync", lambda e: e.dma_start(dbg["d_u0"], st16[:, :]), "sST", 16)

    out_acc = SBx(es, "out_acc", [128, 8, TOWN])
    ln1 = SBx(es, "ln1", [128, 8, TOWN], BF16)
    regO = ExitStack()
    wob = SBx(regO, "wob", [128, 8, 1024], BF16, side="right")

    for dc in range(8):
        I("sync", lambda e, d=out_acc[:, dc, :],
          s=xown[dc * 128:(dc + 1) * 128, :]: e.dma_start(d, s), "sMx", 16)
    sM_xown = P.val("sMx")
    W("gpsimd", "sPE", P.val("sPE"))
    W("gpsimd", "sDV", P.val("sDV"))
    W("gpsimd", "sACT", P.val("sACT"))
    vwo = None
    for dc in range(8):
        vwo = I("gpsimd", lambda e, d=wob[:, dc, :],
                s=woT[dc * 128:(dc + 1) * 128, :]: e.dma_start(d, s), "sWO", 16)

    pb_free = [0, 0, 0, 0]
    k = 0
    for dc in range(8):
        for qc in range(4):
            bank = k % 4
            k += 1
            vmm = mmg(pst4[bank][:, 0:512],
                      [(wob[:, ic, dc * 128:(dc + 1) * 128],
                        attn[:, ic, qc * 512:(qc + 1) * 512]) for ic in range(8)],
                      [("sWO", vwo), ("sDV", vdv_attn), ("sMx", sM_xown),
                       ("sDV", pb_free[bank])])
            W("vector", "sPE", vmm)
            pb_free[bank] = I("vector",
                              lambda e, o=out_acc[:, dc, qc * 512:(qc + 1) * 512],
                              i0=out_acc[:, dc, qc * 512:(qc + 1) * 512],
                              i1=pst4[bank][:, 0:512]:
                              e.tensor_tensor(o, i0, i1, ALU.add), "sDV", 1)
    vdv_r1 = P.val("sDV")
    if debug:
        W("sync", "sDV", vdv_r1)
        I("sync", lambda e: e.dma_start(dbg["d_r1"],
          out_acc[:, :, :].rearrange("p c t -> p (c t)")), "sST", 16)

    def layer_norm(src_fp32, dst_bf16, vdv_ready, apply_bf16=True,
                   store_dram=None):
        W("vector", "sDV", vdv_ready)
        vmm_prev = 0
        for dc in range(8):
            if vmm_prev:
                W("vector", "sPE", vmm_prev)
            vcp = I("vector", lambda e, o=dst_bf16[:, dc, :], i=src_fp32[:, dc, :]:
                    e.tensor_copy(o, i), "sDV", 1)
            vsq = I("vector", lambda e, o=sqtmp[:, :], i=dst_bf16[:, dc, :]:
                    e.tensor_tensor(o, i, i, ALU.mult), "sDV", 1)
            for qc in range(4):
                mmg(psw[0:1, qc * 512:(qc + 1) * 512],
                    [(onecb[:, :], dst_bf16[:, dc, qc * 512:(qc + 1) * 512])],
                    [("sDV", vcp)], first=(dc == 0))
                vmm_prev = mmg(psw[32:33, qc * 512:(qc + 1) * 512],
                               [(onecb[:, :], sqtmp[:, qc * 512:(qc + 1) * 512])],
                               [("sDV", vsq)], first=(dc == 0))
        vpe_st = P.val("sPE")
        vst_all = 0
        vc2 = 0
        vap_last = 0
        for th in range(2):
            c0 = th * 1024
            W("vector", "sPE", vpe_st)
            I("vector", lambda e, o=mrow[:, :], i=psw[0:1, c0:c0 + 1024]:
              e.tensor_scalar(o, i, 1.0 / D, None, ALU.mult))
            I("vector", lambda e, o=vrow[:, :], i=psw[32:33, c0:c0 + 1024]:
              e.tensor_scalar(o, i, 1.0 / D, None, ALU.mult))
            I("vector", lambda e: e.tensor_tensor(rrow[:, :], mrow[:, :],
                                                  mrow[:, :], ALU.mult))
            I("vector", lambda e: e.tensor_tensor(vrow[:, :], vrow[:, :],
                                                  rrow[:, :], ALU.subtract))
            I("vector", lambda e: e.tensor_scalar(vrow[:, :], vrow[:, :], EPS,
                                                  None, ALU.add))
            vvar = I("vector", lambda e: e.reciprocal(vrow[:, :], vrow[:, :]),
                     "sDV", 1)
            W("scalar", "sDV", vvar)
            vsr = I("scalar", lambda e: e.activation(rrow[:, :], vrow[:, :],
                                                     AF.Sqrt), "sACT", 1)
            for qc in range(2):
                vb1 = mmg(psS[0:128, 0:512],
                          [(onesr[0:1, :], mrow[0:1, qc * 512:(qc + 1) * 512])],
                          [("sACT", vsr), ("sDV", vc2)])
                vb2 = mmg(psS2[0:128, 0:512],
                          [(onesr[0:1, :], rrow[0:1, qc * 512:(qc + 1) * 512])],
                          [])
                W("vector", "sPE", vb2)
                if vap_last:
                    W("vector", "sDV", vap_last)
                vc1 = I("vector", lambda e: e.tensor_copy(mb[:, :], psS[:, 0:512]),
                        "sDV", 1)
                vc2 = I("vector", lambda e: e.tensor_copy(rb[:, :], psS2[:, 0:512]),
                        "sDV", 1)
                cc0 = c0 + qc * 512
                for dc in range(8):
                    if store_dram is not None and vst_all:
                        W("vector", "sST", vst_all)
                    I("vector", lambda e, o=sqtmp[:, 0:512],
                      i0=src_fp32[:, dc, cc0:cc0 + 512], i1=mb[:, :]:
                      e.tensor_tensor(o, i0, i1, ALU.subtract))
                    if store_dram is None:
                        vap_last = I("vector", lambda e,
                                     o=dst_bf16[:, dc, cc0:cc0 + 512],
                                     i0=sqtmp[:, 0:512], i1=rb[:, :]:
                                     e.tensor_tensor(o, i0, i1, ALU.mult),
                                     "sDV", 1)
                    else:
                        vap_last = I("vector", lambda e, o=ost[:, :],
                                     i0=sqtmp[:, 0:512], i1=rb[:, :]:
                                     e.tensor_tensor(o, i0, i1, ALU.mult),
                                     "sDV", 1)
                        W("sync", "sDV", vap_last)
                        vst_all = I("sync", lambda e,
                                    d=store_dram[dc * 128:(dc + 1) * 128,
                                                 cc0:cc0 + 512],
                                    s=ost[:, :]: e.dma_start(d, s), "sST", 16)
        return P.val("sDV")

    vdv_ln1 = layer_norm(out_acc, ln1, vdv_r1)
    if debug:
        W("gpsimd", "sDV", vdv_ln1)
        I("gpsimd", lambda e: e.dma_start(dbg["d_ln1"],
          ln1[:, :, :].rearrange("p c t -> p (c t)")), "sST", 16)
    regO.close()
    regAttn.close()

    scores = SBx(es, "scores", [8, TOWN])
    gall = SBx(es, "gall", [64, TOWN])
    gmask = SBx(es, "gmask", [64, TOWN], BF16)
    regG = ExitStack()
    gwb = SBx(regG, "gwb", [128, 8, 8], BF16, side="right")
    W("gpsimd", "sPE", P.val("sPE"))
    vgw = None
    for dc in range(8):
        vgw = I("gpsimd", lambda e, d=gwb[:, dc, :],
                s=gwT[dc * 128:(dc + 1) * 128, :]: e.dma_start(d, s), "sWV", 16)
    for qc in range(4):
        mmg(psw[0:8, qc * 512:(qc + 1) * 512],
            [(gwb[:, dc, :], ln1[:, dc, qc * 512:(qc + 1) * 512])
             for dc in range(8)],
            [("sWV", vgw), ("sDV", vdv_ln1)])
    vpe_gl = P.val("sPE")
    regG.close()
    W("scalar", "sPE", vpe_gl)
    vex = I("scalar", lambda e: e.activation(scores[:, :], psw[0:8, 0:TOWN],
                                             AF.Exp), "sACT", 1)
    for qc in range(4):
        mmg(psw[32:33, qc * 512:(qc + 1) * 512],
            [(onecf[0:8, 0:1], scores[:, qc * 512:(qc + 1) * 512])],
            [("sACT", vex), ("sDV", vdv_ln1)])
    vpe_gs = P.val("sPE")
    vsc = 0
    for th in range(2):
        c0 = th * 1024
        W("vector", "sPE", vpe_gs)
        vrc = I("vector", lambda e, o=mrow[:, :], i=psw[32:33, c0:c0 + 1024]:
                e.reciprocal(o, i), "sDV", 1)
        for qc in range(2):
            mmg(psS2[0:8, qc * 512:(qc + 1) * 512],
                [(onesr[0:1, 0:8], mrow[0:1, qc * 512:(qc + 1) * 512])],
                [("sDV", vrc), ("sDV", vsc)])
        W("vector", "sPE", P.val("sPE"))
        vsc = I("vector", lambda e, o=scores[:, c0:c0 + 1024],
                i0=scores[:, c0:c0 + 1024], i1=psS2[0:8, 0:1024]:
                e.tensor_tensor(o, i0, i1, ALU.mult), "sDV", 1)
    if debug:
        W("sync", "sDV", vsc)
        I("sync", lambda e: e.dma_start(dbg["d_scores"], scores[:, :]), "sST", 16)

    regF = ExitStack()
    w1b = [SBx(regF, f"w1b{i}", [128, 8, 256], BF16, side="right") for i in range(2)]
    w2b = [SBx(regF, f"w2b{i}", [128, 8, 512], BF16, side="right")
           for i in range(2)]
    hbf = SBx(regF, "hbf", [128, 8, TOWN], BF16, side="right")
    gb = SBx(regF, "gb", [128, TOWN], BF16, side="right")

    def w1_src(e_, fq, fo4):
        a = fq * 1024 + fo4 * 256
        if e_ < 8:
            return lambda dc: w1d[e_, dc * 128:(dc + 1) * 128, a:a + 256]
        return lambda dc: ws1d[dc * 128:(dc + 1) * 128, a:a + 256]

    def w2_src(e_, fq, dh):
        if e_ < 8:
            return lambda fc: w2d[e_, fq * 1024 + fc * 128:
                                  fq * 1024 + (fc + 1) * 128,
                                  dh * 512:(dh + 1) * 512]
        return lambda fc: ws2d[fq * 1024 + fc * 128: fq * 1024 + (fc + 1) * 128,
                               dh * 512:(dh + 1) * 512]

    w1_loaded = {}
    w1_cons = {}

    def load_w1(step):
        e_, fq, fo4 = EORDER[step // 16], (step % 16) // 4, step % 4
        if step - 2 >= 0:
            W("gpsimd", "sPE", w1_cons.get(step - 2, 0))
        src = w1_src(e_, fq, fo4)
        sname = "sW1a" if step % 2 == 0 else "sW1b"
        v = None
        for dc in range(8):
            v = I("gpsimd", lambda e, d=w1b[step % 2][:, dc, :], s=src(dc):
                  e.dma_start(d, s), sname, 16)
        w1_loaded[step] = (sname, v)

    w2_loaded = {}
    w2_cons = {}

    def load_w2(step):
        e_, fq, dh = EORDER[step // 8], (step % 8) // 2, step % 2
        if step - 2 >= 0:
            W("gpsimd", "sPE", w2_cons.get(step - 2, 0))
        src = w2_src(e_, fq, dh)
        sname = "sW2s" if step % 2 == 0 else "sW2t"
        v = None
        for fc in range(8):
            v = I("gpsimd", lambda e, d=w2b[step % 2][:, fc, :], s=src(fc):
                  e.dma_start(d, s), sname, 16)
        w2_loaded[step] = (sname, v)

    W("gpsimd", "sPE", vpe_gl)
    W("gpsimd", "sDV", vdv_ln1)
    load_w1(0)
    load_w1(1)
    load_w2(0)
    load_w2(1)
    W("sync", "sDV", vsc)
    vgi = I("sync", lambda e: e.dma_start(ccin[:, :], scores[:, :]), "sGP", 16)
    W("gpsimd", "sGP", vgi)
    vcc = I("gpsimd", lambda e: e.collective_compute(
        "AllGather", ALU.bypass, replica_groups=[list(range(8))],
        ins=[ccin[:, :]], outs=[ccout[:, :]]), "sCC", 1)
    W("sync", "sCC", vcc)
    vgo = I("sync", lambda e: e.dma_start(gall[:, :], ccout[:, :]), "sGP", 16)

    chain("vector", lambda e: e.memset(st16[:, 8:9], 0.0))
    chain("vector", lambda e: e.memset(st16[:, 9:10], 1.0))
    chain("vector", lambda e: e.memset(st16[:, 10:11], 1.0))
    chain("vector", lambda e: e.memset(midc[:, :], 1.0))
    vmem_sh = chain("vector", lambda e: e.memset(rowX[:, :], 1.0))
    W("vector", "sGP", vgo)
    for it in range(NITER):
        if it > 0:
            W("vector", "sDM2", 16 * it)
        I("vector", lambda e: e.tensor_scalar(gmask[:, :], gall[:, :],
                                              midc[:, :], None, ALU.is_ge))
        vcn = I("vector", lambda e: e.reduce_sum(cn64[:, :], gmask[:, :], AX.X),
                "sDV", 1)
        W("sync", "sDV", vcn)
        vdm = I("sync", lambda e: e.dma_start(crow[:, :], cn64[:, :]), "sDM", 16)
        W("vector", "sDM", vdm)
        chain("vector", lambda e: e.tensor_reduce(st16[:, 11:12], crow[:, :],
                                                  AX.X, ALU.add))
        chain("vector", lambda e: e.tensor_scalar(st16[:, 12:13], st16[:, 11:12],
                                                  KSEL, None, ALU.is_ge))
        chain("vector", lambda e: e.tensor_tensor(st16[:, 13:14], st16[:, 12:13],
                                                  st16[:, 9:10], ALU.mult))
        chain("vector", lambda e: e.tensor_tensor(st16[:, 8:9], st16[:, 8:9],
                                                  st16[:, 13:14], ALU.add))
        last = (it == NITER - 1)
        if not last:
            chain("vector", lambda e: e.tensor_scalar(st16[:, 9:10],
                                                      st16[:, 9:10],
                                                      0.5, None, ALU.mult))
            chain("vector", lambda e: e.tensor_tensor(st16[:, 10:11],
                                                      st16[:, 8:9],
                                                      st16[:, 9:10], ALU.add))
        srcc = st16[:, 8:9] if last else st16[:, 10:11]
        vmr = I("vector", lambda e, s_=srcc: e.tensor_scalar(
            midrow[:, :], on64r[:, :], s_, None, ALU.mult), "sDV", 1)
        if not last:
            W("sync", "sDV", vmr)
            I("sync", lambda e: e.dma_start(midc[:, :], midrow[:, :]), "sDM2", 16)
    W("sync", "sDV", P.val("sDV"))
    vlc = I("sync", lambda e: e.dma_start(locol[:, :], midrow[0:1, 0:8]),
            "sDM", 16)
    W("vector", "sDM", vlc)
    I("vector", lambda e: e.tensor_scalar(gmask[0:8, :], scores[:, :],
                                          locol[:, :], None, ALU.is_ge))
    vgt = I("vector", lambda e: e.tensor_tensor(scores[:, :], gmask[0:8, :],
                                                scores[:, :], ALU.mult),
            "sDV", 1)
    if debug:
        W("sync", "sDV", vgt)
        I("sync", lambda e: e.dma_start(dbg["d_gate"], scores[:, :]), "sST", 16)


    hA_free = [0, 0, 0, 0]
    yB_free = [0, 0, 0, 0]
    rowx_free = 0

    for step_e, e_ in enumerate(EORDER):
        for th in range(2):
            if e_ < 8:
                W("sync", "sDV", vgt)
                if rowx_free:
                    W("sync", "sPE", rowx_free)
                stw = I("sync", lambda e, r=e_, c0=th * 1024: e.dma_start(
                    rowX[:, :], scores[r:r + 1, c0:c0 + 1024]), "sDM", 16)
                stg_w = [("sDM", stw)]
            else:
                stg_w = [("sCH", vmem_sh)]
            for qc in range(2):
                tc = th * 2 + qc
                vbc = mmg(pswq[tc][0:128, :],
                          [(onesr[0:1, :], rowX[0:1, qc * 512:(qc + 1) * 512])],
                          stg_w + [("sDV", yB_free[tc])])
                rowx_free = vbc
                W("vector", "sPE", vbc)
                yB_free[tc] = I("vector",
                                lambda e, o=gb[:, tc * 512:(tc + 1) * 512],
                                i=pswq[tc][:, :]: e.tensor_copy(o, i), "sDV", 1)

        for fq in range(4):
            for fo4 in range(4):
                step = step_e * 16 + fq * 4 + fo4
                bi = step % 2
                sw1n, vw1 = w1_loaded[step]
                if step + 1 < NE * 16 and (step + 1) not in w1_loaded:
                    load_w1(step + 1)
                vlast = None
                for ftl in range(2):
                    ft = fo4 * 2 + ftl
                    for tc in range(4):
                        bank = (ftl * 4 + tc) % 4
                        vmm = mmg(pst4[bank][:, 0:512],
                                  [(w1b[bi][:, dc, ftl * 128:(ftl + 1) * 128],
                                    ln1[:, dc, tc * 512:(tc + 1) * 512])
                                   for dc in range(8)],
                                  [(sw1n, vw1), ("sDV", vdv_ln1),
                                   ("sACT", hA_free[bank])])
                        vlast = vmm
                        W("scalar", "sPE", vmm)
                        bcol = b1sb[:, e_, fq * 8 + ft: fq * 8 + ft + 1]
                        hA_free[bank] = I("scalar", lambda e,
                                          o=hbf[:, ft, tc * 512:(tc + 1) * 512],
                                          i=pst4[bank][:, 0:512], bc=bcol:
                                          e.activation(o, i, AF.Gelu_apprx_tanh,
                                                       bias=bc), "sACT", 1)
                w1_cons[step] = vlast
            vact_h = P.val("sACT")
            for dh in range(2):
                s2 = step_e * 8 + fq * 2 + dh
                if s2 not in w2_loaded:
                    load_w2(s2)
                sw2n, vw2 = w2_loaded[s2]
                if s2 + 1 < NE * 8 and (s2 + 1) not in w2_loaded:
                    load_w2(s2 + 1)
                vlb = None
                for dtl in range(4):
                    dt = dh * 4 + dtl
                    for tc in range(4):
                        vmm = mmg(pswq[tc][0:128, :],
                                  [(w2b[s2 % 2][:, fc, dtl * 128:(dtl + 1) * 128],
                                    hbf[:, fc, tc * 512:(tc + 1) * 512])
                                   for fc in range(8)],
                                  [(sw2n, vw2), ("sACT", vact_h),
                                   ("sDV", yB_free[tc])])
                        vlb = vmm
                        W("vector", "sPE", vmm)
                        bval = b2sb[:, e_, dt:dt + 1] if fq == 3 else 0.0
                        I("vector", lambda e, o=sqtmp[:, 0:512],
                          i0=pswq[tc][:, :], sc=bval,
                          i1=gb[:, tc * 512:(tc + 1) * 512]:
                          e.scalar_tensor_tensor(o, i0, sc, i1, ALU.add,
                                                 ALU.mult))
                        yB_free[tc] = I("vector", lambda e,
                                        o=out_acc[:, dt, tc * 512:(tc + 1) * 512],
                                        i0=out_acc[:, dt,
                                                   tc * 512:(tc + 1) * 512],
                                        i1=sqtmp[:, 0:512]:
                                        e.tensor_tensor(o, i0, i1, ALU.add),
                                        "sDV", 1)
                w2_cons[s2] = vlb

    vdv_ffn = P.val("sDV")
    regF.close()

    ost = SBx(es, "ost", [128, 512])
    layer_norm(out_acc, ln1, vdv_ffn, store_dram=out)
    W("sync", "sST", P.val("sST"))

    with nc.Block() as block:
        def mk(eng_name):
            def body(e):
                for (en, fn) in P.items:
                    if en == eng_name:
                        fn(e)
            return body
        block.tensor(mk("tensor"))
        block.vector(mk("vector"))
        block.scalar(mk("scalar"))
        block.gpsimd(mk("gpsimd"))
        block.sync(mk("sync"))

    es.close()
    return nc


def _rope_tables():
    t = np.arange(T, dtype=np.float64)
    inv = 1.0 / (10000.0 ** (np.arange(0, HD, 2, dtype=np.float64) / HD))
    ang = np.outer(inv, t)
    c = np.cos(ang); s = np.sin(ang)
    cos128 = np.zeros((128, T), np.float64)
    sin128 = np.zeros((128, T), np.float64)
    for r in range(64):
        g, i = divmod(r, 32)
        if i < 16:
            d = 16 * g + i
            cos128[r] = c[d]; cos128[64 + r] = c[d]
            sin128[r] = -s[d]; sin128[64 + r] = -s[d]
        else:
            d = 16 * g + i - 16
            cos128[r] = c[d]; cos128[64 + r] = c[d]
            sin128[r] = s[d]; sin128[64 + r] = s[d]
    return cos128, sin128


def _perm_head():
    p = []
    for h in range(H):
        base = h * HD
        p += [base + i for i in range(16)]
        p += [base + 32 + i for i in range(16)]
        p += [base + 16 + i for i in range(16)]
        p += [base + 48 + i for i in range(16)]
    return np.array(p)


def _amask_for(sub):
    obl = _owned_blocks(sub)
    m = np.zeros((128, NB * 256), np.float32)
    kk = np.arange(128)[:, None]
    qq = np.arange(128)[None, :]
    tri = (kk <= qq).astype(np.float32)
    for kt in range(NB):
        qsm = _qs_min(kt)
        for colblk in range(2):
            a = qsm // 128 + colblk
            if a >= 16:
                continue
            j = obl[a]
            if j < kt:
                blk = np.zeros((128, 128), np.float32)
            elif j == kt:
                blk = tri
            else:
                blk = np.ones((128, 128), np.float32)
            m[:, kt * 256 + colblk * 128: kt * 256 + (colblk + 1) * 128] = blk
    return m


_nc_cache = {}


def kernel(L0_attn_x, L0_attn_wq, L0_attn_wk, L0_attn_wv, L0_attn_wo,
           L0_ffn_gate_weight, L0_ffn_W1, L0_ffn_b1, L0_ffn_W2, L0_ffn_b2,
           L0_ffn_Ws1, L0_ffn_bs1, L0_ffn_Ws2, L0_ffn_bs2):
    x = np.ascontiguousarray(np.asarray(L0_attn_x, dtype=np.float32))
    wq = np.asarray(L0_attn_wq, np.float32); wk = np.asarray(L0_attn_wk, np.float32)
    wv = np.asarray(L0_attn_wv, np.float32); wo = np.asarray(L0_attn_wo, np.float32)
    gw = np.asarray(L0_ffn_gate_weight, np.float32)
    w1 = np.ascontiguousarray(np.asarray(L0_ffn_W1, np.float32))
    w2 = np.ascontiguousarray(np.asarray(L0_ffn_W2, np.float32))
    b1 = np.asarray(L0_ffn_b1, np.float32); b2 = np.asarray(L0_ffn_b2, np.float32)
    ws1 = np.ascontiguousarray(np.asarray(L0_ffn_Ws1, np.float32))
    ws2 = np.ascontiguousarray(np.asarray(L0_ffn_Ws2, np.float32))
    bs1 = np.asarray(L0_ffn_bs1, np.float32); bs2 = np.asarray(L0_ffn_bs2, np.float32)

    perm = _perm_head()
    wqT = np.ascontiguousarray(wq[perm].T)
    wkT = np.ascontiguousarray(wk[perm].T)
    wvT = np.ascontiguousarray(wv.T)
    woT = np.ascontiguousarray(wo.T)
    gwT = np.ascontiguousarray(gw.T)
    b1a = np.zeros((128, NE, F // 128), np.float32)
    b2a = np.zeros((128, NE, D // 128), np.float32)
    b1a[:, :8, :] = b1.reshape(E, F // 128, 128).transpose(2, 0, 1)
    b1a[:, 8, :] = bs1.reshape(F // 128, 128).T
    b2a[:, :8, :] = b2.reshape(E, D // 128, 128).transpose(2, 0, 1)
    b2a[:, 8, :] = bs2.reshape(D // 128, 128).T

    cos128, sin128 = _rope_tables()
    cosKb = cos128.astype(BF16NP)
    sinKb = sin128.astype(BF16NP)
    onc_b = np.ones((128, 1), BF16NP)
    on33 = np.zeros((128, 33), np.float32)
    on33[0:64, 0] = 1.0
    on33[64:128, 32] = 1.0
    on33 = on33.astype(BF16NP)
    onc_f = np.ones((128, 1), np.float32)
    onr_f = np.ones((1, 128), np.float32)
    on64 = np.ones((1, 64), np.float32)

    in_maps = []
    for c in range(8):
        b, sub = divmod(c, 2)
        obl = _owned_blocks(sub)
        own_tok = np.concatenate([np.arange(j * 128, (j + 1) * 128) for j in obl])
        xtb = np.ascontiguousarray(x[b].T)
        xownb = np.ascontiguousarray(xtb[:, own_tok])
        cosQ = np.ascontiguousarray(cos128[:, own_tok]).astype(BF16NP)
        sinQ = np.ascontiguousarray(sin128[:, own_tok]).astype(BF16NP)
        am = _amask_for(sub).astype(BF16NP)
        in_maps.append({
            "xt": xtb, "xown": xownb, "wqT": wqT, "wkT": wkT, "wvT": wvT,
            "woT": woT, "gwT": gwT, "w1": w1, "w2": w2, "ws1": ws1, "ws2": ws2,
            "b1a": b1a, "b2a": b2a, "cosK": cosKb, "sinK": sinKb,
            "cosQ": cosQ, "sinQ": sinQ, "amask": am, "onc_b": onc_b, "on33": on33,
            "onc_f": onc_f, "onr_f": onr_f, "on64": on64,
        })

    key = ("nc", DEBUG)
    if key not in _nc_cache:
        _nc_cache[key] = build_nc(debug=DEBUG)
    nc = _nc_cache[key]
    res = run_bass_kernel_spmd(nc, in_maps, core_ids=list(range(8)))

    outp = np.zeros((B, T, D), np.float32)
    for c in range(8):
        b, sub = divmod(c, 2)
        obl = _owned_blocks(sub)
        own_tok = np.concatenate([np.arange(j * 128, (j + 1) * 128) for j in obl])
        outp[b, own_tok, :] = res.results[c]["out"].T
    if DEBUG:
        kernel._dbg = res.results
    return outp


# version 10 (final)
# speedup vs baseline: 1.0783x; 1.0052x over previous
import numpy as np
import ml_dtypes
import concourse.bass as bass
import concourse.mybir as mybir
from concourse.bass_utils import run_bass_kernel_spmd

F32 = mybir.dt.float32
BF16 = mybir.dt.bfloat16
AF = mybir.ActivationFunctionType
ALU = mybir.AluOpType
AX = mybir.AxisListType
BF16NP = ml_dtypes.bfloat16

B, T, D = 4, 4096, 1024
H, HD = 16, 64
NB = T // 128
TOWN = 2048
E, F = 8, 4096
NE = 9
NITER = 36
KSEL = float(2 * B * T)
EPS = 1e-5
SHUF = [(i + 16) % 32 for i in range(32)]
EORDER = [8, 0, 1, 2, 3, 4, 5, 6, 7]

DEBUG = 0


def _owned_blocks(sub):
    return [j for j in range(NB) if j % 4 in ((0, 3) if sub == 0 else (1, 2))]


def _qs_min(kt):
    c0 = len([j for j in _owned_blocks(0) if j < kt])
    c1 = len([j for j in _owned_blocks(1) if j < kt])
    return 128 * min(c0, c1)


class Plan:
    def __init__(self):
        self.items = []
        self.cnt = {}

    def on(self, eng, fn):
        self.items.append((eng, fn))

    def inc(self, sem, by=1):
        self.cnt[sem] = self.cnt.get(sem, 0) + by
        return self.cnt[sem]

    def val(self, sem):
        return self.cnt.get(sem, 0)


def build_nc(debug=0):
    from contextlib import ExitStack
    nc = bass.Bass("TRN2", target_bir_lowering=False)
    es = ExitStack()

    def SBx(stack, n, s, d=F32, side=None):
        return stack.enter_context(nc.sbuf_tensor(n, list(s), d, side=side))

    def din(name, shape, dt=F32):
        return nc.dram_tensor(name, list(shape), dt, kind="ExternalInput").ap()

    xt = din("xt", [D, T])
    xown = din("xown", [D, TOWN])
    wqT = din("wqT", [D, D]); wkT = din("wkT", [D, D])
    wvT = din("wvT", [D, D]); woT = din("woT", [D, D])
    gwT = din("gwT", [D, E])
    w1d = din("w1", [E, D, F]); w2d = din("w2", [E, F, D])
    ws1d = din("ws1", [D, F]); ws2d = din("ws2", [F, D])
    b1a = din("b1a", [128, NE, F // 128])
    b2a = din("b2a", [128, NE, D // 128])
    cosK = din("cosK", [128, T], BF16)
    sinK = din("sinK", [128, T], BF16)
    cosQ = din("cosQ", [128, TOWN], BF16)
    sinQ = din("sinQ", [128, TOWN], BF16)
    amask = din("amask", [128, NB * 256], BF16)
    onc_b = din("onc_b", [128, 1], BF16)
    on33 = din("on33", [128, 33], BF16)
    onc_f = din("onc_f", [128, 1])
    onr_f = din("onr_f", [1, 128])
    on64 = din("on64", [1, 64])

    out = nc.dram_tensor("out", [D, TOWN], F32, kind="ExternalOutput").ap()
    dbg = {}
    if debug:
        for nm, sh in [("d_psw0", [128, 2048]), ("d_psw1", [128, 2048]),
                       ("d_s0", [128, 1024]),
                       ("d_p0", [128, 1024]), ("d_z1", [128, 1024]),
                       ("d_p1", [128, 1024]), ("d_z4", [128, 64]),
                       ("d_z8", [128, 64]), ("d_z16", [128, 64]),
                       ("d_rz0", [1, 1024]), ("d_rz1", [1, 1024]),
                       ("d_attnE", [128, 4 * TOWN]),
                       ("d_qh", [128, 4 * TOWN]), ("d_kh", [128, 4 * T]),
                       ("d_vh", [128, NB * 520]), ("d_biasq", [128, 8]),
                       ("d_attn", [128, 8 * TOWN]), ("d_r1", [128, 8 * TOWN]),
                       ("d_ln1", [128, 8 * TOWN]), ("d_scores", [8, TOWN]),
                       ("d_gate", [8, TOWN]), ("d_u0", [1, 16])]:
            dbg[nm] = nc.dram_tensor(nm, sh, F32, kind="ExternalOutput").ap()

    ccin = nc.dram_tensor("ccin", [E, TOWN], F32)
    ccout = nc.dram_tensor("ccout", [64, TOWN], F32)

    mrow = SBx(es, "mrow", [1, 1024])
    vrow = SBx(es, "vrow", [1, 1024])
    rrow = SBx(es, "rrow", [1, 1024])
    rowX = SBx(es, "rowX", [1, 1024])
    onesr = SBx(es, "onesr", [1, 128])
    on64r = SBx(es, "on64r", [1, 64])
    onecb = SBx(es, "onecb", [128, 1], BF16)
    zcol = SBx(es, "zcol", [128, 1], BF16)
    on33b = SBx(es, "on33b", [128, 33], BF16)
    mqcol = SBx(es, "mqcol", [128, 4])
    mkcol = SBx(es, "mkcol", [128, 4])
    ubcol = SBx(es, "ubcol", [128, 4])
    ubrow = SBx(es, "ubrow", [1, 8])
    onecf = SBx(es, "onecf", [128, 1])
    biasq = SBx(es, "biasq", [128, 8])
    b1sb = SBx(es, "b1sb", [128, NE, F // 128])
    b2sb = SBx(es, "b2sb", [128, NE, D // 128])
    st16 = SBx(es, "st16", [1, 16])
    cn64 = SBx(es, "cn64", [64, 1])
    crow = SBx(es, "crow", [1, 64])
    midc = SBx(es, "midc", [64, 1])
    midrow = SBx(es, "midrow", [1, 64])
    locol = SBx(es, "locol", [8, 1])
    sqtmp = SBx(es, "sqtmp", [128, TOWN], BF16)
    mb = SBx(es, "mb", [128, 512])
    rb = SBx(es, "rb", [128, 512])
    regAttn = ExitStack()
    attn = SBx(regAttn, "attn", [128, 8, TOWN], BF16, side="right")

    psS = es.enter_context(nc.psum_tensor("psS", [128, 1024], F32))
    psS2 = es.enter_context(nc.psum_tensor("psS2", [128, 1024], F32))
    psw = es.enter_context(nc.psum_tensor("psw", [128, 2048], F32))
    pst4 = [psS[:, 0:512], psS[:, 512:1024], psS2[:, 0:512], psS2[:, 512:1024]]
    pswq = [psw[:, i * 512:(i + 1) * 512] for i in range(4)]

    sems = {}
    for n in ["sX", "sW", "sM", "sPE", "sACT", "sDV", "sGP", "sST",
              "sCH", "sDM", "sDM2", "sCC", "sW0", "sW1", "sW2", "sW3",
              "sWV", "sWO", "sMt", "sMk", "sMx", "sW1a", "sW1b", "sW2s", "sW2t"]:
        sems[n] = es.enter_context(nc.semaphore(n))

    P = Plan()

    def W(eng, sem, val):
        if val:
            P.on(eng, lambda e, s=sems[sem], v=val: e.wait_ge(s, v))

    def I(eng, fn, sem=None, by=1):
        if sem is None:
            P.on(eng, fn)
            return None
        v = P.inc(sem, by)
        P.on(eng, lambda e, f=fn, s=sems[sem], b=by: f(e).then_inc(s, b))
        return v

    def chain(eng, fn, sem="sCH"):
        v = P.inc(sem)
        def go(e, f=fn, s=sems[sem], vv=v):
            f(e).then_inc(s, 1)
            e.wait_ge(s, vv)
        P.on(eng, go)
        return v

    def mmg(out_ap, pairs, waits=(), first=True):
        for (sname, v) in waits:
            W("tensor", sname, v)
        n = len(pairs)
        for i in range(n - 1):
            l, r = pairs[i]
            I("tensor", lambda e, l_=l, r_=r, o=out_ap, s1=(i == 0 and first):
              e.matmul(o, l_, r_, start=s1, stop=False))
        l, r = pairs[n - 1]
        return I("tensor", lambda e, l_=l, r_=r, o=out_ap,
                 s1=(n == 1 and first):
                 e.matmul(o, l_, r_, start=s1, stop=True), "sPE", 1)

    I("gpsimd", lambda e: e.memset(zcol[:, :], 0.0), "sGP", 1)
    sGP_z = P.val("sGP")
    for (dst, src) in [(b1sb[:, :, :], b1a), (b2sb[:, :, :], b2a),
                       (onesr[:, :], onr_f), (on64r[:, :], on64),
                       (onecb[:, :], onc_b), (onecf[:, :], onc_f),
                       (on33b[:, :], on33)]:
        I("sync", lambda e, d=dst, s=src: e.dma_start(d, s), "sM", 16)
    sM_const = P.val("sM")

    vdv_attn_parts = []
    for hg in range(2):
        regH = ExitStack()
        q_h = SBx(regH, f"q_h{hg}", [128, 4, TOWN], BF16, side="right")
        k_h = SBx(regH, f"k_h{hg}", [128, 4, T], BF16, side="right")
        v_h = SBx(regH, f"v_h{hg}", [128, NB * 520], BF16, side="right")

        regP = ExitStack()
        cosKt = SBx(regP, f"cosKt{hg}", [128, T], BF16, side="right")
        sinKt = SBx(regP, f"sinKt{hg}", [128, T], BF16, side="right")
        cosQt = SBx(regP, f"cosQt{hg}", [128, TOWN], BF16, side="right")
        sinQt = SBx(regP, f"sinQt{hg}", [128, TOWN], BF16, side="right")
        xbf = SBx(regP, f"xbf{hg}", [128, 8, 1024], BF16, side="right")
        xqf = SBx(regP, f"xqf{hg}", [128, 8, 512], BF16, side="right")
        wsl = [SBx(regP, f"wsl{hg}_{i}", [128, 8, 128], BF16, side="right") for i in range(4)]
        wvb = SBx(regP, f"wvb{hg}", [128, 8, 512], BF16, side="right")
        rt2 = SBx(regP, f"rt2_{hg}", [128, 512], BF16, side="right")
        rt3 = SBx(regP, f"rt3_{hg}", [128, 512], BF16, side="right")
        rt4 = SBx(regP, f"rt4_{hg}", [128, 512], BF16, side="right")

        W("sync", "sPE", P.val("sPE"))
        W("sync", "sDV", P.val("sDV"))
        W("sync", "sACT", P.val("sACT"))
        for (dst, src) in [(cosKt[:, :], cosK), (sinKt[:, :], sinK),
                           (cosQt[:, :], cosQ), (sinQt[:, :], sinQ)]:
            I("sync", lambda e, d=dst, s=src: e.dma_start(d, s), "sMt", 16)
        W("vector", "sMt", P.val("sMt"))
        if hg > 0:
            W("gpsimd", "sPE", P.val("sPE"))
            W("gpsimd", "sDV", P.val("sDV"))
            W("gpsimd", "sACT", P.val("sACT"))
        vap = bass.AP(v_h, 64, [[520 * NB, 128], [520, NB], [65, 8], [1, 1]])
        vgp_ones = I("gpsimd", lambda e, a=vap: e.memset(a, 1.0), "sGP", 1)

        ps_free = [0, 0, 0, 0]
        ps_rot = [0]

        def next_bank():
            i = ps_rot[0] % 4
            ps_rot[0] += 1
            return i

        wsl_cons = [0, 0, 0, 0]
        wsl_rot = [0]

        def load_wslice(src_dram, gc):
            bi = wsl_rot[0] % 4
            wsl_rot[0] += 1
            if wsl_cons[bi]:
                W("gpsimd", "sPE", wsl_cons[bi])
            sname = f"sW{bi}"
            v = None
            for dc in range(8):
                v = I("gpsimd", lambda e, d=wsl[bi][:, dc, :],
                      s=src_dram[dc * 128:(dc + 1) * 128, gc * 128:(gc + 1) * 128]:
                      e.dma_start(d, s), sname, 16)
            return bi, v, sname

        def rope_evac(bank, dstap, cosap, sinap):
            pst = pst4[bank][:, 0:512]
            I("vector", lambda e, o=rt4[:, :], i=pst: e.tensor_copy(o, i))
            I("vector", lambda e, o=rt2[:, :], i=rt4[:, :], m=SHUF:
              e.stream_shuffle(o, i, m))
            I("vector", lambda e, o=rt3[:, :], i0=pst, i1=cosap:
              e.tensor_tensor(o, i0, i1, ALU.mult))
            I("vector", lambda e, o=rt2[:, :], i0=rt2[:, :], i1=sinap:
              e.tensor_tensor(o, i0, i1, ALU.mult))
            return I("vector", lambda e, o=dstap, i0=rt3[:, :], i1=rt2[:, :]:
                     e.tensor_tensor(o, i0, i1, ALU.add), "sDV", 1)

        first_vdma = [True]
        for sec in range(4):
            if sec > 0 or hg > 0:
                W("gpsimd", "sPE", P.val("sPE"))
            vx = None
            for dc in range(8):
                vx = I("gpsimd", lambda e, d=xbf[:, dc, :],
                       s=xt[dc * 128:(dc + 1) * 128, sec * 1024:(sec + 1) * 1024]:
                       e.dma_start(d, s), "sX", 16)
            for dc in range(8):
                vx = I("gpsimd", lambda e, d=xqf[:, dc, :],
                       s=xown[dc * 128:(dc + 1) * 128, sec * 512:(sec + 1) * 512]:
                       e.dma_start(d, s), "sX", 16)
            sXv = P.val("sX")

            for cl in range(4):
                gc = 4 * hg + cl
                bi, vw, swn = load_wslice(wkT, gc)
                for half in range(2):
                    bank = next_bank()
                    vmm = mmg(pst4[bank][:, 0:512],
                              [(wsl[bi][:, dc, :],
                                xbf[:, dc, half * 512:(half + 1) * 512])
                               for dc in range(8)],
                              [(swn, vw), ("sX", sXv), ("sDV", ps_free[bank])])
                    wsl_cons[bi] = vmm
                    c0 = sec * 1024 + half * 512
                    W("vector", "sPE", vmm)
                    ps_free[bank] = rope_evac(bank, k_h[:, cl, c0:c0 + 512],
                                              cosKt[:, c0:c0 + 512],
                                              sinKt[:, c0:c0 + 512])
            for cl in range(4):
                gc = 4 * hg + cl
                bi, vw, swn = load_wslice(wqT, gc)
                bank = next_bank()
                vmm = mmg(pst4[bank][:, 0:512],
                          [(wsl[bi][:, dc, :], xqf[:, dc, :]) for dc in range(8)],
                          [(swn, vw), ("sX", sXv), ("sDV", ps_free[bank])])
                wsl_cons[bi] = vmm
                c0 = sec * 512
                W("vector", "sPE", vmm)
                ps_free[bank] = rope_evac(bank, q_h[:, cl, c0:c0 + 512],
                                          cosQt[:, c0:c0 + 512],
                                          sinQt[:, c0:c0 + 512])

            if sec == 0:
                W("gpsimd", "sPE", P.val("sPE")) if hg > 0 else None
                vwv = None
                for dc in range(8):
                    vwv = I("gpsimd", lambda e, d=wvb[:, dc, :],
                            s=wvT[dc * 128:(dc + 1) * 128,
                                  hg * 512:(hg + 1) * 512]:
                            e.dma_start(d, s), "sWV", 16)
                vwv_val = vwv
            for tt in range(8):
                bank = next_bank()
                vmm = mmg(pst4[bank][:, 0:512],
                          [(xbf[:, dc, tt * 128:(tt + 1) * 128], wvb[:, dc, :])
                           for dc in range(8)],
                          [("sWV", vwv_val), ("sX", sXv), ("sDV", ps_free[bank])])
                kt = sec * 8 + tt
                dstap = bass.AP(v_h, kt * 520, [[520 * NB, 128], [65, 8], [1, 64]])
                W("vector", "sPE", vmm)
                if first_vdma[0]:
                    W("vector", "sGP", vgp_ones)
                    first_vdma[0] = False
                srcap = pst4[bank][:, 0:512].rearrange("p (h x) -> p h x", h=8)
                ps_free[bank] = I("vector", lambda e, o=dstap, i=srcap:
                                  e.tensor_copy(o, i), "sDV", 1)

        vdv_red = 0
        for (srct, iscol, nch, ncols) in [(q_h, mqcol, 4, TOWN),
                                          (k_h, mkcol, 4, T)]:
            for dc in range(nch):
                nhalf = ncols // TOWN
                vmmu = 0
                for ch in range(nhalf):
                    if vdv_red:
                        W("vector", "sPE", vmmu) if False else None
                    vs = I("vector", lambda e, o=sqtmp[:, :],
                           i=srct[:, dc, ch * TOWN:(ch + 1) * TOWN]:
                           e.tensor_tensor(o, i, i, ALU.mult), "sDV", 1)
                    for qc in range(4):
                        vmmu = mmg(psw[0:33, qc * 512:(qc + 1) * 512],
                                   [(on33b[:, :],
                                     sqtmp[:, qc * 512:(qc + 1) * 512])],
                                   [("sDV", vs), ("sDV", vdv_red)],
                                   first=(ch == 0))
                W("vector", "sPE", vmmu)
                vdv_red = I("vector", lambda e, o=iscol[0:33, dc:dc + 1],
                            i=psw[0:33, 0:TOWN]:
                            e.reduce_max(o, i, AX.X), "sDV", 1)
        W("vector", "sDV", vdv_red)
        vub = I("vector", lambda e: e.tensor_tensor(ubcol[:, :], mqcol[:, :],
                                                    mkcol[:, :], ALU.mult),
                "sDV", 1)
        W("scalar", "sDV", vub)
        vsq_u = I("scalar", lambda e: e.activation(ubcol[:, :], ubcol[:, :],
                                                   AF.Sqrt), "sACT", 1)
        W("vector", "sACT", vsq_u)
        chain("vector", lambda e: e.tensor_scalar(ubcol[:, :], ubcol[:, :],
                                                  -0.125, None, ALU.mult))
        vng = I("vector", lambda e: e.engine_nop(), "sDV", 1)
        W("gpsimd", "sDV", P.val("sDV"))
        W("sync", "sDV", P.val("sDV"))
        W("sync", "sCH", P.val("sCH"))
        if debug >= 2 and hg == 0:
            I("gpsimd", lambda e: e.dma_start(dbg["d_qh"],
              q_h[:, :, :].rearrange("p c t -> p (c t)")), "sST", 16)
            I("gpsimd", lambda e: e.dma_start(dbg["d_kh"],
              k_h[:, :, :].rearrange("p c t -> p (c t)")), "sST", 16)
            I("gpsimd", lambda e: e.dma_start(dbg["d_vh"], v_h[:, :]), "sST", 16)
        W("sync", "sDV", vng)
        vgu = I("sync", lambda e: e.dma_start(
            ubrow[:, :], bass.AP(ubcol, 0, [[32 * 4, 2], [1, 4]])), "sDM", 16)
        vdv_b = 0
        for dcl in range(4):
            for hhx in range(2):
                srcj = hhx * 4 + dcl
                vqb = mmg(psS[0:128, 0:1],
                          [(onesr[0:1, :], ubrow[0:1, srcj:srcj + 1])],
                          [("sDM", vgu), ("sDV", vdv_b),
                           ("sACT", P.val("sACT"))])
                W("vector", "sPE", vqb)
                col = 2 * dcl + hhx
                chain("vector", lambda e, o=biasq[:, col:col + 1],
                      i=psS[:, 0:1]: e.tensor_copy(o, i))
                vdv_b = I("vector", lambda e: e.engine_nop(), "sDV", 1)
        if debug >= 2 and hg == 0:
            W("sync", "sDV", P.val("sDV"))
            I("sync", lambda e: e.dma_start(dbg["d_biasq"], biasq[:, :]),
              "sST", 16)
        regP.close()

        regA = ExitStack()
        trim = SBx(regA, f"trim{hg}", [128, NB * 256], BF16, side="right")
        ptA = SBx(regA, f"ptA{hg}", [128, 1024], BF16, side="right")
        ptB = SBx(regA, f"ptB{hg}", [128, 1024], BF16, side="right")
        W("sync", "sPE", P.val("sPE"))
        W("sync", "sDV", P.val("sDV"))
        W("sync", "sACT", P.val("sACT"))
        vtr = I("sync", lambda e: e.dma_start(trim[:, :], amask), "sMk", 16)
        W("vector", "sMk", vtr)

        def att_tiles(qh):
            res = []
            lo_, hi_ = qh * 1024, (qh + 1) * 1024
            for kt in range(NB):
                a = max(_qs_min(kt), lo_)
                if a < hi_:
                    res.append((kt, a, hi_))
            return res

        W("tensor", "sDV", vdv_b)
        for hpl in range(4):
            hp = 4 * hg + hpl
            for qh in range(2):
                tiles = att_tiles(qh)
                kt0 = tiles[0][0]
                prev = None
                exp_prev = 0
                pv_prev = 0

                def subchunks(l0):
                    ck = []
                    if l0 < 512:
                        ck.append((l0, 512))
                    ck.append((max(l0, 512), 1024))
                    return ck

                def emit_pv(pkt, pl0, vmsk, start_flag):
                    W("tensor", "sDV", vmsk)
                    vv = None
                    for (c0_, c1_) in subchunks(pl0):
                        for hhx in range(2):
                            off = pkt * 520 + (2 * hpl + hhx) * 65
                            vv = I("tensor", lambda e,
                                   l=v_h[:, off:off + 65],
                                   r=(ptA if hhx == 0 else ptB)[:, c0_:c1_],
                                   o=psw[0:65, hhx * 1024 + c0_:
                                         hhx * 1024 + c1_]:
                                   e.matmul(o, l, r, start=start_flag,
                                            stop=True), "sPE", 1)
                    return vv

                for (kt, a, b_) in tiles:
                    l0 = a - qh * 1024
                    if exp_prev:
                        W("tensor", "sACT", exp_prev)
                    vb = None
                    for (c0_, c1_) in subchunks(l0):
                        ga = qh * 1024 + c0_
                        gb_ = qh * 1024 + c1_
                        I("tensor", lambda e,
                          l=k_h[0:64, hpl, kt * 128:(kt + 1) * 128],
                          r=q_h[0:64, hpl, ga:gb_], o=psS[0:128, c0_:c1_]:
                          e.matmul(o, l, r, start=True, stop=True))
                        vb = I("tensor", lambda e,
                               l=k_h[64:128, hpl, kt * 128:(kt + 1) * 128],
                               r=q_h[64:128, hpl, ga:gb_],
                               o=psS2[0:128, c0_:c1_]:
                               e.matmul(o, l, r, start=True, stop=True),
                               "sPE", 1)
                    if prev is not None:
                        pkt, pl0, vmsk = prev
                        pv_prev = emit_pv(pkt, pl0, vmsk, pkt == kt0)
                        if debug and hg == 0 and hpl == -97 and qh == 0 and kt in (2, 8):
                            nm = {2: "d_z4", 8: "d_z8"}[kt]
                            W("vector", "sPE", pv_prev)
                            vz1 = I("vector", lambda e: e.tensor_copy(
                                mb[:, 0:64], psw[0:128, 0:64]), "sDV", 1)
                            W("sync", "sDV", vz1)
                            vz2 = I("sync", lambda e, n=nm: e.dma_start(
                                dbg[n], mb[:, 0:64]), "sST", 16)
                            W("tensor", "sST", vz2)
                        if debug and hg == 0 and hpl == -98 and qh == 0 and kt == 2:
                            W("vector", "sPE", pv_prev)
                            vz1 = I("vector", lambda e: e.tensor_copy(
                                mb[:, :], psw[0:128, 0:512]), "sDV", 1)
                            W("sync", "sDV", vz1)
                            vz2 = I("sync", lambda e: e.dma_start(
                                dbg["d_z1"][:, 0:512], mb[:, :]), "sST", 16)
                            W("gpsimd", "sDV", vz1)
                            vz3 = I("gpsimd", lambda e: e.dma_start(
                                dbg["d_p1"], ptA[:, :]), "sST", 16)
                            W("tensor", "sST", vz2)
                            W("tensor", "sST", vz3)
                    W("scalar", "sPE", vb)
                    if pv_prev:
                        W("scalar", "sPE", pv_prev)
                    I("scalar", lambda e, o=ptA[:, l0:1024],
                      i=psS[0:128, l0:1024], bc=biasq[:, 2 * hpl:2 * hpl + 1]:
                      e.activation(o, i, AF.Exp, bias=bc, scale=0.125))
                    veB = I("scalar", lambda e, o=ptB[:, l0:1024],
                            i=psS2[0:128, l0:1024],
                            bc=biasq[:, 2 * hpl + 1:2 * hpl + 2]:
                            e.activation(o, i, AF.Exp, bias=bc, scale=0.125),
                            "sACT", 1)
                    exp_prev = veB
                    if debug and hg == 0 and hpl == 1 and qh == 0 and kt == 0:
                        W("vector", "sACT", veB)
                        vd1 = I("vector", lambda e: e.tensor_copy(
                            mb[:, :], psS[0:128, 0:512]), "sDV", 1)
                        vd2 = I("vector", lambda e: e.tensor_copy(
                            rb[:, :], psS[0:128, 512:1024]), "sDV", 1)
                        W("sync", "sDV", vd2)
                        I("sync", lambda e: e.dma_start(
                            dbg["d_s0"][:, 0:512], mb[:, :]), "sST", 16)
                        I("sync", lambda e: e.dma_start(
                            dbg["d_s0"][:, 512:1024], rb[:, :]), "sST", 16)
                        W("gpsimd", "sDV", vd2)
                        vd3 = I("gpsimd", lambda e: e.dma_start(
                            dbg["d_p0"], ptA[:, :]), "sST", 16)
                        W("tensor", "sDV", vd2)
                        W("tensor", "sST", vd3)
                    qs = _qs_min(kt)
                    m0, m1 = max(a, qs), min(b_, qs + 256)
                    W("vector", "sACT", veB)
                    if m0 < m1:
                        mw = m1 - m0
                        ml = m0 - qh * 1024
                        I("vector", lambda e, o=ptA[:, ml:ml + mw],
                          i0=ptA[:, ml:ml + mw],
                          i1=trim[:, kt * 256 + m0 - qs:
                                  kt * 256 + m0 - qs + mw]:
                          e.tensor_tensor(o, i0, i1, ALU.mult))
                        chain("vector", lambda e, o=ptB[:, ml:ml + mw],
                              i0=ptB[:, ml:ml + mw],
                              i1=trim[:, kt * 256 + m0 - qs:
                                      kt * 256 + m0 - qs + mw]:
                              e.tensor_tensor(o, i0, i1, ALU.mult))
                        vmsk = I("vector", lambda e: e.engine_nop(), "sDV", 1)
                    else:
                        vmsk = I("vector", lambda e: e.engine_nop(), "sDV", 1)
                    prev = (kt, l0, vmsk)
                pkt, pl0, vmsk = prev
                vpv = emit_pv(pkt, pl0, vmsk, pkt == kt0)
                W("tensor", "sGP", sGP_z)
                for bk in range(4):
                    vpv = I("tensor", lambda e, o=psw[0:1, bk * 512:bk * 512 + 1]:
                            e.matmul(o, zcol[0:1, 0:1], zcol[0:1, 0:1],
                                     start=False, stop=True), "sPE", 1)
                if debug and hg == 0 and hpl == -95:
                    W("vector", "sPE", vpv)
                    vdbg = I("vector", lambda e: e.tensor_copy(
                        mb[:, :] if False else sqtmp[:, :].rearrange("p t -> p t"),
                        psw[0:128, 0:2048]), "sDV", 1)
                    W("gpsimd", "sDV", vdbg)
                    vdbg2 = I("gpsimd", lambda e, n=f"d_psw{qh}": e.dma_start(
                        dbg[n], sqtmp[:, :]), "sST", 16)
                    W("tensor", "sST", vdbg2)
                W("vector", "sPE", vpv)
                vfence = I("vector", lambda e: e.engine_nop(), "sDV", 1)
                W("tensor", "sDV", vfence)
                vev_last = 0
                for hh in range(2):
                    region = hh * 1024
                    W("vector", "sPE", vpv)
                    chain("vector", lambda e, o=rowX[:, 0:1024],
                          i=psw[64:65, region:region + 1024]:
                          e.reciprocal(o, i))
                    vrz = I("vector", lambda e: e.engine_nop(), "sDV", 1)
                    if debug and hg == 0 and hpl == -96 and hh == 0:
                        W("sync", "sDV", vrz)
                        vzz = I("sync", lambda e, n=f"d_rz{qh}": e.dma_start(
                            dbg[n], rowX[:, 0:1024]), "sST", 16)
                        W("vector", "sST", vzz)
                    for qc in range(2):
                        vbc = mmg(psS[0:64, 0:512],
                                  [(onesr[0:1, 0:64],
                                    rowX[0:1, qc * 512:(qc + 1) * 512])],
                                  [("sDV", vrz), ("sACT", P.val("sACT"))])
                        W("scalar", "sPE", vbc)
                        if vev_last:
                            W("scalar", "sDV", vev_last)
                        vcp = I("scalar", lambda e:
                                e.copy(rb[0:64, :], psS[0:64, 0:512]),
                                "sACT", 1)
                        W("vector", "sACT", vcp)
                        dst = attn[64 * hh:64 * (hh + 1), hp,
                                   qh * 1024 + qc * 512:
                                   qh * 1024 + (qc + 1) * 512]
                        vev_last = I("vector", lambda e, o=dst,
                                     i0=psw[0:64, region + qc * 512:
                                            region + (qc + 1) * 512],
                                     i1=rb[0:64, :]:
                                     e.tensor_tensor(o, i0, i1, ALU.mult),
                                     "sDV", 1)
                W("tensor", "sDV", P.val("sDV"))
                W("tensor", "sACT", P.val("sACT"))
        W("gpsimd", "sDV", P.val("sDV"))
        W("sync", "sDV", P.val("sDV"))
        if debug >= 2 and hg == 0:
            vda = I("gpsimd", lambda e: e.dma_start(dbg["d_attnE"],
                    attn[:, 0:4, :].rearrange("p c t -> p (c t)")), "sST", 16)
            W("tensor", "sST", vda)
        regA.close()
        regH.close()

    vdv_attn = P.val("sDV")
    if debug >= 2:
        W("gpsimd", "sDV", vdv_attn)
        I("gpsimd", lambda e: e.dma_start(dbg["d_attn"],
          attn[:, :, :].rearrange("p c t -> p (c t)")), "sST", 16)
        W("sync", "sDV", vdv_attn)
        I("sync", lambda e: e.dma_start(dbg["d_u0"], st16[:, :]), "sST", 16)

    out_acc = SBx(es, "out_acc", [128, 8, TOWN])
    ln1 = SBx(es, "ln1", [128, 8, TOWN], BF16)
    regO = ExitStack()
    wob = SBx(regO, "wob", [128, 8, 1024], BF16, side="right")

    for dc in range(8):
        I("sync", lambda e, d=out_acc[:, dc, :],
          s=xown[dc * 128:(dc + 1) * 128, :]: e.dma_start(d, s), "sMx", 16)
    sM_xown = P.val("sMx")
    W("gpsimd", "sPE", P.val("sPE"))
    W("gpsimd", "sDV", P.val("sDV"))
    W("gpsimd", "sACT", P.val("sACT"))
    vwo = None
    for dc in range(8):
        vwo = I("gpsimd", lambda e, d=wob[:, dc, :],
                s=woT[dc * 128:(dc + 1) * 128, :]: e.dma_start(d, s), "sWO", 16)

    pb_free = [0, 0, 0, 0]
    k = 0
    for dc in range(8):
        for qc in range(4):
            bank = k % 4
            k += 1
            vmm = mmg(pst4[bank][:, 0:512],
                      [(wob[:, ic, dc * 128:(dc + 1) * 128],
                        attn[:, ic, qc * 512:(qc + 1) * 512]) for ic in range(8)],
                      [("sWO", vwo), ("sDV", vdv_attn), ("sMx", sM_xown),
                       ("sDV", pb_free[bank])])
            W("vector", "sPE", vmm)
            pb_free[bank] = I("vector",
                              lambda e, o=out_acc[:, dc, qc * 512:(qc + 1) * 512],
                              i0=out_acc[:, dc, qc * 512:(qc + 1) * 512],
                              i1=pst4[bank][:, 0:512]:
                              e.tensor_tensor(o, i0, i1, ALU.add), "sDV", 1)
    vdv_r1 = P.val("sDV")
    if debug:
        W("sync", "sDV", vdv_r1)
        I("sync", lambda e: e.dma_start(dbg["d_r1"],
          out_acc[:, :, :].rearrange("p c t -> p (c t)")), "sST", 16)

    def layer_norm(src_fp32, dst_bf16, vdv_ready, apply_bf16=True,
                   store_dram=None):
        W("vector", "sDV", vdv_ready)
        vmm_prev = 0
        for dc in range(8):
            if vmm_prev:
                W("vector", "sPE", vmm_prev)
            vcp = I("vector", lambda e, o=dst_bf16[:, dc, :], i=src_fp32[:, dc, :]:
                    e.tensor_copy(o, i), "sDV", 1)
            vsq = I("vector", lambda e, o=sqtmp[:, :], i=dst_bf16[:, dc, :]:
                    e.tensor_tensor(o, i, i, ALU.mult), "sDV", 1)
            for qc in range(4):
                mmg(psw[0:1, qc * 512:(qc + 1) * 512],
                    [(onecb[:, :], dst_bf16[:, dc, qc * 512:(qc + 1) * 512])],
                    [("sDV", vcp)], first=(dc == 0))
                vmm_prev = mmg(psw[32:33, qc * 512:(qc + 1) * 512],
                               [(onecb[:, :], sqtmp[:, qc * 512:(qc + 1) * 512])],
                               [("sDV", vsq)], first=(dc == 0))
        vpe_st = P.val("sPE")
        vst_all = 0
        vc2 = 0
        vap_last = 0
        for th in range(2):
            c0 = th * 1024
            W("vector", "sPE", vpe_st)
            I("vector", lambda e, o=mrow[:, :], i=psw[0:1, c0:c0 + 1024]:
              e.tensor_scalar(o, i, 1.0 / D, None, ALU.mult))
            I("vector", lambda e, o=vrow[:, :], i=psw[32:33, c0:c0 + 1024]:
              e.tensor_scalar(o, i, 1.0 / D, None, ALU.mult))
            I("vector", lambda e: e.tensor_tensor(rrow[:, :], mrow[:, :],
                                                  mrow[:, :], ALU.mult))
            I("vector", lambda e: e.tensor_tensor(vrow[:, :], vrow[:, :],
                                                  rrow[:, :], ALU.subtract))
            I("vector", lambda e: e.tensor_scalar(vrow[:, :], vrow[:, :], EPS,
                                                  None, ALU.add))
            vvar = I("vector", lambda e: e.reciprocal(vrow[:, :], vrow[:, :]),
                     "sDV", 1)
            W("scalar", "sDV", vvar)
            vsr = I("scalar", lambda e: e.activation(rrow[:, :], vrow[:, :],
                                                     AF.Sqrt), "sACT", 1)
            for qc in range(2):
                vb1 = mmg(psS[0:128, 0:512],
                          [(onesr[0:1, :], mrow[0:1, qc * 512:(qc + 1) * 512])],
                          [("sACT", vsr), ("sDV", vc2)])
                vb2 = mmg(psS2[0:128, 0:512],
                          [(onesr[0:1, :], rrow[0:1, qc * 512:(qc + 1) * 512])],
                          [])
                W("vector", "sPE", vb2)
                if vap_last:
                    W("vector", "sDV", vap_last)
                vc1 = I("vector", lambda e: e.tensor_copy(mb[:, :], psS[:, 0:512]),
                        "sDV", 1)
                vc2 = I("vector", lambda e: e.tensor_copy(rb[:, :], psS2[:, 0:512]),
                        "sDV", 1)
                cc0 = c0 + qc * 512
                for dc in range(8):
                    if store_dram is not None and vst_all:
                        W("vector", "sST", vst_all)
                    I("vector", lambda e, o=sqtmp[:, 0:512],
                      i0=src_fp32[:, dc, cc0:cc0 + 512], i1=mb[:, :]:
                      e.tensor_tensor(o, i0, i1, ALU.subtract))
                    if store_dram is None:
                        vap_last = I("vector", lambda e,
                                     o=dst_bf16[:, dc, cc0:cc0 + 512],
                                     i0=sqtmp[:, 0:512], i1=rb[:, :]:
                                     e.tensor_tensor(o, i0, i1, ALU.mult),
                                     "sDV", 1)
                    else:
                        vap_last = I("vector", lambda e, o=ost[:, :],
                                     i0=sqtmp[:, 0:512], i1=rb[:, :]:
                                     e.tensor_tensor(o, i0, i1, ALU.mult),
                                     "sDV", 1)
                        W("sync", "sDV", vap_last)
                        vst_all = I("sync", lambda e,
                                    d=store_dram[dc * 128:(dc + 1) * 128,
                                                 cc0:cc0 + 512],
                                    s=ost[:, :]: e.dma_start(d, s), "sST", 16)
        return P.val("sDV")

    vdv_ln1 = layer_norm(out_acc, ln1, vdv_r1)
    if debug:
        W("gpsimd", "sDV", vdv_ln1)
        I("gpsimd", lambda e: e.dma_start(dbg["d_ln1"],
          ln1[:, :, :].rearrange("p c t -> p (c t)")), "sST", 16)
    regO.close()
    regAttn.close()

    scores = SBx(es, "scores", [8, TOWN])
    gall = SBx(es, "gall", [64, TOWN])
    gmask = SBx(es, "gmask", [64, TOWN], BF16)
    regG = ExitStack()
    gwb = SBx(regG, "gwb", [128, 8, 8], BF16, side="right")
    W("gpsimd", "sPE", P.val("sPE"))
    vgw = None
    for dc in range(8):
        vgw = I("gpsimd", lambda e, d=gwb[:, dc, :],
                s=gwT[dc * 128:(dc + 1) * 128, :]: e.dma_start(d, s), "sWV", 16)
    for qc in range(4):
        mmg(psw[0:8, qc * 512:(qc + 1) * 512],
            [(gwb[:, dc, :], ln1[:, dc, qc * 512:(qc + 1) * 512])
             for dc in range(8)],
            [("sWV", vgw), ("sDV", vdv_ln1)])
    vpe_gl = P.val("sPE")
    regG.close()
    W("scalar", "sPE", vpe_gl)
    vex = I("scalar", lambda e: e.activation(scores[:, :], psw[0:8, 0:TOWN],
                                             AF.Exp), "sACT", 1)
    for qc in range(4):
        mmg(psw[32:33, qc * 512:(qc + 1) * 512],
            [(onecf[0:8, 0:1], scores[:, qc * 512:(qc + 1) * 512])],
            [("sACT", vex), ("sDV", vdv_ln1)])
    vpe_gs = P.val("sPE")
    vsc = 0
    for th in range(2):
        c0 = th * 1024
        W("vector", "sPE", vpe_gs)
        vrc = I("vector", lambda e, o=mrow[:, :], i=psw[32:33, c0:c0 + 1024]:
                e.reciprocal(o, i), "sDV", 1)
        for qc in range(2):
            mmg(psS2[0:8, qc * 512:(qc + 1) * 512],
                [(onesr[0:1, 0:8], mrow[0:1, qc * 512:(qc + 1) * 512])],
                [("sDV", vrc), ("sDV", vsc)])
        W("vector", "sPE", P.val("sPE"))
        vsc = I("vector", lambda e, o=scores[:, c0:c0 + 1024],
                i0=scores[:, c0:c0 + 1024], i1=psS2[0:8, 0:1024]:
                e.tensor_tensor(o, i0, i1, ALU.mult), "sDV", 1)
    if debug:
        W("sync", "sDV", vsc)
        I("sync", lambda e: e.dma_start(dbg["d_scores"], scores[:, :]), "sST", 16)

    regF = ExitStack()
    w1b = [SBx(regF, f"w1b{i}", [128, 8, 256], BF16, side="right") for i in range(2)]
    w2b = [SBx(regF, f"w2b{i}", [128, 8, 512], BF16, side="right")
           for i in range(2)]
    hbf = SBx(regF, "hbf", [128, 8, TOWN], BF16, side="right")
    gb = SBx(regF, "gb", [128, TOWN], BF16, side="right")

    def w1_src(e_, fq, fo4):
        a = fq * 1024 + fo4 * 256
        if e_ < 8:
            return lambda dc: w1d[e_, dc * 128:(dc + 1) * 128, a:a + 256]
        return lambda dc: ws1d[dc * 128:(dc + 1) * 128, a:a + 256]

    def w2_src(e_, fq, dh):
        if e_ < 8:
            return lambda fc: w2d[e_, fq * 1024 + fc * 128:
                                  fq * 1024 + (fc + 1) * 128,
                                  dh * 512:(dh + 1) * 512]
        return lambda fc: ws2d[fq * 1024 + fc * 128: fq * 1024 + (fc + 1) * 128,
                               dh * 512:(dh + 1) * 512]

    w1_loaded = {}
    w1_cons = {}

    def load_w1(step):
        e_, fq, fo4 = EORDER[step // 16], (step % 16) // 4, step % 4
        if step - 2 >= 0:
            W("gpsimd", "sPE", w1_cons.get(step - 2, 0))
        src = w1_src(e_, fq, fo4)
        sname = "sW1a" if step % 2 == 0 else "sW1b"
        v = None
        for dc in range(8):
            v = I("gpsimd", lambda e, d=w1b[step % 2][:, dc, :], s=src(dc):
                  e.dma_start(d, s), sname, 16)
        w1_loaded[step] = (sname, v)

    w2_loaded = {}
    w2_cons = {}

    def load_w2(step):
        e_, fq, dh = EORDER[step // 8], (step % 8) // 2, step % 2
        if step - 2 >= 0:
            W("gpsimd", "sPE", w2_cons.get(step - 2, 0))
        src = w2_src(e_, fq, dh)
        sname = "sW2s" if step % 2 == 0 else "sW2t"
        v = None
        for fc in range(8):
            v = I("gpsimd", lambda e, d=w2b[step % 2][:, fc, :], s=src(fc):
                  e.dma_start(d, s), sname, 16)
        w2_loaded[step] = (sname, v)

    W("gpsimd", "sPE", vpe_gl)
    W("gpsimd", "sDV", vdv_ln1)
    load_w1(0)
    load_w1(1)
    load_w2(0)
    load_w2(1)
    W("sync", "sDV", vsc)
    vgi = I("sync", lambda e: e.dma_start(ccin[:, :], scores[:, :]), "sGP", 16)
    W("gpsimd", "sGP", vgi)
    vcc = I("gpsimd", lambda e: e.collective_compute(
        "AllGather", ALU.bypass, replica_groups=[list(range(8))],
        ins=[ccin[:, :]], outs=[ccout[:, :]]), "sCC", 1)
    W("sync", "sCC", vcc)
    vgo = I("sync", lambda e: e.dma_start(gall[:, :], ccout[:, :]), "sGP", 16)

    chain("vector", lambda e: e.memset(st16[:, 8:9], 0.0))
    chain("vector", lambda e: e.memset(st16[:, 9:10], 1.0))
    chain("vector", lambda e: e.memset(st16[:, 10:11], 1.0))
    chain("vector", lambda e: e.memset(midc[:, :], 1.0))
    vmem_sh = chain("vector", lambda e: e.memset(rowX[:, :], 1.0))
    W("vector", "sGP", vgo)
    for it in range(NITER):
        if it > 0:
            W("vector", "sDM2", 16 * it)
        I("vector", lambda e: e.tensor_scalar(gmask[:, :], gall[:, :],
                                              midc[:, :], None, ALU.is_ge))
        vcn = I("vector", lambda e: e.reduce_sum(cn64[:, :], gmask[:, :], AX.X),
                "sDV", 1)
        W("sync", "sDV", vcn)
        vdm = I("sync", lambda e: e.dma_start(crow[:, :], cn64[:, :]), "sDM", 16)
        W("vector", "sDM", vdm)
        chain("vector", lambda e: e.tensor_reduce(st16[:, 11:12], crow[:, :],
                                                  AX.X, ALU.add))
        chain("vector", lambda e: e.tensor_scalar(st16[:, 12:13], st16[:, 11:12],
                                                  KSEL, None, ALU.is_ge))
        chain("vector", lambda e: e.tensor_tensor(st16[:, 13:14], st16[:, 12:13],
                                                  st16[:, 9:10], ALU.mult))
        chain("vector", lambda e: e.tensor_tensor(st16[:, 8:9], st16[:, 8:9],
                                                  st16[:, 13:14], ALU.add))
        last = (it == NITER - 1)
        if not last:
            chain("vector", lambda e: e.tensor_scalar(st16[:, 9:10],
                                                      st16[:, 9:10],
                                                      0.5, None, ALU.mult))
            chain("vector", lambda e: e.tensor_tensor(st16[:, 10:11],
                                                      st16[:, 8:9],
                                                      st16[:, 9:10], ALU.add))
        srcc = st16[:, 8:9] if last else st16[:, 10:11]
        vmr = I("vector", lambda e, s_=srcc: e.tensor_scalar(
            midrow[:, :], on64r[:, :], s_, None, ALU.mult), "sDV", 1)
        if not last:
            W("sync", "sDV", vmr)
            I("sync", lambda e: e.dma_start(midc[:, :], midrow[:, :]), "sDM2", 16)
    W("sync", "sDV", P.val("sDV"))
    vlc = I("sync", lambda e: e.dma_start(locol[:, :], midrow[0:1, 0:8]),
            "sDM", 16)
    W("vector", "sDM", vlc)
    I("vector", lambda e: e.tensor_scalar(gmask[0:8, :], scores[:, :],
                                          locol[:, :], None, ALU.is_ge))
    vgt = I("vector", lambda e: e.tensor_tensor(scores[:, :], gmask[0:8, :],
                                                scores[:, :], ALU.mult),
            "sDV", 1)
    if debug:
        W("sync", "sDV", vgt)
        I("sync", lambda e: e.dma_start(dbg["d_gate"], scores[:, :]), "sST", 16)


    hA_free = [0, 0, 0, 0]
    yB_free = [0, 0, 0, 0]
    rowx_free = 0

    for step_e, e_ in enumerate(EORDER):
        for th in range(2):
            if e_ < 8:
                W("sync", "sDV", vgt)
                if rowx_free:
                    W("sync", "sPE", rowx_free)
                stw = I("sync", lambda e, r=e_, c0=th * 1024: e.dma_start(
                    rowX[:, :], scores[r:r + 1, c0:c0 + 1024]), "sDM", 16)
                stg_w = [("sDM", stw)]
            else:
                stg_w = [("sCH", vmem_sh)]
            for qc in range(2):
                tc = th * 2 + qc
                vbc = mmg(pswq[tc][0:128, :],
                          [(onesr[0:1, :], rowX[0:1, qc * 512:(qc + 1) * 512])],
                          stg_w + [("sDV", yB_free[tc])])
                rowx_free = vbc
                W("vector", "sPE", vbc)
                yB_free[tc] = I("vector",
                                lambda e, o=gb[:, tc * 512:(tc + 1) * 512],
                                i=pswq[tc][:, :]: e.tensor_copy(o, i), "sDV", 1)

        for fq in range(4):
            for fo4 in range(4):
                step = step_e * 16 + fq * 4 + fo4
                bi = step % 2
                sw1n, vw1 = w1_loaded[step]
                if step + 1 < NE * 16 and (step + 1) not in w1_loaded:
                    load_w1(step + 1)
                vlast = None
                for ftl in range(2):
                    ft = fo4 * 2 + ftl
                    for tc in range(4):
                        bank = (ftl * 4 + tc) % 4
                        vmm = mmg(pst4[bank][:, 0:512],
                                  [(w1b[bi][:, dc, ftl * 128:(ftl + 1) * 128],
                                    ln1[:, dc, tc * 512:(tc + 1) * 512])
                                   for dc in range(8)],
                                  [(sw1n, vw1), ("sDV", vdv_ln1),
                                   ("sACT", hA_free[bank])])
                        vlast = vmm
                        W("scalar", "sPE", vmm)
                        bcol = b1sb[:, e_, fq * 8 + ft: fq * 8 + ft + 1]
                        hA_free[bank] = I("scalar", lambda e,
                                          o=hbf[:, ft, tc * 512:(tc + 1) * 512],
                                          i=pst4[bank][:, 0:512], bc=bcol:
                                          e.activation(o, i, AF.Gelu_apprx_tanh,
                                                       bias=bc), "sACT", 1)
                w1_cons[step] = vlast
            vact_h = P.val("sACT")
            for dh in range(2):
                s2 = step_e * 8 + fq * 2 + dh
                if s2 not in w2_loaded:
                    load_w2(s2)
                sw2n, vw2 = w2_loaded[s2]
                if s2 + 1 < NE * 8 and (s2 + 1) not in w2_loaded:
                    load_w2(s2 + 1)
                vlb = None
                for dtl in range(4):
                    dt = dh * 4 + dtl
                    for tc in range(4):
                        vmm = mmg(pswq[tc][0:128, :],
                                  [(w2b[s2 % 2][:, fc, dtl * 128:(dtl + 1) * 128],
                                    hbf[:, fc, tc * 512:(tc + 1) * 512])
                                   for fc in range(8)],
                                  [(sw2n, vw2), ("sACT", vact_h),
                                   ("sDV", yB_free[tc])])
                        vlb = vmm
                        W("vector", "sPE", vmm)
                        bval = b2sb[:, e_, dt:dt + 1] if fq == 3 else 0.0
                        I("vector", lambda e, o=sqtmp[:, 0:512],
                          i0=pswq[tc][:, :], sc=bval,
                          i1=gb[:, tc * 512:(tc + 1) * 512]:
                          e.scalar_tensor_tensor(o, i0, sc, i1, ALU.add,
                                                 ALU.mult))
                        yB_free[tc] = I("vector", lambda e,
                                        o=out_acc[:, dt, tc * 512:(tc + 1) * 512],
                                        i0=out_acc[:, dt,
                                                   tc * 512:(tc + 1) * 512],
                                        i1=sqtmp[:, 0:512]:
                                        e.tensor_tensor(o, i0, i1, ALU.add),
                                        "sDV", 1)
                w2_cons[s2] = vlb

    vdv_ffn = P.val("sDV")
    regF.close()

    ost = SBx(es, "ost", [128, 512])
    layer_norm(out_acc, ln1, vdv_ffn, store_dram=out)
    W("sync", "sST", P.val("sST"))

    with nc.Block() as block:
        def mk(eng_name):
            def body(e):
                for (en, fn) in P.items:
                    if en == eng_name:
                        fn(e)
            return body
        block.tensor(mk("tensor"))
        block.vector(mk("vector"))
        block.scalar(mk("scalar"))
        block.gpsimd(mk("gpsimd"))
        block.sync(mk("sync"))

    es.close()
    return nc


def _rope_tables():
    t = np.arange(T, dtype=np.float64)
    inv = 1.0 / (10000.0 ** (np.arange(0, HD, 2, dtype=np.float64) / HD))
    ang = np.outer(inv, t)
    c = np.cos(ang); s = np.sin(ang)
    cos128 = np.zeros((128, T), np.float64)
    sin128 = np.zeros((128, T), np.float64)
    for r in range(64):
        g, i = divmod(r, 32)
        if i < 16:
            d = 16 * g + i
            cos128[r] = c[d]; cos128[64 + r] = c[d]
            sin128[r] = -s[d]; sin128[64 + r] = -s[d]
        else:
            d = 16 * g + i - 16
            cos128[r] = c[d]; cos128[64 + r] = c[d]
            sin128[r] = s[d]; sin128[64 + r] = s[d]
    return cos128, sin128


def _perm_head():
    p = []
    for h in range(H):
        base = h * HD
        p += [base + i for i in range(16)]
        p += [base + 32 + i for i in range(16)]
        p += [base + 16 + i for i in range(16)]
        p += [base + 48 + i for i in range(16)]
    return np.array(p)


def _amask_for(sub):
    obl = _owned_blocks(sub)
    m = np.zeros((128, NB * 256), np.float32)
    kk = np.arange(128)[:, None]
    qq = np.arange(128)[None, :]
    tri = (kk <= qq).astype(np.float32)
    for kt in range(NB):
        qsm = _qs_min(kt)
        for colblk in range(2):
            a = qsm // 128 + colblk
            if a >= 16:
                continue
            j = obl[a]
            if j < kt:
                blk = np.zeros((128, 128), np.float32)
            elif j == kt:
                blk = tri
            else:
                blk = np.ones((128, 128), np.float32)
            m[:, kt * 256 + colblk * 128: kt * 256 + (colblk + 1) * 128] = blk
    return m


_nc_cache = {}


def kernel(L0_attn_x, L0_attn_wq, L0_attn_wk, L0_attn_wv, L0_attn_wo,
           L0_ffn_gate_weight, L0_ffn_W1, L0_ffn_b1, L0_ffn_W2, L0_ffn_b2,
           L0_ffn_Ws1, L0_ffn_bs1, L0_ffn_Ws2, L0_ffn_bs2):
    x = np.ascontiguousarray(np.asarray(L0_attn_x, dtype=np.float32))
    wq = np.asarray(L0_attn_wq, np.float32); wk = np.asarray(L0_attn_wk, np.float32)
    wv = np.asarray(L0_attn_wv, np.float32); wo = np.asarray(L0_attn_wo, np.float32)
    gw = np.asarray(L0_ffn_gate_weight, np.float32)
    w1 = np.ascontiguousarray(np.asarray(L0_ffn_W1, np.float32))
    w2 = np.ascontiguousarray(np.asarray(L0_ffn_W2, np.float32))
    b1 = np.asarray(L0_ffn_b1, np.float32); b2 = np.asarray(L0_ffn_b2, np.float32)
    ws1 = np.ascontiguousarray(np.asarray(L0_ffn_Ws1, np.float32))
    ws2 = np.ascontiguousarray(np.asarray(L0_ffn_Ws2, np.float32))
    bs1 = np.asarray(L0_ffn_bs1, np.float32); bs2 = np.asarray(L0_ffn_bs2, np.float32)

    perm = _perm_head()
    wqT = np.ascontiguousarray(wq[perm].T)
    wkT = np.ascontiguousarray(wk[perm].T)
    wvT = np.ascontiguousarray(wv.T)
    woT = np.ascontiguousarray(wo.T)
    gwT = np.ascontiguousarray(gw.T)
    b1a = np.zeros((128, NE, F // 128), np.float32)
    b2a = np.zeros((128, NE, D // 128), np.float32)
    b1a[:, :8, :] = b1.reshape(E, F // 128, 128).transpose(2, 0, 1)
    b1a[:, 8, :] = bs1.reshape(F // 128, 128).T
    b2a[:, :8, :] = b2.reshape(E, D // 128, 128).transpose(2, 0, 1)
    b2a[:, 8, :] = bs2.reshape(D // 128, 128).T

    cos128, sin128 = _rope_tables()
    cosKb = cos128.astype(BF16NP)
    sinKb = sin128.astype(BF16NP)
    onc_b = np.ones((128, 1), BF16NP)
    on33 = np.zeros((128, 33), np.float32)
    on33[0:64, 0] = 1.0
    on33[64:128, 32] = 1.0
    on33 = on33.astype(BF16NP)
    onc_f = np.ones((128, 1), np.float32)
    onr_f = np.ones((1, 128), np.float32)
    on64 = np.ones((1, 64), np.float32)

    in_maps = []
    for c in range(8):
        b, sub = divmod(c, 2)
        obl = _owned_blocks(sub)
        own_tok = np.concatenate([np.arange(j * 128, (j + 1) * 128) for j in obl])
        xtb = np.ascontiguousarray(x[b].T)
        xownb = np.ascontiguousarray(xtb[:, own_tok])
        cosQ = np.ascontiguousarray(cos128[:, own_tok]).astype(BF16NP)
        sinQ = np.ascontiguousarray(sin128[:, own_tok]).astype(BF16NP)
        am = _amask_for(sub).astype(BF16NP)
        in_maps.append({
            "xt": xtb, "xown": xownb, "wqT": wqT, "wkT": wkT, "wvT": wvT,
            "woT": woT, "gwT": gwT, "w1": w1, "w2": w2, "ws1": ws1, "ws2": ws2,
            "b1a": b1a, "b2a": b2a, "cosK": cosKb, "sinK": sinKb,
            "cosQ": cosQ, "sinQ": sinQ, "amask": am, "onc_b": onc_b, "on33": on33,
            "onc_f": onc_f, "onr_f": onr_f, "on64": on64,
        })

    key = ("nc", DEBUG)
    if key not in _nc_cache:
        _nc_cache[key] = build_nc(debug=DEBUG)
    nc = _nc_cache[key]
    res = run_bass_kernel_spmd(nc, in_maps, core_ids=list(range(8)))

    outp = np.zeros((B, T, D), np.float32)
    for c in range(8):
        b, sub = divmod(c, 2)
        obl = _owned_blocks(sub)
        own_tok = np.concatenate([np.arange(j * 128, (j + 1) * 128) for j in obl])
        outp[b, own_tok, :] = res.results[c]["out"].T
    if DEBUG:
        kernel._dbg = res.results
    return outp
